# version 20
# baseline (speedup 1.0000x reference)
import math
import numpy as np
import ml_dtypes
from contextlib import ExitStack

import concourse.bass as bass
import concourse.mybir as mybir
from concourse.bass_utils import run_bass_kernel_spmd

F32 = mybir.dt.float32
BF16 = mybir.dt.bfloat16
AF = mybir.ActivationFunctionType
ALU = mybir.AluOpType
AX = mybir.AxisListType

S = 2048
D = 2048
NT = 16
NH = 16
QL = 1536
KVL = 512
DN = 128
DR = 64
DV = 128
NSEQ = 2
SCALE = (DN + DR) ** -0.5
NEG = -1.0e30
EPS_RMS = 1e-6
EPS_LN = 1e-5
INV_SQRT_RANK = 1.0 / math.sqrt(8.0)


def _chunks(qmin, qmax, bank=512):
    out = []
    q = qmin
    while q < qmax:
        w = min(bank - (q % bank), qmax - q)
        out.append((q, w))
        q += w
    return out


class Ch:
    def __init__(self, nc, name):
        self.h = nc.alloc_semaphore(name=name)
        self.n = 0
        self.name = name

    @property
    def mark(self):
        return (self, self.n)


class Prog:
    ENGINES = ("tensor", "vector", "scalar", "sync", "gpsimd")

    def __init__(self, nc):
        self.nc = nc
        self.ops = {e: [] for e in self.ENGINES}

    def emit(self, eng, fn, waits=(), inc=None, inc_amt=1):
        wl = [(c.h if isinstance(c, Ch) else c, v) for c, v in waits]
        if inc is not None:
            inc.n += inc_amt
            ih, amt = inc.h, inc_amt
        else:
            ih, amt = None, 0

        def run(engine):
            for h, v in wl:
                engine.wait_ge(h, v)
            ins = fn(engine)
            if ih is not None:
                ins.then_inc(ih, amt)
        self.ops[eng].append(run)

    def wait(self, eng, waits):
        wl = [(c.h if isinstance(c, Ch) else c, v) for c, v in waits]

        def run(engine):
            for h, v in wl:
                engine.wait_ge(h, v)
        self.ops[eng].append(run)


def build_nc():
    nc = bass.Bass("TRN2", target_bir_lowering=False,
                   dynamic_dma_scratch_size=16384)

    x_in = nc.dram_tensor("x", [NSEQ, S, D], F32, kind="ExternalInput")
    wqa = nc.dram_tensor("wqa", [D, QL], F32, kind="ExternalInput")
    qln = nc.dram_tensor("qln", [QL], F32, kind="ExternalInput")
    wqb = nc.dram_tensor("wqb", [QL, NH * (DN + DR)], F32, kind="ExternalInput")
    wkva = nc.dram_tensor("wkva", [D, KVL + DR], F32, kind="ExternalInput")
    kvln = nc.dram_tensor("kvln", [KVL], F32, kind="ExternalInput")
    wkvb = nc.dram_tensor("wkvb", [KVL, NH * (DN + DV)], F32, kind="ExternalInput")
    wo = nc.dram_tensor("wo", [NH * DV, D], F32, kind="ExternalInput")
    fac = nc.dram_tensor("fac", [8, D, D], F32, kind="ExternalInput")
    fbias = nc.dram_tensor("fbias", [D], F32, kind="ExternalInput")
    identc = nc.dram_tensor("identc", [128, 128], BF16, kind="ExternalInput")
    identfc = nc.dram_tensor("identfc", [128, 128], F32, kind="ExternalInput")
    dmaskc = nc.dram_tensor("dmaskc", [128, 128], F32, kind="ExternalInput")
    tmaskc = nc.dram_tensor("tmaskc", [128, 128], BF16, kind="ExternalInput")
    cosc = nc.dram_tensor("cosc", [S, DR], F32, kind="ExternalInput")
    sinnc = nc.dram_tensor("sinnc", [S, DR], F32, kind="ExternalInput")

    x_out = nc.dram_tensor("out", [NSEQ, S, D], F32, kind="ExternalOutput")

    qpe_s = nc.dram_tensor("qpe_s", [S, NH * DR], BF16, kind="Internal")
    r1_s = nc.dram_tensor("r1_s", [S, D], F32, kind="Internal")
    f_s = nc.dram_tensor("f_s", [D, D], BF16, kind="Internal")
    ot_s = nc.dram_tensor("ot_s", [NH, 128, S], BF16, kind="Internal")
    ln2t_s = nc.dram_tensor("ln2t_s", [NT, 128, D], BF16, kind="Internal")

    ctx = ExitStack()

    def sb(name, shape, dt, side="left"):
        return ctx.enter_context(nc.sbuf_tensor(name, shape, dt, side=side))

    ident = sb("ident", [128, 128], BF16)
    identf = sb("identf", [128, 128], F32)
    diag = sb("diag", [128, 128], F32)
    maskT = sb("maskT", [128, 128], BF16)
    ones1 = sb("ones1", [1, 128], BF16)
    halfs1 = sb("halfs1", [1, 128], BF16)
    ones_col = sb("ones_col", [128, 1], BF16)
    bias_hl = sb("bias_hl", [1, 2 * D], BF16)
    qlnw = sb("qlnw", [128, 12], F32)
    kvlnw = sb("kvlnw", [128, 4], F32)
    ptpool = sb("ptpool", [128, 2 * S], BF16)
    acc_f = sb("acc_f", [128, D], F32)
    stat1 = sb("stat1", [128, 16], F32)
    stat_h = sb("stat_h", [128, 32], BF16)
    sb_st = sb("sb_st", [32, 128], BF16)
    stat_ln2 = sb("stat_ln2", [128, 2 * NT], F32)
    cvals = sb("cvals", [128, 4], F32)

    psA = ctx.enter_context(nc.psum_tensor("psA", [128, 2048], F32))
    psB = ctx.enter_context(nc.psum_tensor("psB", [128, 2048], F32))

    p = Prog(nc)
    E = p.emit
    ch = lambda name: Ch(nc, name)

    c_const = ch("const")
    c_fb = ch("fb")
    c_bias = ch("bias")
    c_x = [ch("xload0"), ch("xload1")]
    c_lnv = ch("lnv")
    c_ln1 = ch("ln1")
    c_trL = ch("trL"); c_trL_ev = ch("trL_ev")
    c_wa = ch("wa")
    c_ckv = ch("ckv"); c_ckv_ev = ch("ckv_ev"); c_rskv = ch("rskv")
    c_qa = ch("qa"); c_qa_ev = ch("qa_ev"); c_rsqa = ch("rsqa")
    c_trQ = ch("trQ"); c_trQ_ev = ch("trQ_ev")
    c_trC = ch("trC"); c_trC_ev = ch("trC_ev")
    c_wpe = ch("wpe"); c_wpef = ch("wpef")
    c_bpe = ch("bpe"); c_bpe_ev = ch("bpe_ev")
    c_qpew = [ch("qpew0"), ch("qpew1")]
    c_tkpe = ch("tkpe"); c_tkpe_ev = ch("tkpe_ev")
    c_wh = ch("wh"); c_whf = ch("whf"); c_qpeh = ch("qpeh")
    c_bn = ch("bn"); c_bn_ev = ch("bn_ev")
    c_c1 = ch("c1"); c_c1_ev = ch("c1_ev")
    c_c2 = ch("c2"); c_c2_ev = ch("c2_ev")
    c_tq = ch("tq"); c_tq_ev = ch("tq_ev")
    c_s = ch("s"); c_smax = ch("smax"); c_sdone = ch("sdone"); c_sln = ch("sln")
    c_stat = ch("stat"); c_statT = ch("statT"); c_statT_ev = ch("statT_ev")
    c_aug = ch("aug")
    c_st = ch("st"); c_pt = ch("pt"); c_ptm = ch("ptm"); c_av = ch("av")
    c_ot_ev = ch("ot_ev"); c_otw = ch("otw"); c_otld = ch("otld")
    c_wo = ch("wo"); c_xre = [ch("xre0"), ch("xre1")]; c_woG = ch("woG")
    c_r1 = ch("r1"); c_ln2s = ch("ln2s")
    c_r1wp = [ch("r1wp0"), ch("r1wp1")]
    c_fld = ch("fld"); c_r1r = [ch("r1r0"), ch("r1r1"), ch("r1r2")]; c_ln2 = ch("ln2")
    c_trF = ch("trF"); c_trF_ev = ch("trF_ev")
    c_ffn = ch("ffn"); c_outv = ch("outv")
    c_out = [ch("out0"), ch("out1")]
    c_cos = ch("cos"); c_bld = ch("bld")
    c_v = ch("vroll"); c_sc = ch("scroll")
    c_mq = ch("mq"); c_mt = ch("mt"); c_mv = ch("mv"); c_sg = ch("sg")
    c_l2w = [ch("l2w0"), ch("l2w1")]
    c_fldp = [ch(f"fldp{i}") for i in range(4)]
    c_l2r = [ch("l2r0"), ch("l2r1")]
    c_flo = ch("flo")

    for fn in [
        lambda e: e.dma_start(out=ident[:], in_=identc[:]),
        lambda e: e.dma_start(out=identf[:], in_=identfc[:]),
        lambda e: e.dma_start(out=diag[:], in_=dmaskc[:]),
        lambda e: e.dma_start(out=maskT[:], in_=tmaskc[:]),
        lambda e: e.dma_start(out=qlnw[:], in_=qln[:].rearrange("(a p) -> p a", p=128),
                              allow_slow_non_contiguous=True),
        lambda e: e.dma_start(out=kvlnw[:], in_=kvln[:].rearrange("(a p) -> p a", p=128),
                              allow_slow_non_contiguous=True),
    ]:
        E("sync", fn, inc=c_const, inc_amt=16)
    E("vector", lambda e: e.memset(ones1[:], 1.0))
    E("vector", lambda e: e.memset(halfs1[:], 0.5))
    E("vector", lambda e: e.memset(ones_col[:], 1.0))
    c_cc = ch("cc")
    E("vector", lambda e: e.memset(cvals[:, 0:1], EPS_LN))
    E("vector", lambda e: e.memset(cvals[:, 1:2], EPS_RMS))
    E("vector", lambda e: e.memset(cvals[:, 2:3], math.log(SCALE)))
    E("vector", lambda e: e.memset(cvals[:, 3:4], math.log(INV_SQRT_RANK)),
      inc=c_cc)
    p.wait("scalar", [(c_cc, 1), (c_const, c_const.n)])
    p.wait("tensor", [(c_const, c_const.n)])
    p.wait("vector", [(c_const, c_const.n)])

    E("sync", lambda e: e.dma_start(out=acc_f[0:1, 0:D], in_=fbias[:].unsqueeze(0)),
      inc=c_bld, inc_amt=16)
    E("vector", lambda e: e.tensor_copy(out=bias_hl[:, 0:D], in_=acc_f[0:1, 0:D]),
      waits=[(c_bld, 16)], inc=c_bias)
    E("vector", lambda e: e.tensor_sub(out=bias_hl[:, D:2 * D], in0=acc_f[0:1, 0:D],
                                       in1=bias_hl[:, 0:D]),
      waits=[(c_bias, 1)], inc=c_bias)

    def emit_f_build():
        p.wait("gpsimd", [(c_pt, 40)])
        for rb in range(16):
            E("gpsimd", lambda e, rb=rb: e.dma_start(
                out=acc_f[:], in_=fac[0, rb * 128:(rb + 1) * 128, :]),
              waits=[(c_bias, 2)] if rb == 0 else [(c_fb, c_fb.n)],
              inc=c_fb, inc_amt=16)
            for r in range(1, 8):
                E("gpsimd", lambda e, r=r, rb=rb: e.dma_start(
                    out=acc_f[:], in_=fac[r, rb * 128:(rb + 1) * 128, :],
                    accum_op=ALU.add),
                  waits=[(c_fb, c_fb.n)], inc=c_fb, inc_amt=16)
            E("gpsimd", lambda e, rb=rb: e.dma_start(
                out=f_s[rb * 128:(rb + 1) * 128, :], in_=acc_f[:]),
              waits=[(c_fb, c_fb.n)], inc=c_fb, inc_amt=16)

    slot_marks = {"A": [], "B": []}

    rctx = [None]

    def open_right():
        rctx[0] = ExitStack()

    _uid = [0]

    def rsb(name, shape, dt):
        _uid[0] += 1
        return rctx[0].enter_context(
            nc.sbuf_tensor(f"{name}_u{_uid[0]}", shape, dt, side="right"))

    def close_right():
        rctx[0].close()
        rctx[0] = None

    gate = []

    def _emit_seq(sq):
        nonlocal gate
        open_right()
        xs = [rsb(f"xs{i}", [128, D], F32) for i in range(2)]
        ln1_bf = rsb("ln1_bf", [128, D], BF16)
        _l1t = rsb("ln1T", [128, D], BF16)
        ln1T = [_l1t, _l1t]
        qa_n = rsb("qa_n", [128, QL], BF16)
        ckv_n = rsb("ckv_n", [128, KVL], BF16)
        rope1 = rsb("rope1", [128, 1024], BF16)
        rope2 = rsb("rope2", [128, 1024], BF16)
        cos_t = rsb("cos_t", [128, NT * DR], F32)
        sinn_t = rsb("sinn_t", [128, NT * DR], F32)
        kpe_tok = rsb("kpe_tok", [128, NT * DR], BF16)
        g2ctx = ExitStack()
        qa_nT = g2ctx.enter_context(nc.sbuf_tensor(f"qa_nT_{sq}", [128, 12 * S], BF16, side="left"))
        ckv_nT = g2ctx.enter_context(nc.sbuf_tensor(f"ckv_nT_{sq}", [128, 4 * S], BF16, side="left"))
        kpeT_aug = g2ctx.enter_context(nc.sbuf_tensor(f"kpeT_aug_{sq}", [66, S], BF16, side="left"))
        E("vector", lambda e: e.memset(kpeT_aug[64:66, :], 1.0),
          waits=(list(gate) if sq else []))
        wa_ctx = ExitStack()
        wa_bf = wa_ctx.enter_context(nc.sbuf_tensor(f"wa_bf_{sq}", [128, 16 * 2112], BF16, side="right"))

        qa_v = lambda: qa_nT[:].rearrange("p (a t) -> p a t", t=S)
        ckv_v = lambda: ckv_nT[:].rearrange("p (a t) -> p a t", t=S)

        cosv = cosc[:].rearrange("(a p) j -> p a j", p=128)
        sinv = sinnc[:].rearrange("(a p) j -> p a j", p=128)
        E("sync", lambda e: e.dma_start(
            out=cos_t[:].rearrange("p (a j) -> p a j", j=DR), in_=cosv),
          waits=list(gate), inc=c_cos, inc_amt=16)
        E("sync", lambda e: e.dma_start(
            out=sinn_t[:].rearrange("p (a j) -> p a j", j=DR), in_=sinv),
          inc=c_cos, inc_amt=16)
        cos_ready = c_cos.n

        for kb in range(16):
            E("gpsimd", lambda e, kb=kb: e.dma_start(
                out=wa_bf[:, kb * 2112:kb * 2112 + 1536],
                in_=wqa[kb * 128:(kb + 1) * 128, :]),
              waits=(list(gate) if kb == 0 else []), inc=c_wa, inc_amt=16)
            E("gpsimd", lambda e, kb=kb: e.dma_start(
                out=wa_bf[:, kb * 2112 + 1536:kb * 2112 + 2112],
                in_=wkva[kb * 128:(kb + 1) * 128, :]),
              inc=c_wa, inc_amt=16)
        wa_done = c_wa.n

        ln1_base = c_ln1.n
        for tt in range(NT):
            ti = tt % 2
            xw = list(gate) if tt == 0 else ([] if tt == 1 else [(c_ln1, ln1_base + tt - 1)])
            E("sync", lambda e, sq=sq, tt=tt, ti=ti: e.dma_start(
                out=xs[ti][:], in_=x_in[sq, tt * 128:(tt + 1) * 128, :]),
              waits=xw, inc=c_x[ti], inc_amt=16)
            E("scalar", lambda e, ti=ti: e.activation(
                out=ptpool[:, 0:D], in_=xs[ti][:], func=AF.Copy, scale=-1.0 / D,
                accum_out=stat1[:, 2:3]),
              waits=[(c_x[ti], c_x[ti].n)] + ([(c_av, c_av.n)] if (sq > 0 and tt == 0) else []),
              inc=c_sc)
            E("scalar", lambda e, ti=ti: e.activation(
                out=ptpool[:, 0:D], in_=xs[ti][:], func=AF.Square, scale=D ** -0.5,
                accum_out=stat1[:, 1:2]),
              waits=[(c_sc, c_sc.n)], inc=c_sc)
            E("scalar", lambda e: e.activation(
                out=stat1[:, 3:4], in_=stat1[:, 2:3], func=AF.Square),
              waits=[(c_sc, c_sc.n)], inc=c_sc)
            E("scalar", lambda e: e.activation(
                out=stat1[:, 4:5], in_=stat1[:, 3:4], func=AF.Identity, scale=-1.0,
                bias=stat1[:, 1:2]),
              waits=[(c_sc, c_sc.n)], inc=c_sc)
            E("scalar", lambda e: e.activation(
                out=stat1[:, 5:6], in_=stat1[:, 4:5], func=AF.Ln, bias=cvals[:, 0:1]),
              waits=[(c_sc, c_sc.n)], inc=c_sc)
            E("scalar", lambda e: e.activation(
                out=stat1[:, 5:6], in_=stat1[:, 5:6], func=AF.Exp, scale=-0.5),
              waits=[(c_sc, c_sc.n)], inc=c_sc)
            E("scalar", lambda e: e.activation(
                out=stat1[:, 6:7], in_=stat1[:, 2:3], func=AF.Identity,
                scale=stat1[:, 5:6]),
              waits=[(c_sc, c_sc.n)], inc=c_sc)
            lw = [(c_sc, c_sc.n)] + \
                ([(c_trL, c_trL.n)] if tt > 0 or sq > 0 else [])
            E("scalar", lambda e, ti=ti: e.activation(
                out=ln1_bf[:], in_=xs[ti][:], func=AF.Identity,
                bias=stat1[:, 6:7], scale=stat1[:, 5:6]),
              waits=lw, inc=c_ln1)

            for g in range(4):
                gs = 512 * (g % 2)
                tw = [(c_ln1, c_ln1.n), (c_trQ_ev, c_trQ_ev.n)]
                if sq == 0 and tt == 0 and g < 2:
                    pass
                else:
                    tw.append((c_trL_ev, c_trL_ev.n - 1))
                if sq > 0 and tt == 0 and g < 2:
                    tw += slot_marks["B"]
                for j in range(4):
                    blk = g * 4 + j
                    E("tensor", lambda e, blk=blk, gs=gs, j=j: e.matmul(
                        out=psB[:, gs + j * 128:gs + (j + 1) * 128],
                        lhsT=ln1_bf[:, blk * 128:(blk + 1) * 128], rhs=ident[:],
                        start=(j == 0), stop=(j == 3)),
                      waits=(tw if j == 0 else []),
                      inc=(c_trL if j == 3 else None))
                E("vector", lambda e, ti=ti, g=g, gs=gs: e.tensor_copy(
                    out=ln1T[ti][:, g * 512:(g + 1) * 512], in_=psB[:, gs:gs + 512]),
                  waits=[(c_trL, c_trL.n), (c_qa, c_qa.n), (c_ckv, c_ckv.n)],
                  inc=c_trL_ev)

            cw = [(c_trL_ev, c_trL_ev.n), (c_wa, wa_done)]
            if sq == 0 and tt == 0:
                pass
            elif tt == 0:
                cw += slot_marks["B"]
            else:
                cw.append((c_ckv_ev, c_ckv_ev.n))
            qw = [(c_trL_ev, c_trL_ev.n)]
            if sq == 0 and tt == 0:
                pass
            elif tt == 0:
                qw += slot_marks["A"]
            else:
                qw.append((c_qa_ev, c_qa_ev.n))
            for kb in range(16):
                for cci in range(5):
                    if cci < 3:
                        dst, do, wdt, woff = psA, cci * 512, 512, cci * 512
                    elif cci == 3:
                        dst, do, wdt, woff = psB, 1024, 512, 1536
                    else:
                        dst, do, wdt, woff = psB, 1536, 64, 2048
                    E("tensor", lambda e, ti=ti, kb=kb, dst=dst, do=do, wdt=wdt, woff=woff: e.matmul(
                        out=dst[:, do:do + wdt],
                        lhsT=ln1T[ti][:, kb * 128:(kb + 1) * 128],
                        rhs=wa_bf[:, kb * 2112 + woff:kb * 2112 + woff + wdt],
                        start=(kb == 0), stop=(kb == 15)),
                      waits=((qw + cw) if (cci == 0 and kb == 0) else []),
                      inc=((c_qa if cci == 2 else (c_ckv if cci == 4 else None))
                           if kb == 15 else None))
            E("scalar", lambda e: e.activation(
                out=ptpool[:, 0:512], in_=psB[:, 1024:1536], func=AF.Square,
                accum_out=stat1[:, 7:8]),
              waits=[(c_ckv, c_ckv.n)], inc=c_sc)
            E("scalar", lambda e: e.activation(
                out=stat1[:, 8:9], in_=stat1[:, 7:8], func=AF.Ln, bias=cvals[:, 1:2],
                scale=1.0 / KVL),
              waits=[(c_sc, c_sc.n)], inc=c_sc)
            E("scalar", lambda e: e.activation(
                out=stat1[:, 8:9], in_=stat1[:, 8:9], func=AF.Exp, scale=-0.5),
              waits=[(c_sc, c_sc.n)], inc=c_rskv)
            ckw = [(c_rskv, c_rskv.n)] + ([(c_trC, c_trC.n)] if tt > 0 or sq > 0 else [])
            E("vector", lambda e: e.tensor_scalar(
                out=ckv_n[:], in0=psB[:, 1024:1536], scalar1=stat1[:, 8:9], scalar2=None,
                op0=ALU.mult), waits=ckw, inc=c_ckv_ev)
            E("vector", lambda e, tt=tt: e.tensor_tensor(
                out=rope1[:, 0:DR], in0=psB[:, 1536:1600],
                in1=cos_t[:, tt * DR:(tt + 1) * DR], op=ALU.mult),
              waits=[(c_cos, cos_ready)], inc=c_v)
            E("vector", lambda e, tt=tt: e.tensor_tensor(
                out=rope2[:, 0:32], in0=psB[:, 1568:1600],
                in1=sinn_t[:, tt * DR:tt * DR + 32], op=ALU.mult), inc=c_v)
            E("vector", lambda e, tt=tt: e.tensor_tensor(
                out=rope2[:, 32:DR], in0=psB[:, 1536:1568],
                in1=sinn_t[:, tt * DR + 32:(tt + 1) * DR], op=ALU.mult), inc=c_v)
            E("vector", lambda e, tt=tt: e.tensor_add(
                out=kpe_tok[:, tt * DR:(tt + 1) * DR], in0=rope1[:, 0:DR],
                in1=rope2[:, 0:DR]),
              waits=[(c_v, c_v.n)], inc=c_ckv_ev)

            pass
            pass
            E("scalar", lambda e: e.activation(
                out=ptpool[:, 0:1536], in_=psA[:, 0:1536], func=AF.Square,
                accum_out=stat1[:, 9:10]),
              waits=[(c_qa, c_qa.n)], inc=c_sc)
            E("scalar", lambda e: e.activation(
                out=stat1[:, 10:11], in_=stat1[:, 9:10], func=AF.Ln, bias=cvals[:, 1:2],
                scale=1.0 / QL),
              waits=[(c_sc, c_sc.n)], inc=c_sc)
            E("scalar", lambda e: e.activation(
                out=stat1[:, 10:11], in_=stat1[:, 10:11], func=AF.Exp, scale=-0.5,
                bias=cvals[:, 2:3]),
              waits=[(c_sc, c_sc.n)], inc=c_rsqa)
            qnw = [(c_rsqa, c_rsqa.n)] + ([(c_trQ, c_trQ.n)] if tt > 0 or sq > 0 else [])
            E("vector", lambda e: e.tensor_scalar(
                out=qa_n[:], in0=psA[:, 0:1536], scalar1=stat1[:, 10:11], scalar2=None,
                op0=ALU.mult), waits=qnw, inc=c_qa_ev)

            tcw = [(c_ckv_ev, c_ckv_ev.n - 1), (c_trL_ev, c_trL_ev.n),
                   (c_trQ_ev, c_trQ_ev.n)]
            if tt > 0 or sq > 0:
                tcw.append((c_trC_ev, c_trC_ev.n))
            for j in range(4):
                E("tensor", lambda e, j=j: e.matmul(
                    out=psB[:, j * 128:(j + 1) * 128],
                    lhsT=ckv_n[:, j * 128:(j + 1) * 128], rhs=ident[:],
                    start=(j == 0), stop=(j == 3)),
                  waits=(tcw if j == 0 else []),
                  inc=(c_trC if j == 3 else None))
            E("vector", lambda e, tt=tt: e.tensor_copy(
                out=ckv_v()[:, :, tt * 128:(tt + 1) * 128],
                in_=psB[:, 0:512].rearrange("p (a t) -> p a t", t=128)),
              waits=[(c_trC, c_trC.n)], inc=c_trC_ev)

            for g in range(3):
                gs = 512 if g % 2 == 0 else 0
                tqw = [(c_qa_ev, c_qa_ev.n)]
                tqw.append((c_trC_ev, c_trC_ev.n) if gs == 0 else (c_trL_ev, c_trL_ev.n))
                if not (sq == 0 and tt == 0 and g == 0):
                    tqw.append((c_trQ_ev, c_trQ_ev.n - 1))
                for j in range(4):
                    blk = g * 4 + j
                    E("tensor", lambda e, blk=blk, gs=gs, j=j: e.matmul(
                        out=psB[:, gs + j * 128:gs + (j + 1) * 128],
                        lhsT=qa_n[:, blk * 128:(blk + 1) * 128], rhs=ident[:],
                        start=(j == 0), stop=(j == 3)),
                      waits=(tqw if j == 0 else []),
                      inc=(c_trQ if j == 3 else None))
                E("vector", lambda e, g=g, tt=tt, gs=gs: e.tensor_copy(
                    out=qa_v()[:, g * 4:(g + 1) * 4, tt * 128:(tt + 1) * 128],
                    in_=psB[:, gs:gs + 512].rearrange("p (a t) -> p a t", t=128)),
                  waits=[(c_trQ, c_trQ.n)], inc=c_trQ_ev)

        slot_marks["A"] = [(c_qa_ev, c_qa_ev.n)]
        slot_marks["B"] = [(c_trQ_ev, c_trQ_ev.n), (c_trC_ev, c_trC_ev.n),
                           (c_ckv_ev, c_ckv_ev.n)]

        wa_ctx.close()
        bctx = ExitStack()
        wqbpe_bf = bctx.enter_context(nc.sbuf_tensor(f"wqbpe_bf_{sq}", [128, 12 * 1024], BF16, side="right"))
        qpe_stage = [bctx.enter_context(nc.sbuf_tensor(f"qpes{i}_{sq}", [128, NH * DR], BF16, side="right"))
                     for i in range(2)]
        pe_cols = lambda kb: wqb[kb * 128:(kb + 1) * 128, :].rearrange(
            "p (h c) -> p h c", c=DN + DR)[:, :, DN:DN + DR]
        for kb in range(12):
            E("gpsimd", lambda e, kb=kb: e.dma_start(
                out=rope1[:].rearrange("p (h j) -> p h j", j=DR), in_=pe_cols(kb)),
              waits=([(c_qa, c_qa.n), (c_ckv, c_ckv.n), (c_ckv_ev, c_ckv_ev.n),
                      (c_trQ_ev, c_trQ_ev.n)]
                     if kb == 0 else [(c_wpef, c_wpef.n)]),
              inc=c_wpe, inc_amt=16)
            E("scalar", lambda e, kb=kb: e.mul(
                out=wqbpe_bf[:, kb * 1024:(kb + 1) * 1024], in_=rope1[:],
                mul=qlnw[:, kb:kb + 1]),
              waits=[(c_wpe, c_wpe.n)], inc=c_wpef)
        if sq == 0:
            emit_f_build()
        bpe_ev_at = {}
        for tt in range(NT):
            base = 1024 * (tt % 2)
            bw = [(c_wpef, c_wpef.n)]
            if tt < 2:
                bw += slot_marks["A"]
            else:
                bw.append((c_bpe_ev, bpe_ev_at[tt - 2]))
            for kb in range(12):
                for cci in range(2):
                    E("tensor", lambda e, tt=tt, kb=kb, cci=cci, base=base: e.matmul(
                        out=psA[:, base + cci * 512:base + (cci + 1) * 512],
                        lhsT=qa_v()[:, kb, tt * 128:(tt + 1) * 128],
                        rhs=wqbpe_bf[:, kb * 1024 + cci * 512:kb * 1024 + (cci + 1) * 512],
                        start=(kb == 0), stop=(kb == 11)),
                      waits=(bw if (cci == 0 and kb == 0) else []),
                      inc=(c_bpe if (kb == 11 and cci == 1) else None))
            pe = psA[:, base:base + 1024].rearrange("p (h j) -> p h j", j=DR)
            cos_b = cos_t[:, tt * DR:(tt + 1) * DR].unsqueeze(1).broadcast_to([128, NH, DR])
            sin_lo = sinn_t[:, tt * DR:tt * DR + 32].unsqueeze(1).broadcast_to([128, NH, 32])
            sin_hi = sinn_t[:, tt * DR + 32:(tt + 1) * DR].unsqueeze(1).broadcast_to([128, NH, 32])
            rw = [(c_bpe, c_bpe.n), (c_bpe_ev, c_bpe_ev.n)]
            if tt >= 2:
                rw.append((c_qpew[tt % 2], c_qpew[tt % 2].n))
            E("vector", lambda e, pe=pe, cos_b=cos_b: e.tensor_tensor(
                out=rope1[:].rearrange("p (h j) -> p h j", j=DR), in0=pe, in1=cos_b,
                op=ALU.mult), waits=rw, inc=c_v)
            E("vector", lambda e, pe=pe, sin_lo=sin_lo: e.tensor_tensor(
                out=rope2[:].rearrange("p (h j) -> p h j", j=DR)[:, :, 0:32],
                in0=pe[:, :, 32:DR], in1=sin_lo, op=ALU.mult), inc=c_v)
            E("vector", lambda e, pe=pe, sin_hi=sin_hi: e.tensor_tensor(
                out=rope2[:].rearrange("p (h j) -> p h j", j=DR)[:, :, 32:DR],
                in0=pe[:, :, 0:32], in1=sin_hi, op=ALU.mult), inc=c_v)
            E("vector", lambda e, tt=tt: e.tensor_add(
                out=qpe_stage[tt % 2][:], in0=rope1[:], in1=rope2[:]),
              waits=[(c_v, c_v.n)], inc=c_bpe_ev)
            bpe_ev_at[tt] = c_bpe_ev.n
            E("scalar", lambda e, tt=tt: e.dma_start(
                out=qpe_s[tt * 128:(tt + 1) * 128, :], in_=qpe_stage[tt % 2][:]),
              waits=[(c_bpe_ev, c_bpe_ev.n)], inc=c_qpew[tt % 2], inc_amt=16)
        slot_marks["A"] = [(c_bpe_ev, c_bpe_ev.n)]

        for g in range(4):
            gs = 512 * (g % 2)
            kw = [(c_ckv_ev, c_ckv_ev.n)]
            if g < 2:
                kw += slot_marks["B"]
            else:
                kw.append((c_tkpe_ev, c_tkpe_ev.n - 1))
            for j in range(4):
                tt2 = g * 4 + j
                E("tensor", lambda e, tt2=tt2, gs=gs, j=j: e.matmul(
                    out=psB[0:64, gs + j * 128:gs + (j + 1) * 128],
                    lhsT=kpe_tok[:, tt2 * DR:(tt2 + 1) * DR], rhs=ident[:],
                    start=(j == 0), stop=(j == 3)),
                  waits=(kw if j == 0 else []),
                  inc=(c_tkpe if j == 3 else None))
            E("vector", lambda e, g=g, gs=gs: e.tensor_copy(
                out=kpeT_aug[0:64, g * 512:(g + 1) * 512], in_=psB[0:64, gs:gs + 512]),
              waits=[(c_tkpe, c_tkpe.n)], inc=c_tkpe_ev)
        slot_marks["B"] = [(c_tkpe_ev, c_tkpe_ev.n)]
        kpesq_pending = [True]

        bctx.close()
        close_right()
        open_right()
        ot_stage = [rsb(f"otst{i}", [128, S], BF16) for i in range(2)]
        stage_w = rsb("stage_w", [128, 1536], F32)
        sq_all = rsb("sq_all", [128, S], BF16)
        sq_pe = rsb("sq_pe", [64, S], BF16)
        kpesq = rsb("kpesq", [64, S], BF16)
        neg_mh = rsb("neg_mh", [128, 32], F32)
        stat_sa = rsb("stat_sa", [128, 32], F32)
        stat_sb = rsb("stat_sb", [128, 32], F32)
        kmax = rsb("kmax", [128, 1], F32)
        kmax_bf = rsb("kmax_bf", [128, 1], BF16)
        kpemax_bf = rsb("kpemax_bf", [64, 1], BF16)
        lnmk = rsb("lnmk", [1, 1], BF16)
        lnmk_col = rsb("lnmk_col", [128, 1], F32)
        invcol = rsb("invcol", [128, 16], F32)
        invT = rsb("invT", [16, 128], F32)
        invb = rsb("invb", [128, D], F32)
        invThi = rsb("invThi", [16, 128], BF16)
        invTlo = rsb("invTlo", [16, 128], BF16)
        invflat = rsb("invflat", [1, 4096], BF16)
        wqbn_h = [rsb(f"wqbnh{i}", [128, 12 * 128], BF16) for i in range(2)]
        wkvbk_h = [rsb(f"wkvbkh{i}", [128, 4 * 128], BF16) for i in range(2)]
        wkvbv_h = [rsb(f"wkvbvh{i}", [128, 4 * 128], BF16) for i in range(2)]
        QnT = [rsb(f"QnT{i}", [128, S], BF16) for i in range(2)]
        QpeT_aug = [rsb(f"QpeTa{i}", [66, S], BF16) for i in range(2)]
        KnT = [rsb(f"KnT{i}", [128, S], BF16) for i in range(2)]
        Vh = [rsb(f"Vh{i}", [128, S], BF16) for i in range(2)]
        qpe_sl = [rsb(f"qpesl{i}", [128, NH * DR], BF16) for i in range(2)]

        gate_A = [(c_bpe, c_bpe.n), (c_bpe_ev, c_bpe_ev.n), (c_tkpe, c_tkpe.n),
                  (c_wpef, c_wpef.n), (c_qpew[0], c_qpew[0].n), (c_qpew[1], c_qpew[1].n), (c_tkpe_ev, c_tkpe_ev.n)]

        def head_dmas(h, first=False):
            hb = h % 2
            hw = list(gate_A) if first else [(c_whf, c_whf.n), (c_bn, c_bn.n)]
            E("sync", lambda e, h=h: e.dma_start(
                out=stage_w[:].rearrange("p (a c) -> p a c", c=128),
                in_=wqb[:, h * 192:h * 192 + 128].rearrange("(a p) c -> p a c", p=128)),
              waits=hw, inc=c_wh, inc_amt=16)
            for kb in range(12):
                E("scalar", lambda e, kb=kb, hb=hb: e.mul(
                    out=wqbn_h[hb][:, kb * 128:(kb + 1) * 128],
                    in_=stage_w[:, kb * 128:(kb + 1) * 128], mul=qlnw[:, kb:kb + 1]),
                  waits=([(c_wh, c_wh.n)] if kb == 0 else []),
                  inc=(c_whf if kb == 11 else None))
            E("sync", lambda e, h=h: e.dma_start(
                out=stage_w[:, 0:512].rearrange("p (a c) -> p a c", c=128),
                in_=wkvb[:, h * 256:h * 256 + 128].rearrange("(a p) c -> p a c", p=128)),
              waits=[(c_whf, c_whf.n)], inc=c_wh, inc_amt=16)
            for kb in range(4):
                E("scalar", lambda e, kb=kb, hb=hb: e.mul(
                    out=wkvbk_h[hb][:, kb * 128:(kb + 1) * 128],
                    in_=stage_w[:, kb * 128:(kb + 1) * 128], mul=kvlnw[:, kb:kb + 1]),
                  waits=([(c_wh, c_wh.n)] if kb == 0 else []),
                  inc=(c_whf if kb == 3 else None))
            E("sync", lambda e, h=h: e.dma_start(
                out=stage_w[:, 0:512].rearrange("p (a c) -> p a c", c=128),
                in_=wkvb[:, h * 256 + 128:h * 256 + 256].rearrange("(a p) c -> p a c", p=128)),
              waits=[(c_whf, c_whf.n)], inc=c_wh, inc_amt=16)
            for kb in range(4):
                E("scalar", lambda e, kb=kb, hb=hb: e.mul(
                    out=wkvbv_h[hb][:, kb * 128:(kb + 1) * 128],
                    in_=stage_w[:, kb * 128:(kb + 1) * 128], mul=kvlnw[:, kb:kb + 1]),
                  waits=([(c_wh, c_wh.n)] if kb == 0 else []),
                  inc=(c_whf if kb == 3 else None))
            E("sync", lambda e, h=h, hb=hb: e.dma_start(
                out=qpe_sl[hb][:].rearrange("p (a j) -> p a j", j=DR),
                in_=qpe_s[:, h * DR:(h + 1) * DR].rearrange("(a p) j -> p a j", p=128)),
              waits=[(c_qpew[0], c_qpew[0].n), (c_qpew[1], c_qpew[1].n), (c_tq, c_tq.n)], inc=c_qpeh, inc_amt=16)

        def prep(h, pb0, pb1):
            hb = h % 2
            whf_mark = (c_whf, c_whf.n)
            qpe_mark = (c_qpeh, c_qpeh.n)
            groups = []

            def bn_group(cci):
                def g(bps, boff, waits):
                    for kb in range(12):
                        E("tensor", lambda e, kb=kb, cci=cci, bps=bps, boff=boff: e.matmul(
                            out=bps[:, boff:boff + 512],
                            lhsT=wqbn_h[hb][:, kb * 128:(kb + 1) * 128],
                            rhs=qa_v()[:, kb, cci * 512:(cci + 1) * 512],
                            start=(kb == 0), stop=(kb == 11)),
                          waits=(waits if kb == 0 else []),
                          inc=(c_bn if kb == 11 else None))
                    E("scalar", lambda e, cci=cci, bps=bps, boff=boff: e.activation(
                        out=QnT[hb][:, cci * 512:(cci + 1) * 512],
                        in_=bps[:, boff:boff + 512], func=AF.Copy),
                      waits=[(c_bn, c_bn.n)], inc=c_bn_ev)
                    return (c_bn_ev, c_bn_ev.n)
                return g, [whf_mark]

            def c1_group(cci):
                def g(bps, boff, waits):
                    for kb in range(4):
                        E("tensor", lambda e, kb=kb, cci=cci, bps=bps, boff=boff: e.matmul(
                            out=bps[:, boff:boff + 512],
                            lhsT=wkvbk_h[hb][:, kb * 128:(kb + 1) * 128],
                            rhs=ckv_v()[:, kb, cci * 512:(cci + 1) * 512],
                            start=(kb == 0), stop=(kb == 3)),
                          waits=(waits if kb == 0 else []),
                          inc=(c_c1 if kb == 3 else None))
                    E("scalar", lambda e, cci=cci, bps=bps, boff=boff: e.activation(
                        out=KnT[hb][:, cci * 512:(cci + 1) * 512],
                        in_=bps[:, boff:boff + 512], func=AF.Copy),
                      waits=[(c_c1, c_c1.n)], inc=c_c1_ev)
                    return (c_c1_ev, c_c1_ev.n)
                return g, [whf_mark]

            def c2_group(gg):
                def g(bps, boff, waits):
                    first = [True]
                    for ti2 in range(4):
                        tt2 = gg * 4 + ti2
                        for kb in range(4):
                            E("tensor", lambda e, tt2=tt2, kb=kb, ti2=ti2, bps=bps, boff=boff: e.matmul(
                                out=bps[:, boff + ti2 * 128:boff + (ti2 + 1) * 128],
                                lhsT=ckv_v()[:, kb, tt2 * 128:(tt2 + 1) * 128],
                                rhs=wkvbv_h[hb][:, kb * 128:(kb + 1) * 128],
                                start=(ti2 == 0 and kb == 0), stop=(ti2 == 3 and kb == 3)),
                              waits=(waits if first[0] else []),
                              inc=(c_c2 if (ti2 == 3 and kb == 3) else None))
                            first[0] = False
                    E("vector", lambda e, gg=gg, bps=bps, boff=boff: e.tensor_copy(
                        out=Vh[hb][:, gg * 512:(gg + 1) * 512],
                        in_=bps[:, boff:boff + 512]),
                      waits=[(c_c2, c_c2.n)], inc=c_c2_ev)
                    return (c_c2_ev, c_c2_ev.n)
                return g, [whf_mark]

            def tq_group(gg):
                def g(bps, boff, waits):
                    for j in range(4):
                        tt2 = gg * 4 + j
                        E("tensor", lambda e, tt2=tt2, j=j, bps=bps, boff=boff: e.matmul(
                            out=bps[0:64, boff + j * 128:boff + (j + 1) * 128],
                            lhsT=qpe_sl[hb][:, tt2 * DR:(tt2 + 1) * DR], rhs=ident[:],
                            start=(j == 0), stop=(j == 3)),
                          waits=(waits if j == 0 else []),
                          inc=(c_tq if j == 3 else None))
                    E("vector", lambda e, gg=gg, bps=bps, boff=boff: e.tensor_copy(
                        out=QpeT_aug[hb][0:64, gg * 512:(gg + 1) * 512],
                        in_=bps[0:64, boff:boff + 512]),
                      waits=[(c_tq, c_tq.n)], inc=c_tq_ev)
                    return (c_tq_ev, c_tq_ev.n)
                return g, [qpe_mark]

            for cci in range(4):
                groups.append(bn_group(cci))
            for cci in range(4):
                groups.append(c1_group(cci))
            for gg in range(4):
                groups.append(c2_group(gg))
            for gg in range(4):
                groups.append(tq_group(gg))

            banks = [(psA, 0), (psA, 512), (psA, 1024), (psA, 1536),
                     (psB, 1024), (psB, 1536)]
            last = {i: None for i in range(6)}
            for i, (g, extra) in enumerate(groups):
                bsel = i % 6
                bps, boff = banks[bsel]
                w = list(extra)
                if last[bsel] is None:
                    w += (pb1 if bsel == 5 else pb0)
                else:
                    w.append(last[bsel])
                last[bsel] = g(bps, boff, w)
            marks = [last[4] or pb0[0], last[5] or pb0[0],
                     (c_bn_ev, c_bn_ev.n), (c_c1_ev, c_c1_ev.n),
                     (c_c2_ev, c_c2_ev.n), (c_tq_ev, c_tq_ev.n)]
            return marks, [last[i] for i in range(4)]

        def emit_mhat(h, pm):
            hb = h % 2
            if kpesq_pending[0]:
                kpesq_pending[0] = False
                E("vector", lambda e: e.tensor_mul(
                    out=kpesq[0:64, :], in0=kpeT_aug[0:64, :], in1=kpeT_aug[0:64, :]),
                  waits=[(c_tkpe_ev, c_tkpe_ev.n)], inc=c_mv)
                E("vector", lambda e: e.reduce_max(
                    out=kmax[0:64, :], in_=kpesq[0:64, :], axis=AX.X),
                  waits=[(c_mv, c_mv.n)], inc=c_mv)
                E("vector", lambda e: e.tensor_copy(
                    out=kpemax_bf[:], in_=kmax[0:64, :]),
                  waits=[(c_mv, c_mv.n)], inc=c_mv)
            E("scalar", lambda e, hb=hb: e.activation(
                out=sq_all[:], in_=KnT[hb][:], func=AF.Square),
              waits=[(c_mq, c_mq.n), (c_c1_ev, c_c1_ev.n), (c_sc, c_sc.n)],
              inc=c_sc)
            E("vector", lambda e: e.reduce_max(
                out=kmax[:], in_=sq_all[:], axis=AX.X),
              waits=[(c_sc, c_sc.n), (c_mv, c_mv.n)], inc=c_mv)
            E("vector", lambda e: e.tensor_copy(out=kmax_bf[:], in_=kmax[:]),
              waits=[(c_mv, c_mv.n)], inc=c_mv)
            km_mark = c_mv.n
            E("scalar", lambda e, hb=hb: e.activation(
                out=sq_all[:], in_=QnT[hb][:], func=AF.Square),
              waits=[(c_mv, km_mark), (c_bn_ev, c_bn_ev.n), (c_sc, c_sc.n)],
              inc=c_sc)
            E("scalar", lambda e, hb=hb: e.activation(
                out=sq_pe[0:64, :], in_=QpeT_aug[hb][0:64, :], func=AF.Square),
              waits=[(c_tq_ev, c_tq_ev.n), (c_sc, c_sc.n)], inc=c_sc)
            qsq_mark = c_sc.n
            smw = [(c_mv, km_mark)] + list(pm[0:2]) + \
                [(c_sdone, c_sdone.n)] + list(bank3_marks)
            E("tensor", lambda e: e.matmul(
                out=psB[0:1, 1700:1701], lhsT=kmax_bf[:], rhs=ones_col[:],
                start=True, stop=False),
              waits=smw)
            E("tensor", lambda e: e.matmul(
                out=psB[0:1, 1700:1701], lhsT=kpemax_bf[:], rhs=ones_col[0:64, :],
                start=False, stop=True),
              inc=c_mt)
            E("scalar", lambda e: e.activation(
                out=lnmk[:], in_=psB[0:1, 1700:1701], func=AF.Ln,
                scale=SCALE * SCALE * 1.05),
              waits=[(c_mt, c_mt.n)], inc=c_sc)
            lnmk_mark = c_sc.n
            for qt in range(NT):
                E("tensor", lambda e, qt=qt: e.matmul(
                    out=psB[:, 1040 + qt:1041 + qt],
                    lhsT=sq_all[:, qt * 128:(qt + 1) * 128], rhs=ones_col[:],
                    start=True, stop=False),
                  waits=([(c_sc, qsq_mark)] if qt == 0 else []))
                E("tensor", lambda e, qt=qt: e.matmul(
                    out=psB[:, 1040 + qt:1041 + qt],
                    lhsT=sq_pe[0:64, qt * 128:(qt + 1) * 128], rhs=ones_col[0:64, :],
                    start=False, stop=True),
                  inc=(c_mq if qt == NT - 1 else None))
            E("tensor", lambda e: e.matmul(
                out=psB[:, 1704:1705], lhsT=halfs1[:], rhs=lnmk[:],
                start=True, stop=True),
              waits=[(c_sc, lnmk_mark)], inc=c_mt)
            E("vector", lambda e: e.tensor_copy(out=lnmk_col[:], in_=psB[:, 1704:1705]),
              waits=[(c_mt, c_mt.n)], inc=c_mv)
            lc_mark = c_mv.n
            E("scalar", lambda e, hb=hb: e.activation(
                out=neg_mh[:, hb * 16:(hb + 1) * 16], in_=psB[:, 1040:1056], func=AF.Ln),
              waits=[(c_mq, c_mq.n)], inc=c_sc)
            E("scalar", lambda e, hb=hb: e.activation(
                out=neg_mh[:, hb * 16:(hb + 1) * 16],
                in_=neg_mh[:, hb * 16:(hb + 1) * 16], func=AF.Exp, scale=0.5,
                bias=lnmk_col[:]),
              waits=[(c_sc, c_sc.n), (c_mv, lc_mark)], inc=c_sc)
            E("vector", lambda e, hb=hb: e.tensor_scalar(
                out=neg_mh[:, hb * 16:(hb + 1) * 16],
                in0=neg_mh[:, hb * 16:(hb + 1) * 16], scalar1=-1.0,
                scalar2=None, op0=ALU.mult),
              waits=[(c_sc, c_sc.n)], inc=c_mv)
            return (c_mv, c_mv.n)

        head_dmas(0, first=True)
        otw_at = {}
        bank3_marks = []
        bcast_ev_mark = [None]
        prev_w2 = [None]

        def augstats(h, mh_mark):
            hb = h % 2
            E("vector", lambda e, hb=hb: e.tensor_copy(
                out=stat_h[:, 0:16], in_=neg_mh[:, hb * 16:(hb + 1) * 16]),
              waits=[mh_mark, (c_statT, c_statT.n)], inc=c_v)
            E("vector", lambda e, hb=hb: e.tensor_sub(
                out=stat_h[:, 16:32], in0=neg_mh[:, hb * 16:(hb + 1) * 16],
                in1=stat_h[:, 0:16]),
              waits=[(c_v, c_v.n)], inc=c_stat)
            E("tensor", lambda e: e.matmul(
                out=psB[0:32, 1536:1664], lhsT=stat_h[:], rhs=ident[:],
                start=True, stop=True),
              waits=[(c_stat, c_stat.n)] + list(bank3_marks), inc=c_statT)
            E("vector", lambda e: e.tensor_copy(out=sb_st[:], in_=psB[0:32, 1536:1664]),
              waits=[(c_statT, c_statT.n), (c_aug, c_aug.n)], inc=c_statT_ev)
            E("scalar", lambda e, hb=hb: e.dma_start(
                out=QpeT_aug[hb][64:65, :], in_=sb_st[0:16, :]),
              waits=[(c_statT_ev, c_statT_ev.n)], inc=c_aug, inc_amt=16)
            E("scalar", lambda e, hb=hb: e.dma_start(
                out=QpeT_aug[hb][65:66, :], in_=sb_st[16:32, :]),
              inc=c_aug, inc_amt=16)
            return (c_statT_ev, c_statT_ev.n)

        prep_marks, prep_psA = prep(0, slot_marks["A"] + slot_marks["B"], slot_marks["B"])
        mh_ready = emit_mhat(0, prep_marks)
        bank3_marks = [prep_marks[1], mh_ready]
        st_ev_mark = augstats(0, mh_ready)
        bank3_marks = [st_ev_mark]

        for h in range(NH):
            hb = h % 2
            if h + 1 < NH:
                head_dmas(h + 1)

            aug_mark = (c_aug, c_aug.n)
            av_head_base = c_av.n
            binit = [(c_pt, c_pt.n)]
            if bcast_ev_mark[0]:
                binit.append(bcast_ev_mark[0])
            if h == 0 and prev_w2[0]:
                binit.append(prev_w2[0])
            st_last = {0: list(binit), 1: list(binit), 2: list(binit)}
            sg_first = list(bank3_marks)
            st_ctr = 0
            av_abs = []
            pm_ev = [m for m in prep_marks if m]
            for kt in range(NT):
                qmin = kt * 128
                pto = 2048 * (kt % 2)
                ch_list = _chunks(qmin, S)
                pt_of_chunk = []
                for (q0, wd) in ch_list:
                    bsel = st_ctr % 3
                    st_ctr += 1
                    boff = 512 * bsel
                    stw = list(st_last[bsel]) if isinstance(st_last[bsel], list) \
                        else [st_last[bsel]]
                    E("tensor", lambda e, hb=hb, kt=kt, q0=q0, wd=wd, boff=boff: e.matmul(
                        out=psB[:, boff:boff + wd],
                        lhsT=KnT[hb][:, kt * 128:(kt + 1) * 128],
                        rhs=QnT[hb][:, q0:q0 + wd], start=True, stop=False),
                      waits=stw)
                    E("tensor", lambda e, hb=hb, kt=kt, q0=q0, wd=wd, boff=boff: e.matmul(
                        out=psB[:, boff:boff + wd],
                        lhsT=kpeT_aug[:, kt * 128:(kt + 1) * 128],
                        rhs=QpeT_aug[hb][:, q0:q0 + wd], start=False, stop=True),
                      waits=([aug_mark] if kt == 0 and q0 == qmin else []),
                      inc=c_st)
                    ew = [(c_st, c_st.n)]
                    if q0 == qmin:
                        if kt >= 2:
                            ew.append((c_av, av_abs[kt - 2]))
                        elif av_head_base > 0:
                            ew.append((c_av, av_head_base - (1 if kt == 0 else 0)))
                    E("scalar", lambda e, q0=q0, wd=wd, boff=boff, pto=pto: e.activation(
                        out=ptpool[:, pto + q0:pto + q0 + wd],
                        in_=psB[:, boff:boff + wd], func=AF.Exp),
                      waits=ew, inc=c_pt)
                    st_last[bsel] = (c_pt, c_pt.n)
                    pt_of_chunk.append(c_pt.n)
                    if q0 == qmin:
                        E("vector", lambda e, qmin=qmin, pto=pto: e.tensor_tensor(
                            out=ptpool[:, pto + qmin:pto + qmin + 128],
                            in0=ptpool[:, pto + qmin:pto + qmin + 128],
                            in1=maskT[:], op=ALU.mult),
                          waits=[(c_pt, c_pt.n)], inc=c_ptm)
                for i2, (q0, wd) in enumerate(ch_list):
                    avw = [(c_pt, pt_of_chunk[i2])]
                    if i2 == 0:
                        avw.append((c_ptm, c_ptm.n))
                        if kt == 0:
                            avw += pm_ev
                            if h > 0 or sq > 0:
                                avw.append((c_ot_ev, c_ot_ev.n))
                    E("tensor", lambda e, hb=hb, kt=kt, q0=q0, wd=wd, pto=pto: e.matmul(
                        out=psA[:, q0:q0 + wd],
                        lhsT=Vh[hb][:, kt * 128:(kt + 1) * 128],
                        rhs=ptpool[:, pto + q0:pto + q0 + wd],
                        start=(kt == 0),
                        stop=(kt == min(NT - 1, (q0 + wd - 1) // 128))),
                      waits=avw, inc=c_av)
                av_abs.append(c_av.n)
                for qt in range(kt, NT):
                    sgw = []
                    if kt == 0 and qt == 0:
                        sgw = list(sg_first)
                    E("tensor", lambda e, kt=kt, qt=qt, pto=pto: e.matmul(
                        out=psB[:, 1536 + qt:1537 + qt],
                        lhsT=ptpool[:, pto + qt * 128:pto + (qt + 1) * 128],
                        rhs=ones_col[:],
                        start=(kt == 0 and qt == 0),
                        stop=(kt == NT - 1 and qt == NT - 1)),
                      waits=sgw,
                      inc=(c_sg if qt == NT - 1 else None))

            E("vector", lambda e, h=h: e.tensor_copy(
                out=ot_stage[h % 2][:], in_=psA[:, :]),
              waits=[(c_av, c_av.n)] +
                    ([(c_otw, c_otw.n)] if h >= 2 else []),
              inc=c_ot_ev)
            ev_mark = (c_ot_ev, c_ot_ev.n)
            E("vector", lambda e: e.reciprocal(out=invcol[:], in_=psB[:, 1536:1552]),
              waits=[(c_sg, c_sg.n)], inc=c_mv)
            rec_mark = (c_mv, c_mv.n)
            E("tensor", lambda e: e.matmul(
                out=psB[0:16, 1552:1680], lhsT=invcol[:], rhs=identf[:],
                start=True, stop=True),
              waits=[rec_mark], inc=c_statT)
            E("vector", lambda e: e.tensor_copy(out=invT[:], in_=psB[0:16, 1552:1680]),
              waits=[(c_statT, c_statT.n)], inc=c_v)
            E("vector", lambda e: e.tensor_copy(out=invThi[:], in_=invT[:]),
              waits=[(c_v, c_v.n)], inc=c_v)
            E("vector", lambda e: e.tensor_sub(out=invTlo[:], in0=invT[:], in1=invThi[:]),
              waits=[(c_v, c_v.n)], inc=c_v)
            iv_mark = (c_v, c_v.n)
            E("scalar", lambda e: e.dma_start(out=invflat[:, 0:2048], in_=invThi[:]),
              waits=[iv_mark, (c_aug, c_aug.n)] +
                    ([prev_w2[0]] if prev_w2[0] else []),
              inc=c_aug, inc_amt=16)
            E("sync", lambda e: e.dma_start(out=invflat[:, 2048:4096], in_=invTlo[:]),
              waits=[iv_mark, (c_flo, c_flo.n)] +
                    ([prev_w2[0]] if prev_w2[0] else []),
              inc=c_flo, inc_amt=16)
            flat_mark = [(c_aug, c_aug.n), (c_flo, c_flo.n)]
            bank3_marks = [(c_statT_ev, c_statT_ev.n), rec_mark, iv_mark]

            if h + 1 < NH:
                prep_marks, prep_psA = prep(
                    h + 1,
                    [ev_mark, (c_pt, c_pt.n)],
                    [(c_pt, c_pt.n)] + list(bank3_marks))

            pm01 = [prep_marks[0]] if h + 1 < NH else []
            for qc in range(2):
                bw2 = list(flat_mark) + [(c_pt, c_pt.n)] if qc == 0 else []
                E("tensor", lambda e, qc=qc: e.matmul(
                    out=psB[:, qc * 512:(qc + 1) * 512], lhsT=ones1[:],
                    rhs=invflat[:, qc * 512:(qc + 1) * 512], start=True, stop=False),
                  waits=bw2)
                E("tensor", lambda e, qc=qc: e.matmul(
                    out=psB[:, qc * 512:(qc + 1) * 512], lhsT=ones1[:],
                    rhs=invflat[:, 2048 + qc * 512:2048 + (qc + 1) * 512],
                    start=False, stop=True),
                  inc=(c_sg if qc == 1 else None))
            sgm1 = (c_sg, c_sg.n)
            E("scalar", lambda e: e.activation(
                out=invb[:, 0:1024], in_=psB[:, 0:1024], func=AF.Copy),
              waits=[sgm1, (c_sc, c_sc.n)], inc=c_sc)
            ib1 = (c_sc, c_sc.n)
            for qc in range(2, 4):
                bw2 = [ib1] if qc == 2 else []
                E("tensor", lambda e, qc=qc: e.matmul(
                    out=psB[:, (qc - 2) * 512:(qc - 1) * 512], lhsT=ones1[:],
                    rhs=invflat[:, qc * 512:(qc + 1) * 512], start=True, stop=False),
                  waits=bw2)
                E("tensor", lambda e, qc=qc: e.matmul(
                    out=psB[:, (qc - 2) * 512:(qc - 1) * 512], lhsT=ones1[:],
                    rhs=invflat[:, 2048 + qc * 512:2048 + (qc + 1) * 512],
                    start=False, stop=True),
                  inc=(c_sg if qc == 3 else None))
            sgm2 = (c_sg, c_sg.n)
            prev_w2[0] = sgm2
            E("scalar", lambda e: e.activation(
                out=invb[:, 1024:2048], in_=psB[:, 0:1024], func=AF.Copy),
              waits=[sgm2, (c_sc, c_sc.n)], inc=c_sc)
            ib2 = (c_sc, c_sc.n)
            bcast_ev_mark[0] = ib2

            if h + 1 < NH:
                mh_ready = emit_mhat(h + 1, prep_marks)

            E("vector", lambda e, h=h: e.tensor_mul(
                out=ot_stage[h % 2][:], in0=ot_stage[h % 2][:], in1=invb[:]),
              waits=[ev_mark, ib2, (c_v, c_v.n)], inc=c_v)
            mul_mark = (c_v, c_v.n)
            E("scalar", lambda e, h=h: e.dma_start(
                out=ot_s[h], in_=ot_stage[h % 2][:]),
              waits=[mul_mark], inc=c_otw, inc_amt=16)
            otw_at[h] = c_otw.n

            if h + 1 < NH:
                bank3_marks = [prep_marks[1], mh_ready]
                st_ev_mark = augstats(h + 1, mh_ready)
                bank3_marks = [st_ev_mark]

        slot_marks["A"] = [(c_ot_ev, c_ot_ev.n)]
        slot_marks["B"] = [(c_pt, c_pt.n), (c_v, c_v.n), (c_sg, c_sg.n),
                           (c_statT_ev, c_statT_ev.n)] + list(bank3_marks)

        gate_att = [(c_av, c_av.n), (c_ot_ev, c_ot_ev.n), (c_otw, c_otw.n),
                    (c_sg, c_sg.n), (c_v, c_v.n), (c_mv, c_mv.n),
                    (c_whf, c_whf.n),
                    (c_bn_ev, c_bn_ev.n), (c_c1_ev, c_c1_ev.n), (c_c2_ev, c_c2_ev.n),
                    (c_tq_ev, c_tq_ev.n), (c_statT_ev, c_statT_ev.n),
                    (c_aug, c_aug.n), (c_ptm, c_ptm.n)]
        g2ctx.close()
        close_right()
        open_right()
        wo_bf = rsb("wo_bf", [128, NH * 2048], BF16)
        OT_wo = rsb("OT_wo", [128, NH * 2048], BF16)
        x_re = [rsb(f"xre{i}", [128, D], F32) for i in range(2)]
        r1_sb = [rsb(f"r1sb{i}", [128, D], F32) for i in range(2)]
        ln2w = [rsb(f"ln2w{i}", [128, D], BF16) for i in range(2)]
        ln2ts = [rsb(f"ln2ts{i}", [128, D], BF16) for i in range(2)]

        for hh in range(NH):
            E("gpsimd", lambda e, hh=hh: e.dma_start(
                out=wo_bf[:, hh * 2048:(hh + 1) * 2048],
                in_=wo[hh * 128:(hh + 1) * 128, :]),
              waits=(list(gate_att) + [(c_fb, c_fb.n), (c_ffn, c_ffn.n)] if hh == 0 else []),
              inc=c_wo, inc_amt=16)
        for hh in range(NH):
            E("sync", lambda e, hh=hh: e.dma_start(
                out=OT_wo[:, hh * 2048:(hh + 1) * 2048], in_=ot_s[hh]),
              waits=(list(gate_att) if hh == 0 else []),
              inc=c_otld, inc_amt=16)
        wo_loaded = c_wo.n
        ot_loaded = c_otld.n
        r1_at = {}
        r1wp_at = {}
        trFw_at = {}
        l2w_at = {}
        evict_at = {}
        ln2_at = {}

        def emit_trW(tt):
            bi = tt % 2
            ps = psA if tt % 2 == 0 else psB
            for g in range(4):
                gs = 512 * (g % 2)
                tw = [(c_ln2, ln2_at[tt])]
                if g < 2:
                    tw.append((c_r1, r1_at[tt]))
                else:
                    tw.append((c_trF_ev, c_trF_ev.n - 1))
                for j in range(4):
                    blk = g * 4 + j
                    E("tensor", lambda e, blk=blk, gs=gs, j=j, bi=bi, ps=ps: e.matmul(
                        out=ps[:, gs + j * 128:gs + (j + 1) * 128],
                        lhsT=ln2w[bi][:, blk * 128:(blk + 1) * 128], rhs=ident[:],
                        start=(j == 0), stop=(j == 3)),
                      waits=(tw if j == 0 else []),
                      inc=(c_trF if j == 3 else None))
                evw = [(c_trF, c_trF.n)]
                if g == 0 and tt >= 2:
                    evw.append((c_l2w[bi], l2w_at[tt - 2]))
                E("scalar", lambda e, g=g, gs=gs, bi=bi, ps=ps: e.activation(
                    out=ln2ts[bi][:, g * 512:(g + 1) * 512], in_=ps[:, gs:gs + 512],
                    func=AF.Copy),
                  waits=evw, inc=c_trF_ev)
            trFw_at[tt] = c_trF.n
            evict_at[tt] = c_trF_ev.n
            E("scalar", lambda e, tt=tt, bi=bi: e.dma_start(
                out=ln2t_s[tt], in_=ln2ts[bi][:]),
              waits=[(c_trF_ev, c_trF_ev.n)], inc=c_l2w[bi], inc_amt=16)
            l2w_at[tt] = c_l2w[bi].n

        for tt in range(NT):
            bi = tt % 2
            sl = "A" if tt % 2 == 0 else "B"
            ps = psA if sl == "A" else psB
            xrw = list(gate_att) if tt < 2 else [(c_r1, r1_at[tt - 2])]
            E("sync", lambda e, sq=sq, tt=tt, bi=bi: e.dma_start(
                out=x_re[bi][:], in_=x_in[sq, tt * 128:(tt + 1) * 128, :]),
              waits=xrw, inc=c_xre[bi], inc_amt=16)
            xre_mark = c_xre[bi].n
            w0 = list(slot_marks[sl]) if tt < 2 else \
                [(c_r1, r1_at[tt - 2]), (c_trF_ev, evict_at[tt - 2])]
            if tt == 0:
                w0 += [(c_wo, wo_loaded), (c_otld, ot_loaded)]
            first = True
            for hh in range(NH):
                for cci in range(4):
                    E("tensor", lambda e, tt=tt, hh=hh, cci=cci, ps=ps: e.matmul(
                        out=ps[:, cci * 512:(cci + 1) * 512],
                        lhsT=OT_wo[:, hh * 2048 + tt * 128:hh * 2048 + (tt + 1) * 128],
                        rhs=wo_bf[:, hh * 2048 + cci * 512:hh * 2048 + (cci + 1) * 512],
                        start=(hh == 0), stop=(hh == NH - 1)),
                      waits=(w0 if first else []),
                      inc=(c_woG if (hh == NH - 1 and cci == 3) else None))
                    first = False
            if tt >= 1:
                emit_trW(tt - 1)
            E("vector", lambda e, bi=bi, ps=ps: e.tensor_add(
                out=r1_sb[bi][:], in0=ps[:, :], in1=x_re[bi][:]),
              waits=[(c_woG, c_woG.n), (c_xre[bi], xre_mark)] +
                    ([(c_r1wp[bi], r1wp_at[tt - 2]), (c_ln2, ln2_at[tt - 2])]
                     if tt >= 2 else []),
              inc=c_r1)
            r1_at[tt] = c_r1.n
            E("scalar", lambda e, bi=bi: e.activation(
                out=ptpool[:, 0:D], in_=r1_sb[bi][:], func=AF.Copy, scale=-1.0 / D,
                accum_out=stat1[:, 2:3]),
              waits=[(c_r1, c_r1.n)], inc=c_sc)
            E("scalar", lambda e, bi=bi: e.activation(
                out=ptpool[:, 0:D], in_=r1_sb[bi][:], func=AF.Square, scale=D ** -0.5,
                accum_out=stat1[:, 1:2]),
              waits=[(c_sc, c_sc.n)], inc=c_sc)
            E("scalar", lambda e: e.activation(
                out=stat1[:, 3:4], in_=stat1[:, 2:3], func=AF.Square),
              waits=[(c_sc, c_sc.n)], inc=c_sc)
            E("scalar", lambda e: e.activation(
                out=stat1[:, 4:5], in_=stat1[:, 3:4], func=AF.Identity, scale=-1.0,
                bias=stat1[:, 1:2]),
              waits=[(c_sc, c_sc.n)], inc=c_sc)
            E("scalar", lambda e: e.activation(
                out=stat1[:, 5:6], in_=stat1[:, 4:5], func=AF.Ln, bias=cvals[:, 0:1]),
              waits=[(c_sc, c_sc.n)], inc=c_sc)
            E("scalar", lambda e, tt=tt: e.activation(
                out=stat_ln2[:, 2 * tt + 1:2 * tt + 2], in_=stat1[:, 5:6], func=AF.Exp,
                scale=-0.5, bias=cvals[:, 3:4]),
              waits=[(c_sc, c_sc.n)], inc=c_ln2s)
            E("scalar", lambda e, tt=tt: e.activation(
                out=stat_ln2[:, 2 * tt:2 * tt + 1], in_=stat1[:, 2:3], func=AF.Identity,
                scale=stat_ln2[:, 2 * tt + 1:2 * tt + 2]),
              waits=[(c_ln2s, c_ln2s.n)], inc=c_ln2s)
            E("scalar", lambda e, tt=tt, bi=bi: e.dma_start(
                out=r1_s[tt * 128:(tt + 1) * 128, :], in_=r1_sb[bi][:]),
              waits=[(c_ln2s, c_ln2s.n)] +
                    ([(c_r1wp[bi], c_r1wp[bi].n)] if tt >= 2 else []),
              inc=c_r1wp[bi], inc_amt=16)
            r1wp_at[tt] = c_r1wp[bi].n
            E("scalar", lambda e, tt=tt, bi=bi: e.activation(
                out=ln2w[bi][:], in_=r1_sb[bi][:], func=AF.Identity,
                bias=stat_ln2[:, 2 * tt:2 * tt + 1],
                scale=stat_ln2[:, 2 * tt + 1:2 * tt + 2]),
              waits=[(c_ln2s, c_ln2s.n)] + ([(c_trF, trFw_at[tt - 2])] if tt >= 2 else []),
              inc=c_ln2)
            ln2_at[tt] = c_ln2.n
        emit_trW(NT - 1)
        slot_wo = {"A": [(c_r1, r1_at[14]), (c_trF_ev, evict_at[14])],
                   "B": [(c_r1, r1_at[15]), (c_trF_ev, evict_at[15])]}
        r1w_final = [c_r1wp[0].n, c_r1wp[1].n]
        l2w_final = [c_l2w[0].n, c_l2w[1].n]

        close_right()
        open_right()
        f_bf = rsb("f_bf", [128, 16 * 2048], BF16)
        r1f = [rsb(f"r1f{i}", [128, D], F32) for i in range(3)]
        ln2T = [rsb(f"ln2T{i}", [128, D], BF16) for i in range(2)]
        out_sb = [rsb(f"outsb{i}", [128, D], F32) for i in range(2)]

        gate_wo = [(c_woG, c_woG.n), (c_r1, c_r1.n)]

        fld_at = {}
        for kb in range(16):
            pch = c_fldp[kb % 4]
            eng = "gpsimd" if kb % 2 == 0 else "scalar"
            fw = []
            if kb <= 1:
                fw = list(gate_wo) + [(c_fb, c_fb.n)]
            if kb >= 4:
                fw.append((pch, pch.n))
            E(eng, lambda e, kb=kb: e.dma_start(
                out=f_bf[:, kb * 2048:(kb + 1) * 2048],
                in_=f_s[kb * 128:(kb + 1) * 128, :]),
              waits=fw, inc=pch, inc_amt=16)
            fld_at[kb] = (pch, pch.n)
        fld_done = [fld_at[kb] for kb in range(16)]
        outv_at = {}
        ffn_at = {}
        out_at = {}
        for tt in range(NT):
            bi = tt % 2
            ri = tt % 3
            sl = "A" if tt % 2 == 0 else "B"
            ps = psA if sl == "A" else psB
            rw = (list(gate_wo) if tt < 3 else [(c_outv, outv_at[tt - 3])]) + \
                [(c_r1wp[tt % 2], r1wp_at[tt])]
            E("sync", lambda e, tt=tt, ri=ri: e.dma_start(
                out=r1f[ri][:], in_=r1_s[tt * 128:(tt + 1) * 128, :]),
              waits=rw, inc=c_r1r[ri], inc_amt=16)
            r1r_mark = (c_r1r[ri], c_r1r[ri].n)
            E("sync", lambda e, tt=tt, bi=bi, l2w_at=l2w_at: e.dma_start(
                out=ln2T[bi][:], in_=ln2t_s[tt]),
              waits=[(c_l2w[bi], l2w_at[tt])] +
                    ([(c_ffn, ffn_at[tt - 2])] if tt >= 2 else []),
              inc=c_l2r[bi], inc_amt=16)
            l2r_mark = (c_l2r[bi], c_l2r[bi].n)
            ffw = [(c_bias, 2)]
            if tt >= 2:
                ffw.append((c_outv, outv_at[tt - 2]))
            else:
                ffw += slot_wo[sl]
            first = True
            for cci in range(4):
                E("tensor", lambda e, cci=cci, ps=ps: e.matmul(
                    out=ps[:, cci * 512:(cci + 1) * 512], lhsT=ones1[:],
                    rhs=bias_hl[:, cci * 512:(cci + 1) * 512], start=True, stop=False),
                  waits=(ffw if first else []))
                first = False
                E("tensor", lambda e, cci=cci, ps=ps: e.matmul(
                    out=ps[:, cci * 512:(cci + 1) * 512], lhsT=ones1[:],
                    rhs=bias_hl[:, D + cci * 512:D + (cci + 1) * 512],
                    start=False, stop=False))
            ffn_marks = {}
            for kb in range(16):
                for cci in range(4):
                    E("tensor", lambda e, kb=kb, cci=cci, bi=bi, ps=ps: e.matmul(
                        out=ps[:, cci * 512:(cci + 1) * 512],
                        lhsT=ln2T[bi][:, kb * 128:(kb + 1) * 128],
                        rhs=f_bf[:, kb * 2048 + cci * 512:kb * 2048 + (cci + 1) * 512],
                        start=False, stop=(kb == 15)),
                      waits=(([fld_at[kb]] if tt == 0 else []) +
                             ([l2r_mark] if kb == 0 else [])
                             if cci == 0 else []),
                      inc=(c_ffn if kb == 15 else None))
                    if kb == 15:
                        ffn_marks[cci] = c_ffn.n
            ffn_at[tt] = c_ffn.n
            ov0 = [(c_out[bi], out_at[tt - 2])] if tt >= 2 else []
            for cci in range(4):
                E("vector", lambda e, bi=bi, ri=ri, cci=cci, ps=ps: e.tensor_add(
                    out=out_sb[bi][:, cci * 512:(cci + 1) * 512],
                    in0=ps[:, cci * 512:(cci + 1) * 512],
                    in1=r1f[ri][:, cci * 512:(cci + 1) * 512]),
                  waits=[(c_ffn, ffn_marks[cci])] +
                        ([r1r_mark] + ov0 if cci == 0 else []),
                  inc=c_outv, inc_amt=(1 if cci < 3 else 2))
                if cci < 2:
                    outv_at[(tt, cci)] = c_outv.n
            outv_at[tt] = c_outv.n
            E("sync", lambda e, sq=sq, tt=tt, bi=bi: e.dma_start(
                out=x_out[sq, tt * 128:(tt + 1) * 128, :], in_=out_sb[bi][:]),
              waits=[(c_outv, outv_at[tt])], inc=c_out[bi], inc_amt=16)
            out_at[tt] = c_out[bi].n
        slot_marks["A"] = [(c_outv, outv_at[14])]
        slot_marks["B"] = [(c_outv, outv_at[15])]
        gate = [(c_ffn, c_ffn.n), (c_outv, c_outv.n), (c_trF_ev, c_trF_ev.n),
                (c_out[0], c_out[0].n), (c_out[1], c_out[1].n),
                (c_ln2, c_ln2.n), (c_r1r[0], c_r1r[0].n), (c_r1r[1], c_r1r[1].n),
                (c_r1r[2], c_r1r[2].n), (c_l2r[0], c_l2r[0].n),
                (c_l2r[1], c_l2r[1].n)]
        close_right()

    for sq in range(NSEQ):
        _emit_seq(sq)

    p.wait("sync", [(c_out[0], c_out[0].n), (c_out[1], c_out[1].n)])

    with nc.Block() as block:
        @block.tensor
        def _(eng):
            for f in p.ops["tensor"]:
                f(eng)

        @block.vector
        def _(eng):
            for f in p.ops["vector"]:
                f(eng)

        @block.scalar
        def _(eng):
            for f in p.ops["scalar"]:
                f(eng)

        @block.sync
        def _(eng):
            for f in p.ops["sync"]:
                f(eng)

        @block.gpsimd
        def _(eng):
            for f in p.ops["gpsimd"]:
                f(eng)

    ctx.close()
    return nc


def _host_consts():
    ident = np.eye(128, dtype=ml_dtypes.bfloat16)
    dmaskv = np.zeros((128, 128), np.float32)
    dmaskv[np.triu_indices(128, 1)] = NEG
    tmaskv = np.triu(np.ones((128, 128), np.float64)).astype(ml_dtypes.bfloat16)
    inv = 1.0 / (10000.0 ** (np.arange(0, DR, 2, dtype=np.float64) / DR))
    t = np.arange(S, dtype=np.float64)
    fr = np.outer(t, inv)
    emb = np.concatenate([fr, fr], axis=1)
    cosv = np.cos(emb).astype(np.float32)
    sinv = np.sin(emb)
    sinn = sinv.copy()
    sinn[:, 0:DR // 2] = -sinn[:, 0:DR // 2]
    sinn = sinn.astype(np.float32)
    return ident, dmaskv, tmaskv, cosv, sinn


def kernel(L0_attn_hidden_states, L0_attn_Wqa, L0_attn_q_a_ln_w, L0_attn_Wqb,
           L0_attn_Wkva, L0_attn_kv_a_ln_w, L0_attn_Wkvb, L0_attn_Wo,
           L0_ffn_factors, L0_ffn_bias):
    x = np.ascontiguousarray(L0_attn_hidden_states)
    ident, dmaskv, tmaskv, cosv, sinn = _host_consts()
    nc = build_nc()
    common = {
        "wqa": np.ascontiguousarray(L0_attn_Wqa),
        "qln": np.ascontiguousarray(L0_attn_q_a_ln_w),
        "wqb": np.ascontiguousarray(L0_attn_Wqb),
        "wkva": np.ascontiguousarray(L0_attn_Wkva),
        "kvln": np.ascontiguousarray(L0_attn_kv_a_ln_w),
        "wkvb": np.ascontiguousarray(L0_attn_Wkvb),
        "wo": np.ascontiguousarray(L0_attn_Wo),
        "fac": np.ascontiguousarray(L0_ffn_factors),
        "fbias": np.ascontiguousarray(L0_ffn_bias),
        "identc": ident, "identfc": np.eye(128, dtype=np.float32),
        "dmaskc": dmaskv, "tmaskc": tmaskv,
        "cosc": cosv, "sinnc": sinn,
    }
    in_maps = []
    for c in range(8):
        m = dict(common)
        m["x"] = np.ascontiguousarray(x[2 * c:2 * c + 2])
        in_maps.append(m)
    res = run_bass_kernel_spmd(nc, in_maps, core_ids=list(range(8)))
    return np.concatenate([r["out"] for r in res.results], axis=0)


if __name__ == "__main__":
    nc = build_nc()
    print("built ok; instructions:", nc.n_instructions())


# version 21
# speedup vs baseline: 1.0723x; 1.0723x over previous
import math
import numpy as np
import ml_dtypes
from contextlib import ExitStack

import concourse.bass as bass
import concourse.mybir as mybir
from concourse.bass_utils import run_bass_kernel_spmd

F32 = mybir.dt.float32
BF16 = mybir.dt.bfloat16
AF = mybir.ActivationFunctionType
ALU = mybir.AluOpType
AX = mybir.AxisListType

S = 2048
D = 2048
NT = 16
NH = 16
QL = 1536
KVL = 512
DN = 128
DR = 64
DV = 128
NSEQ = 2
SCALE = (DN + DR) ** -0.5
NEG = -1.0e30
EPS_RMS = 1e-6
EPS_LN = 1e-5
INV_SQRT_RANK = 1.0 / math.sqrt(8.0)


def _chunks(qmin, qmax, bank=512):
    out = []
    q = qmin
    while q < qmax:
        w = min(bank - (q % bank), qmax - q)
        out.append((q, w))
        q += w
    return out


class Ch:
    def __init__(self, nc, name):
        self.h = nc.alloc_semaphore(name=name)
        self.n = 0
        self.name = name

    @property
    def mark(self):
        return (self, self.n)


class Prog:
    ENGINES = ("tensor", "vector", "scalar", "sync", "gpsimd")

    def __init__(self, nc):
        self.nc = nc
        self.ops = {e: [] for e in self.ENGINES}

    def emit(self, eng, fn, waits=(), inc=None, inc_amt=1):
        wl = [(c.h if isinstance(c, Ch) else c, v) for c, v in waits]
        if inc is not None:
            inc.n += inc_amt
            ih, amt = inc.h, inc_amt
        else:
            ih, amt = None, 0

        def run(engine):
            for h, v in wl:
                engine.wait_ge(h, v)
            ins = fn(engine)
            if ih is not None:
                ins.then_inc(ih, amt)
        self.ops[eng].append(run)

    def wait(self, eng, waits):
        wl = [(c.h if isinstance(c, Ch) else c, v) for c, v in waits]

        def run(engine):
            for h, v in wl:
                engine.wait_ge(h, v)
        self.ops[eng].append(run)


def build_nc():
    nc = bass.Bass("TRN2", target_bir_lowering=False,
                   dynamic_dma_scratch_size=16384)

    x_in = nc.dram_tensor("x", [NSEQ, S, D], F32, kind="ExternalInput")
    wqa = nc.dram_tensor("wqa", [D, QL], F32, kind="ExternalInput")
    qln = nc.dram_tensor("qln", [QL], F32, kind="ExternalInput")
    wqb = nc.dram_tensor("wqb", [QL, NH * (DN + DR)], F32, kind="ExternalInput")
    wkva = nc.dram_tensor("wkva", [D, KVL + DR], F32, kind="ExternalInput")
    kvln = nc.dram_tensor("kvln", [KVL], F32, kind="ExternalInput")
    wkvb = nc.dram_tensor("wkvb", [KVL, NH * (DN + DV)], F32, kind="ExternalInput")
    wo = nc.dram_tensor("wo", [NH * DV, D], F32, kind="ExternalInput")
    fac = nc.dram_tensor("fac", [8, D, D], F32, kind="ExternalInput")
    fbias = nc.dram_tensor("fbias", [D], F32, kind="ExternalInput")
    identc = nc.dram_tensor("identc", [128, 128], BF16, kind="ExternalInput")
    identfc = nc.dram_tensor("identfc", [128, 128], F32, kind="ExternalInput")
    dmaskc = nc.dram_tensor("dmaskc", [128, 128], F32, kind="ExternalInput")
    tmaskc = nc.dram_tensor("tmaskc", [128, 128], BF16, kind="ExternalInput")
    cosc = nc.dram_tensor("cosc", [S, DR], F32, kind="ExternalInput")
    sinnc = nc.dram_tensor("sinnc", [S, DR], F32, kind="ExternalInput")

    x_out = nc.dram_tensor("out", [NSEQ, S, D], F32, kind="ExternalOutput")

    qpe_s = nc.dram_tensor("qpe_s", [S, NH * DR], BF16, kind="Internal")
    r1_s = nc.dram_tensor("r1_s", [S, D], F32, kind="Internal")
    f_s = nc.dram_tensor("f_s", [D, D], BF16, kind="Internal")
    ot_s = nc.dram_tensor("ot_s", [NH, 128, S], BF16, kind="Internal")
    ln2t_s = nc.dram_tensor("ln2t_s", [NT, 128, D], BF16, kind="Internal")

    ctx = ExitStack()

    def sb(name, shape, dt, side="left"):
        return ctx.enter_context(nc.sbuf_tensor(name, shape, dt, side=side))

    ident = sb("ident", [128, 128], BF16)
    identf = sb("identf", [128, 128], F32)
    diag = sb("diag", [128, 128], F32)
    maskT = sb("maskT", [128, 128], BF16)
    ones1 = sb("ones1", [1, 128], BF16)
    halfs1 = sb("halfs1", [1, 128], BF16)
    ones_col = sb("ones_col", [128, 1], BF16)
    bias_hl = sb("bias_hl", [1, 2 * D], BF16)
    qlnw = sb("qlnw", [128, 12], F32)
    kvlnw = sb("kvlnw", [128, 4], F32)
    ptpool = sb("ptpool", [128, 2 * S], BF16)
    acc_f = sb("acc_f", [128, D], F32)
    stat1 = sb("stat1", [128, 16], F32)
    stat_h = sb("stat_h", [128, 32], BF16)
    sb_st = sb("sb_st", [32, 128], BF16)
    stat_ln2 = sb("stat_ln2", [128, 2 * NT], F32)
    cvals = sb("cvals", [128, 4], F32)

    psA = ctx.enter_context(nc.psum_tensor("psA", [128, 2048], F32))
    psB = ctx.enter_context(nc.psum_tensor("psB", [128, 2048], F32))

    p = Prog(nc)
    E = p.emit
    ch = lambda name: Ch(nc, name)

    c_const = ch("const")
    c_fb = ch("fb")
    c_bias = ch("bias")
    c_x = [ch("xload0"), ch("xload1")]
    c_lnv = ch("lnv")
    c_ln1 = ch("ln1")
    c_trL = ch("trL"); c_trL_ev = ch("trL_ev")
    c_wa = ch("wa")
    c_ckv = ch("ckv"); c_ckv_ev = ch("ckv_ev"); c_rskv = ch("rskv")
    c_qa = ch("qa"); c_qa_ev = ch("qa_ev"); c_rsqa = ch("rsqa")
    c_trQ = ch("trQ"); c_trQ_ev = ch("trQ_ev")
    c_trC = ch("trC"); c_trC_ev = ch("trC_ev")
    c_wpe = ch("wpe"); c_wpef = ch("wpef")
    c_bpe = ch("bpe"); c_bpe_ev = ch("bpe_ev")
    c_qpew = [ch("qpew0"), ch("qpew1")]
    c_tkpe = ch("tkpe"); c_tkpe_ev = ch("tkpe_ev")
    c_wh = ch("wh"); c_whf = ch("whf"); c_qpeh = ch("qpeh")
    c_bn = ch("bn"); c_bn_ev = ch("bn_ev")
    c_c1 = ch("c1"); c_c1_ev = ch("c1_ev")
    c_c2 = ch("c2"); c_c2_ev = ch("c2_ev")
    c_tq = ch("tq"); c_tq_ev = ch("tq_ev")
    c_s = ch("s"); c_smax = ch("smax"); c_sdone = ch("sdone"); c_sln = ch("sln")
    c_stat = ch("stat"); c_statT = ch("statT"); c_statT_ev = ch("statT_ev")
    c_aug = ch("aug")
    c_st = ch("st"); c_pt = ch("pt"); c_ptm = ch("ptm"); c_av = ch("av")
    c_ot_ev = ch("ot_ev"); c_otw = ch("otw"); c_otld = ch("otld")
    c_wo = ch("wo"); c_xre = [ch("xre0"), ch("xre1")]; c_woG = ch("woG")
    c_r1 = ch("r1"); c_ln2s = ch("ln2s")
    c_r1wp = [ch("r1wp0"), ch("r1wp1")]
    c_fld = ch("fld"); c_r1r = [ch("r1r0"), ch("r1r1"), ch("r1r2")]; c_ln2 = ch("ln2")
    c_trF = ch("trF"); c_trF_ev = ch("trF_ev")
    c_ffn = ch("ffn"); c_outv = ch("outv")
    c_out = [ch("out0"), ch("out1")]
    c_cos = ch("cos"); c_bld = ch("bld")
    c_v = ch("vroll"); c_sc = ch("scroll")
    c_mq = ch("mq"); c_mt = ch("mt"); c_mv = ch("mv"); c_sg = ch("sg")
    c_l2w = [ch("l2w0"), ch("l2w1")]
    c_fldp = [ch(f"fldp{i}") for i in range(4)]
    c_l2r = [ch("l2r0"), ch("l2r1")]
    c_flo = ch("flo")

    for fn in [
        lambda e: e.dma_start(out=ident[:], in_=identc[:]),
        lambda e: e.dma_start(out=identf[:], in_=identfc[:]),
        lambda e: e.dma_start(out=diag[:], in_=dmaskc[:]),
        lambda e: e.dma_start(out=maskT[:], in_=tmaskc[:]),
        lambda e: e.dma_start(out=qlnw[:], in_=qln[:].rearrange("(a p) -> p a", p=128),
                              allow_slow_non_contiguous=True),
        lambda e: e.dma_start(out=kvlnw[:], in_=kvln[:].rearrange("(a p) -> p a", p=128),
                              allow_slow_non_contiguous=True),
    ]:
        E("sync", fn, inc=c_const, inc_amt=16)
    E("vector", lambda e: e.memset(ones1[:], 1.0))
    E("vector", lambda e: e.memset(halfs1[:], 0.5))
    E("vector", lambda e: e.memset(ones_col[:], 1.0))
    c_cc = ch("cc")
    E("vector", lambda e: e.memset(cvals[:, 0:1], EPS_LN))
    E("vector", lambda e: e.memset(cvals[:, 1:2], EPS_RMS))
    E("vector", lambda e: e.memset(cvals[:, 2:3], math.log(SCALE)))
    E("vector", lambda e: e.memset(cvals[:, 3:4], math.log(INV_SQRT_RANK)),
      inc=c_cc)
    p.wait("scalar", [(c_cc, 1), (c_const, c_const.n)])
    p.wait("tensor", [(c_const, c_const.n)])
    p.wait("vector", [(c_const, c_const.n)])

    E("sync", lambda e: e.dma_start(out=acc_f[0:1, 0:D], in_=fbias[:].unsqueeze(0)),
      inc=c_bld, inc_amt=16)
    E("vector", lambda e: e.tensor_copy(out=bias_hl[:, 0:D], in_=acc_f[0:1, 0:D]),
      waits=[(c_bld, 16)], inc=c_bias)
    E("vector", lambda e: e.tensor_sub(out=bias_hl[:, D:2 * D], in0=acc_f[0:1, 0:D],
                                       in1=bias_hl[:, 0:D]),
      waits=[(c_bias, 1)], inc=c_bias)

    def emit_f_build():
        p.wait("gpsimd", [(c_pt, 40)])
        for rb in range(16):
            E("gpsimd", lambda e, rb=rb: e.dma_start(
                out=acc_f[:], in_=fac[0, rb * 128:(rb + 1) * 128, :]),
              waits=[(c_bias, 2)] if rb == 0 else [(c_fb, c_fb.n)],
              inc=c_fb, inc_amt=16)
            for r in range(1, 8):
                E("gpsimd", lambda e, r=r, rb=rb: e.dma_start(
                    out=acc_f[:], in_=fac[r, rb * 128:(rb + 1) * 128, :],
                    accum_op=ALU.add),
                  waits=[(c_fb, c_fb.n)], inc=c_fb, inc_amt=16)
            E("gpsimd", lambda e, rb=rb: e.dma_start(
                out=f_s[rb * 128:(rb + 1) * 128, :], in_=acc_f[:]),
              waits=[(c_fb, c_fb.n)], inc=c_fb, inc_amt=16)

    slot_marks = {"A": [], "B": []}

    rctx = [None]

    def open_right():
        rctx[0] = ExitStack()

    _uid = [0]

    def rsb(name, shape, dt):
        _uid[0] += 1
        return rctx[0].enter_context(
            nc.sbuf_tensor(f"{name}_u{_uid[0]}", shape, dt, side="right"))

    def close_right():
        rctx[0].close()
        rctx[0] = None

    gate = []

    def _emit_seq(sq):
        nonlocal gate
        open_right()
        xs = [rsb(f"xs{i}", [128, D], F32) for i in range(2)]
        ln1_bf = rsb("ln1_bf", [128, D], BF16)
        _l1t = rsb("ln1T", [128, D], BF16)
        ln1T = [_l1t, _l1t]
        qa_n = rsb("qa_n", [128, QL], BF16)
        ckv_n = rsb("ckv_n", [128, KVL], BF16)
        rope1 = rsb("rope1", [128, 1024], BF16)
        rope2 = rsb("rope2", [128, 1024], BF16)
        cos_t = rsb("cos_t", [128, NT * DR], F32)
        sinn_t = rsb("sinn_t", [128, NT * DR], F32)
        kpe_tok = rsb("kpe_tok", [128, NT * DR], BF16)
        g2ctx = ExitStack()
        qa_nT = g2ctx.enter_context(nc.sbuf_tensor(f"qa_nT_{sq}", [128, 12 * S], BF16, side="left"))
        ckv_nT = g2ctx.enter_context(nc.sbuf_tensor(f"ckv_nT_{sq}", [128, 4 * S], BF16, side="left"))
        kpeT_aug = g2ctx.enter_context(nc.sbuf_tensor(f"kpeT_aug_{sq}", [66, S], BF16, side="left"))
        E("vector", lambda e: e.memset(kpeT_aug[64:66, :], 1.0),
          waits=(list(gate) if sq else []))
        wa_ctx = ExitStack()
        wa_bf = wa_ctx.enter_context(nc.sbuf_tensor(f"wa_bf_{sq}", [128, 16 * 2112], BF16, side="right"))

        qa_v = lambda: qa_nT[:].rearrange("p (a t) -> p a t", t=S)
        ckv_v = lambda: ckv_nT[:].rearrange("p (a t) -> p a t", t=S)

        cosv = cosc[:].rearrange("(a p) j -> p a j", p=128)
        sinv = sinnc[:].rearrange("(a p) j -> p a j", p=128)
        E("sync", lambda e: e.dma_start(
            out=cos_t[:].rearrange("p (a j) -> p a j", j=DR), in_=cosv),
          waits=list(gate), inc=c_cos, inc_amt=16)
        E("sync", lambda e: e.dma_start(
            out=sinn_t[:].rearrange("p (a j) -> p a j", j=DR), in_=sinv),
          inc=c_cos, inc_amt=16)
        cos_ready = c_cos.n

        for kb in range(16):
            E("gpsimd", lambda e, kb=kb: e.dma_start(
                out=wa_bf[:, kb * 2112:kb * 2112 + 1536],
                in_=wqa[kb * 128:(kb + 1) * 128, :]),
              waits=(list(gate) if kb == 0 else []), inc=c_wa, inc_amt=16)
            E("gpsimd", lambda e, kb=kb: e.dma_start(
                out=wa_bf[:, kb * 2112 + 1536:kb * 2112 + 2112],
                in_=wkva[kb * 128:(kb + 1) * 128, :]),
              inc=c_wa, inc_amt=16)
        wa_done = c_wa.n

        ln1_base = c_ln1.n
        for tt in range(NT):
            ti = tt % 2
            xw = list(gate) if tt == 0 else ([] if tt == 1 else [(c_ln1, ln1_base + tt - 1)])
            E("sync", lambda e, sq=sq, tt=tt, ti=ti: e.dma_start(
                out=xs[ti][:], in_=x_in[sq, tt * 128:(tt + 1) * 128, :]),
              waits=xw, inc=c_x[ti], inc_amt=16)
            E("scalar", lambda e, ti=ti: e.activation(
                out=ptpool[:, 0:D], in_=xs[ti][:], func=AF.Copy, scale=-1.0 / D,
                accum_out=stat1[:, 2:3]),
              waits=[(c_x[ti], c_x[ti].n)] + ([(c_av, c_av.n)] if (sq > 0 and tt == 0) else []),
              inc=c_sc)
            E("scalar", lambda e, ti=ti: e.activation(
                out=ptpool[:, 0:D], in_=xs[ti][:], func=AF.Square, scale=D ** -0.5,
                accum_out=stat1[:, 1:2]),
              waits=[(c_sc, c_sc.n)], inc=c_sc)
            E("scalar", lambda e: e.activation(
                out=stat1[:, 3:4], in_=stat1[:, 2:3], func=AF.Square),
              waits=[(c_sc, c_sc.n)], inc=c_sc)
            E("scalar", lambda e: e.activation(
                out=stat1[:, 4:5], in_=stat1[:, 3:4], func=AF.Identity, scale=-1.0,
                bias=stat1[:, 1:2]),
              waits=[(c_sc, c_sc.n)], inc=c_sc)
            E("scalar", lambda e: e.activation(
                out=stat1[:, 5:6], in_=stat1[:, 4:5], func=AF.Ln, bias=cvals[:, 0:1]),
              waits=[(c_sc, c_sc.n)], inc=c_sc)
            E("scalar", lambda e: e.activation(
                out=stat1[:, 5:6], in_=stat1[:, 5:6], func=AF.Exp, scale=-0.5),
              waits=[(c_sc, c_sc.n)], inc=c_sc)
            E("scalar", lambda e: e.activation(
                out=stat1[:, 6:7], in_=stat1[:, 2:3], func=AF.Identity,
                scale=stat1[:, 5:6]),
              waits=[(c_sc, c_sc.n)], inc=c_sc)
            lw = [(c_sc, c_sc.n)] + \
                ([(c_trL, c_trL.n)] if tt > 0 or sq > 0 else [])
            E("scalar", lambda e, ti=ti: e.activation(
                out=ln1_bf[:], in_=xs[ti][:], func=AF.Identity,
                bias=stat1[:, 6:7], scale=stat1[:, 5:6]),
              waits=lw, inc=c_ln1)

            for g in range(4):
                gs = 512 * (g % 2)
                tw = [(c_ln1, c_ln1.n), (c_trQ_ev, c_trQ_ev.n)]
                if sq == 0 and tt == 0 and g < 2:
                    pass
                else:
                    tw.append((c_trL_ev, c_trL_ev.n - 1))
                if sq > 0 and tt == 0 and g < 2:
                    tw += slot_marks["B"]
                for j in range(4):
                    blk = g * 4 + j
                    E("tensor", lambda e, blk=blk, gs=gs, j=j: e.matmul(
                        out=psB[:, gs + j * 128:gs + (j + 1) * 128],
                        lhsT=ln1_bf[:, blk * 128:(blk + 1) * 128], rhs=ident[:],
                        start=(j == 0), stop=(j == 3)),
                      waits=(tw if j == 0 else []),
                      inc=(c_trL if j == 3 else None))
                E("vector", lambda e, ti=ti, g=g, gs=gs: e.tensor_copy(
                    out=ln1T[ti][:, g * 512:(g + 1) * 512], in_=psB[:, gs:gs + 512]),
                  waits=[(c_trL, c_trL.n), (c_qa, c_qa.n), (c_ckv, c_ckv.n)],
                  inc=c_trL_ev)

            cw = [(c_trL_ev, c_trL_ev.n), (c_wa, wa_done)]
            if sq == 0 and tt == 0:
                pass
            elif tt == 0:
                cw += slot_marks["B"]
            else:
                cw.append((c_ckv_ev, c_ckv_ev.n))
            qw = [(c_trL_ev, c_trL_ev.n)]
            if sq == 0 and tt == 0:
                pass
            elif tt == 0:
                qw += slot_marks["A"]
            else:
                qw.append((c_qa_ev, c_qa_ev.n))
            for kb in range(16):
                for cci in range(5):
                    if cci < 3:
                        dst, do, wdt, woff = psA, cci * 512, 512, cci * 512
                    elif cci == 3:
                        dst, do, wdt, woff = psB, 1024, 512, 1536
                    else:
                        dst, do, wdt, woff = psB, 1536, 64, 2048
                    E("tensor", lambda e, ti=ti, kb=kb, dst=dst, do=do, wdt=wdt, woff=woff: e.matmul(
                        out=dst[:, do:do + wdt],
                        lhsT=ln1T[ti][:, kb * 128:(kb + 1) * 128],
                        rhs=wa_bf[:, kb * 2112 + woff:kb * 2112 + woff + wdt],
                        start=(kb == 0), stop=(kb == 15)),
                      waits=((qw + cw) if (cci == 0 and kb == 0) else []),
                      inc=((c_qa if cci == 2 else (c_ckv if cci == 4 else None))
                           if kb == 15 else None))
            E("scalar", lambda e: e.activation(
                out=ptpool[:, 0:512], in_=psB[:, 1024:1536], func=AF.Square,
                accum_out=stat1[:, 7:8]),
              waits=[(c_ckv, c_ckv.n)], inc=c_sc)
            E("scalar", lambda e: e.activation(
                out=stat1[:, 8:9], in_=stat1[:, 7:8], func=AF.Ln, bias=cvals[:, 1:2],
                scale=1.0 / KVL),
              waits=[(c_sc, c_sc.n)], inc=c_sc)
            E("scalar", lambda e: e.activation(
                out=stat1[:, 8:9], in_=stat1[:, 8:9], func=AF.Exp, scale=-0.5),
              waits=[(c_sc, c_sc.n)], inc=c_rskv)
            ckw = [(c_rskv, c_rskv.n)] + ([(c_trC, c_trC.n)] if tt > 0 or sq > 0 else [])
            E("vector", lambda e: e.tensor_scalar(
                out=ckv_n[:], in0=psB[:, 1024:1536], scalar1=stat1[:, 8:9], scalar2=None,
                op0=ALU.mult), waits=ckw, inc=c_ckv_ev)
            E("vector", lambda e, tt=tt: e.tensor_tensor(
                out=rope1[:, 0:DR], in0=psB[:, 1536:1600],
                in1=cos_t[:, tt * DR:(tt + 1) * DR], op=ALU.mult),
              waits=[(c_cos, cos_ready)], inc=c_v)
            E("vector", lambda e, tt=tt: e.tensor_tensor(
                out=rope2[:, 0:32], in0=psB[:, 1568:1600],
                in1=sinn_t[:, tt * DR:tt * DR + 32], op=ALU.mult), inc=c_v)
            E("vector", lambda e, tt=tt: e.tensor_tensor(
                out=rope2[:, 32:DR], in0=psB[:, 1536:1568],
                in1=sinn_t[:, tt * DR + 32:(tt + 1) * DR], op=ALU.mult), inc=c_v)
            E("vector", lambda e, tt=tt: e.tensor_add(
                out=kpe_tok[:, tt * DR:(tt + 1) * DR], in0=rope1[:, 0:DR],
                in1=rope2[:, 0:DR]),
              waits=[(c_v, c_v.n)], inc=c_ckv_ev)

            pass
            pass
            E("scalar", lambda e: e.activation(
                out=ptpool[:, 0:1536], in_=psA[:, 0:1536], func=AF.Square,
                accum_out=stat1[:, 9:10]),
              waits=[(c_qa, c_qa.n)], inc=c_sc)
            E("scalar", lambda e: e.activation(
                out=stat1[:, 10:11], in_=stat1[:, 9:10], func=AF.Ln, bias=cvals[:, 1:2],
                scale=1.0 / QL),
              waits=[(c_sc, c_sc.n)], inc=c_sc)
            E("scalar", lambda e: e.activation(
                out=stat1[:, 10:11], in_=stat1[:, 10:11], func=AF.Exp, scale=-0.5,
                bias=cvals[:, 2:3]),
              waits=[(c_sc, c_sc.n)], inc=c_rsqa)
            qnw = [(c_rsqa, c_rsqa.n)] + ([(c_trQ, c_trQ.n)] if tt > 0 or sq > 0 else [])
            E("vector", lambda e: e.tensor_scalar(
                out=qa_n[:], in0=psA[:, 0:1536], scalar1=stat1[:, 10:11], scalar2=None,
                op0=ALU.mult), waits=qnw, inc=c_qa_ev)

            tcw = [(c_ckv_ev, c_ckv_ev.n - 1), (c_trL_ev, c_trL_ev.n),
                   (c_trQ_ev, c_trQ_ev.n)]
            if tt > 0 or sq > 0:
                tcw.append((c_trC_ev, c_trC_ev.n))
            for j in range(4):
                E("tensor", lambda e, j=j: e.matmul(
                    out=psB[:, j * 128:(j + 1) * 128],
                    lhsT=ckv_n[:, j * 128:(j + 1) * 128], rhs=ident[:],
                    start=(j == 0), stop=(j == 3)),
                  waits=(tcw if j == 0 else []),
                  inc=(c_trC if j == 3 else None))
            E("vector", lambda e, tt=tt: e.tensor_copy(
                out=ckv_v()[:, :, tt * 128:(tt + 1) * 128],
                in_=psB[:, 0:512].rearrange("p (a t) -> p a t", t=128)),
              waits=[(c_trC, c_trC.n)], inc=c_trC_ev)

            for g in range(3):
                gs = 512 if g % 2 == 0 else 0
                tqw = [(c_qa_ev, c_qa_ev.n)]
                tqw.append((c_trC_ev, c_trC_ev.n) if gs == 0 else (c_trL_ev, c_trL_ev.n))
                if not (sq == 0 and tt == 0 and g == 0):
                    tqw.append((c_trQ_ev, c_trQ_ev.n - 1))
                for j in range(4):
                    blk = g * 4 + j
                    E("tensor", lambda e, blk=blk, gs=gs, j=j: e.matmul(
                        out=psB[:, gs + j * 128:gs + (j + 1) * 128],
                        lhsT=qa_n[:, blk * 128:(blk + 1) * 128], rhs=ident[:],
                        start=(j == 0), stop=(j == 3)),
                      waits=(tqw if j == 0 else []),
                      inc=(c_trQ if j == 3 else None))
                E("vector", lambda e, g=g, tt=tt, gs=gs: e.tensor_copy(
                    out=qa_v()[:, g * 4:(g + 1) * 4, tt * 128:(tt + 1) * 128],
                    in_=psB[:, gs:gs + 512].rearrange("p (a t) -> p a t", t=128)),
                  waits=[(c_trQ, c_trQ.n)], inc=c_trQ_ev)

        slot_marks["A"] = [(c_qa_ev, c_qa_ev.n)]
        slot_marks["B"] = [(c_trQ_ev, c_trQ_ev.n), (c_trC_ev, c_trC_ev.n),
                           (c_ckv_ev, c_ckv_ev.n)]

        wa_ctx.close()
        bctx = ExitStack()
        wqbpe_bf = bctx.enter_context(nc.sbuf_tensor(f"wqbpe_bf_{sq}", [128, 12 * 1024], BF16, side="right"))
        qpe_stage = [bctx.enter_context(nc.sbuf_tensor(f"qpes{i}_{sq}", [128, NH * DR], BF16, side="right"))
                     for i in range(2)]
        pe_cols = lambda kb: wqb[kb * 128:(kb + 1) * 128, :].rearrange(
            "p (h c) -> p h c", c=DN + DR)[:, :, DN:DN + DR]
        for kb in range(12):
            E("gpsimd", lambda e, kb=kb: e.dma_start(
                out=rope1[:].rearrange("p (h j) -> p h j", j=DR), in_=pe_cols(kb)),
              waits=([(c_qa, c_qa.n), (c_ckv, c_ckv.n), (c_ckv_ev, c_ckv_ev.n),
                      (c_trQ_ev, c_trQ_ev.n)]
                     if kb == 0 else [(c_wpef, c_wpef.n)]),
              inc=c_wpe, inc_amt=16)
            E("scalar", lambda e, kb=kb: e.mul(
                out=wqbpe_bf[:, kb * 1024:(kb + 1) * 1024], in_=rope1[:],
                mul=qlnw[:, kb:kb + 1]),
              waits=[(c_wpe, c_wpe.n)], inc=c_wpef)
        if sq == 0:
            emit_f_build()
        bpe_ev_at = {}
        for tt in range(NT):
            base = 1024 * (tt % 2)
            bw = [(c_wpef, c_wpef.n)]
            if tt < 2:
                bw += slot_marks["A"]
            else:
                bw.append((c_bpe_ev, bpe_ev_at[tt - 2]))
            for kb in range(12):
                for cci in range(2):
                    E("tensor", lambda e, tt=tt, kb=kb, cci=cci, base=base: e.matmul(
                        out=psA[:, base + cci * 512:base + (cci + 1) * 512],
                        lhsT=qa_v()[:, kb, tt * 128:(tt + 1) * 128],
                        rhs=wqbpe_bf[:, kb * 1024 + cci * 512:kb * 1024 + (cci + 1) * 512],
                        start=(kb == 0), stop=(kb == 11)),
                      waits=(bw if (cci == 0 and kb == 0) else []),
                      inc=(c_bpe if (kb == 11 and cci == 1) else None))
            pe = psA[:, base:base + 1024].rearrange("p (h j) -> p h j", j=DR)
            cos_b = cos_t[:, tt * DR:(tt + 1) * DR].unsqueeze(1).broadcast_to([128, NH, DR])
            sin_lo = sinn_t[:, tt * DR:tt * DR + 32].unsqueeze(1).broadcast_to([128, NH, 32])
            sin_hi = sinn_t[:, tt * DR + 32:(tt + 1) * DR].unsqueeze(1).broadcast_to([128, NH, 32])
            rw = [(c_bpe, c_bpe.n), (c_bpe_ev, c_bpe_ev.n)]
            if tt >= 2:
                rw.append((c_qpew[tt % 2], c_qpew[tt % 2].n))
            E("vector", lambda e, pe=pe, cos_b=cos_b: e.tensor_tensor(
                out=rope1[:].rearrange("p (h j) -> p h j", j=DR), in0=pe, in1=cos_b,
                op=ALU.mult), waits=rw, inc=c_v)
            E("vector", lambda e, pe=pe, sin_lo=sin_lo: e.tensor_tensor(
                out=rope2[:].rearrange("p (h j) -> p h j", j=DR)[:, :, 0:32],
                in0=pe[:, :, 32:DR], in1=sin_lo, op=ALU.mult), inc=c_v)
            E("vector", lambda e, pe=pe, sin_hi=sin_hi: e.tensor_tensor(
                out=rope2[:].rearrange("p (h j) -> p h j", j=DR)[:, :, 32:DR],
                in0=pe[:, :, 0:32], in1=sin_hi, op=ALU.mult), inc=c_v)
            E("vector", lambda e, tt=tt: e.tensor_add(
                out=qpe_stage[tt % 2][:], in0=rope1[:], in1=rope2[:]),
              waits=[(c_v, c_v.n)], inc=c_bpe_ev)
            bpe_ev_at[tt] = c_bpe_ev.n
            E("scalar", lambda e, tt=tt: e.dma_start(
                out=qpe_s[tt * 128:(tt + 1) * 128, :], in_=qpe_stage[tt % 2][:]),
              waits=[(c_bpe_ev, c_bpe_ev.n)], inc=c_qpew[tt % 2], inc_amt=16)
        slot_marks["A"] = [(c_bpe_ev, c_bpe_ev.n)]

        for g in range(4):
            gs = 512 * (g % 2)
            kw = [(c_ckv_ev, c_ckv_ev.n)]
            if g < 2:
                kw += slot_marks["B"]
            else:
                kw.append((c_tkpe_ev, c_tkpe_ev.n - 1))
            for j in range(4):
                tt2 = g * 4 + j
                E("tensor", lambda e, tt2=tt2, gs=gs, j=j: e.matmul(
                    out=psB[0:64, gs + j * 128:gs + (j + 1) * 128],
                    lhsT=kpe_tok[:, tt2 * DR:(tt2 + 1) * DR], rhs=ident[:],
                    start=(j == 0), stop=(j == 3)),
                  waits=(kw if j == 0 else []),
                  inc=(c_tkpe if j == 3 else None))
            E("vector", lambda e, g=g, gs=gs: e.tensor_copy(
                out=kpeT_aug[0:64, g * 512:(g + 1) * 512], in_=psB[0:64, gs:gs + 512]),
              waits=[(c_tkpe, c_tkpe.n)], inc=c_tkpe_ev)
        slot_marks["B"] = [(c_tkpe_ev, c_tkpe_ev.n)]
        kpesq_pending = [True]

        bctx.close()
        close_right()
        open_right()
        ot_stage = [rsb(f"otst{i}", [128, S], BF16) for i in range(2)]
        stage_w = rsb("stage_w", [128, 1536], F32)
        sq_all = rsb("sq_all", [128, S], BF16)
        sq_pe = rsb("sq_pe", [64, S], BF16)
        kpesq = rsb("kpesq", [64, S], BF16)
        neg_mh = rsb("neg_mh", [128, 32], F32)
        stat_sa = rsb("stat_sa", [128, 32], F32)
        stat_sb = rsb("stat_sb", [128, 32], F32)
        kmax = rsb("kmax", [128, 1], F32)
        kmax_bf = rsb("kmax_bf", [128, 1], BF16)
        kpemax_bf = rsb("kpemax_bf", [64, 1], BF16)
        lnmk = rsb("lnmk", [1, 1], BF16)
        lnmk_col = rsb("lnmk_col", [128, 1], F32)
        invcol = rsb("invcol", [128, 16], F32)
        invT = rsb("invT", [16, 128], F32)
        invb = rsb("invb", [128, D], F32)
        invThi = rsb("invThi", [16, 128], BF16)
        invTlo = rsb("invTlo", [16, 128], BF16)
        invflat = rsb("invflat", [1, 4096], BF16)
        wqbn_h = [rsb(f"wqbnh{i}", [128, 12 * 128], BF16) for i in range(2)]
        wkvbk_h = [rsb(f"wkvbkh{i}", [128, 4 * 128], BF16) for i in range(2)]
        wkvbv_h = [rsb(f"wkvbvh{i}", [128, 4 * 128], BF16) for i in range(2)]
        QnT = [rsb(f"QnT{i}", [128, S], BF16) for i in range(2)]
        QpeT_aug = [rsb(f"QpeTa{i}", [66, S], BF16) for i in range(2)]
        KnT = [rsb(f"KnT{i}", [128, S], BF16) for i in range(2)]
        Vh = [rsb(f"Vh{i}", [128, S], BF16) for i in range(2)]
        qpe_sl = [rsb(f"qpesl{i}", [128, NH * DR], BF16) for i in range(2)]

        gate_A = [(c_bpe, c_bpe.n), (c_bpe_ev, c_bpe_ev.n), (c_tkpe, c_tkpe.n),
                  (c_wpef, c_wpef.n), (c_qpew[0], c_qpew[0].n), (c_qpew[1], c_qpew[1].n), (c_tkpe_ev, c_tkpe_ev.n)]

        def head_stage(h, stage, first=False):
            hb = h % 2
            if stage == 0:
                hw = list(gate_A) if first else [(c_whf, c_whf.n)]
                E("sync", lambda e, h=h: e.dma_start(
                    out=stage_w[:].rearrange("p (a c) -> p a c", c=128),
                    in_=wqb[:, h * 192:h * 192 + 128].rearrange("(a p) c -> p a c", p=128)),
                  waits=hw, inc=c_wh, inc_amt=16)
            elif stage in (1, 2, 3):
                for kb in range((stage - 1) * 4, stage * 4):
                    fw = []
                    if kb == 0:
                        fw = [(c_wh, c_wh.n), (c_bn, c_bn.n)]
                    E("scalar", lambda e, kb=kb, hb=hb: e.mul(
                        out=wqbn_h[hb][:, kb * 128:(kb + 1) * 128],
                        in_=stage_w[:, kb * 128:(kb + 1) * 128], mul=qlnw[:, kb:kb + 1]),
                      waits=fw, inc=(c_whf if kb == 11 else None))
                if stage == 3:
                    E("sync", lambda e, h=h: e.dma_start(
                        out=stage_w[:, 0:512].rearrange("p (a c) -> p a c", c=128),
                        in_=wkvb[:, h * 256:h * 256 + 128].rearrange("(a p) c -> p a c", p=128)),
                      waits=[(c_whf, c_whf.n)], inc=c_wh, inc_amt=16)
            elif stage == 4:
                for kb in range(4):
                    fw = []
                    if kb == 0:
                        fw = [(c_wh, c_wh.n), (c_c1, c_c1.n)]
                    E("scalar", lambda e, kb=kb, hb=hb: e.mul(
                        out=wkvbk_h[hb][:, kb * 128:(kb + 1) * 128],
                        in_=stage_w[:, kb * 128:(kb + 1) * 128], mul=kvlnw[:, kb:kb + 1]),
                      waits=fw, inc=(c_whf if kb == 3 else None))
                E("sync", lambda e, h=h: e.dma_start(
                    out=stage_w[:, 0:512].rearrange("p (a c) -> p a c", c=128),
                    in_=wkvb[:, h * 256 + 128:h * 256 + 256].rearrange("(a p) c -> p a c", p=128)),
                  waits=[(c_whf, c_whf.n)], inc=c_wh, inc_amt=16)
            elif stage == 5:
                for kb in range(4):
                    fw = []
                    if kb == 0:
                        fw = [(c_wh, c_wh.n), (c_c2, c_c2.n)]
                    E("scalar", lambda e, kb=kb, hb=hb: e.mul(
                        out=wkvbv_h[hb][:, kb * 128:(kb + 1) * 128],
                        in_=stage_w[:, kb * 128:(kb + 1) * 128], mul=kvlnw[:, kb:kb + 1]),
                      waits=fw, inc=(c_whf if kb == 3 else None))
                E("sync", lambda e, h=h, hb=hb: e.dma_start(
                    out=qpe_sl[hb][:].rearrange("p (a j) -> p a j", j=DR),
                    in_=qpe_s[:, h * DR:(h + 1) * DR].rearrange("(a p) j -> p a j", p=128)),
                  waits=[(c_qpew[0], c_qpew[0].n), (c_qpew[1], c_qpew[1].n),
                         (c_tq, c_tq.n)], inc=c_qpeh, inc_amt=16)

        def head_dmas(h, first=False):
            for st in range(6):
                head_stage(h, st, first=first)

        def prep(h, pb0, pb1):
            hb = h % 2
            whf_mark = (c_whf, c_whf.n)
            qpe_mark = (c_qpeh, c_qpeh.n)
            groups = []

            def bn_group(cci):
                def g(bps, boff, waits):
                    for kb in range(12):
                        E("tensor", lambda e, kb=kb, cci=cci, bps=bps, boff=boff: e.matmul(
                            out=bps[:, boff:boff + 512],
                            lhsT=wqbn_h[hb][:, kb * 128:(kb + 1) * 128],
                            rhs=qa_v()[:, kb, cci * 512:(cci + 1) * 512],
                            start=(kb == 0), stop=(kb == 11)),
                          waits=(waits if kb == 0 else []),
                          inc=(c_bn if kb == 11 else None))
                    E("scalar", lambda e, cci=cci, bps=bps, boff=boff: e.activation(
                        out=QnT[hb][:, cci * 512:(cci + 1) * 512],
                        in_=bps[:, boff:boff + 512], func=AF.Copy),
                      waits=[(c_bn, c_bn.n)], inc=c_bn_ev)
                    return (c_bn_ev, c_bn_ev.n)
                return g, [whf_mark]

            def c1_group(cci):
                def g(bps, boff, waits):
                    for kb in range(4):
                        E("tensor", lambda e, kb=kb, cci=cci, bps=bps, boff=boff: e.matmul(
                            out=bps[:, boff:boff + 512],
                            lhsT=wkvbk_h[hb][:, kb * 128:(kb + 1) * 128],
                            rhs=ckv_v()[:, kb, cci * 512:(cci + 1) * 512],
                            start=(kb == 0), stop=(kb == 3)),
                          waits=(waits if kb == 0 else []),
                          inc=(c_c1 if kb == 3 else None))
                    E("scalar", lambda e, cci=cci, bps=bps, boff=boff: e.activation(
                        out=KnT[hb][:, cci * 512:(cci + 1) * 512],
                        in_=bps[:, boff:boff + 512], func=AF.Copy),
                      waits=[(c_c1, c_c1.n)], inc=c_c1_ev)
                    return (c_c1_ev, c_c1_ev.n)
                return g, [whf_mark]

            def c2_group(gg):
                def g(bps, boff, waits):
                    first = [True]
                    for ti2 in range(4):
                        tt2 = gg * 4 + ti2
                        for kb in range(4):
                            E("tensor", lambda e, tt2=tt2, kb=kb, ti2=ti2, bps=bps, boff=boff: e.matmul(
                                out=bps[:, boff + ti2 * 128:boff + (ti2 + 1) * 128],
                                lhsT=ckv_v()[:, kb, tt2 * 128:(tt2 + 1) * 128],
                                rhs=wkvbv_h[hb][:, kb * 128:(kb + 1) * 128],
                                start=(ti2 == 0 and kb == 0), stop=(ti2 == 3 and kb == 3)),
                              waits=(waits if first[0] else []),
                              inc=(c_c2 if (ti2 == 3 and kb == 3) else None))
                            first[0] = False
                    E("vector", lambda e, gg=gg, bps=bps, boff=boff: e.tensor_copy(
                        out=Vh[hb][:, gg * 512:(gg + 1) * 512],
                        in_=bps[:, boff:boff + 512]),
                      waits=[(c_c2, c_c2.n)], inc=c_c2_ev)
                    return (c_c2_ev, c_c2_ev.n)
                return g, [whf_mark]

            def tq_group(gg):
                def g(bps, boff, waits):
                    for j in range(4):
                        tt2 = gg * 4 + j
                        E("tensor", lambda e, tt2=tt2, j=j, bps=bps, boff=boff: e.matmul(
                            out=bps[0:64, boff + j * 128:boff + (j + 1) * 128],
                            lhsT=qpe_sl[hb][:, tt2 * DR:(tt2 + 1) * DR], rhs=ident[:],
                            start=(j == 0), stop=(j == 3)),
                          waits=(waits if j == 0 else []),
                          inc=(c_tq if j == 3 else None))
                    E("vector", lambda e, gg=gg, bps=bps, boff=boff: e.tensor_copy(
                        out=QpeT_aug[hb][0:64, gg * 512:(gg + 1) * 512],
                        in_=bps[0:64, boff:boff + 512]),
                      waits=[(c_tq, c_tq.n)], inc=c_tq_ev)
                    return (c_tq_ev, c_tq_ev.n)
                return g, [qpe_mark]

            for cci in range(4):
                groups.append(bn_group(cci))
            for cci in range(4):
                groups.append(c1_group(cci))
            for gg in range(4):
                groups.append(c2_group(gg))
            for gg in range(4):
                groups.append(tq_group(gg))

            banks = [(psA, 0), (psA, 512), (psA, 1024), (psA, 1536),
                     (psB, 1024), (psB, 1536)]
            last = {i: None for i in range(6)}
            for i, (g, extra) in enumerate(groups):
                bsel = i % 6
                bps, boff = banks[bsel]
                w = list(extra)
                if last[bsel] is None:
                    w += (pb1 if bsel == 5 else pb0)
                else:
                    w.append(last[bsel])
                last[bsel] = g(bps, boff, w)
            marks = [last[4] or pb0[0], last[5] or pb0[0],
                     (c_bn_ev, c_bn_ev.n), (c_c1_ev, c_c1_ev.n),
                     (c_c2_ev, c_c2_ev.n), (c_tq_ev, c_tq_ev.n)]
            return marks, [last[i] for i in range(4)]

        def emit_mhat(h, pm):
            hb = h % 2
            if kpesq_pending[0]:
                kpesq_pending[0] = False
                E("vector", lambda e: e.tensor_mul(
                    out=kpesq[0:64, :], in0=kpeT_aug[0:64, :], in1=kpeT_aug[0:64, :]),
                  waits=[(c_tkpe_ev, c_tkpe_ev.n)], inc=c_mv)
                E("vector", lambda e: e.reduce_max(
                    out=kmax[0:64, :], in_=kpesq[0:64, :], axis=AX.X),
                  waits=[(c_mv, c_mv.n)], inc=c_mv)
                E("vector", lambda e: e.tensor_copy(
                    out=kpemax_bf[:], in_=kmax[0:64, :]),
                  waits=[(c_mv, c_mv.n)], inc=c_mv)
            E("scalar", lambda e, hb=hb: e.activation(
                out=sq_all[:], in_=KnT[hb][:], func=AF.Square),
              waits=[(c_mq, c_mq.n), (c_c1_ev, c_c1_ev.n), (c_sc, c_sc.n)],
              inc=c_sc)
            E("vector", lambda e: e.reduce_max(
                out=kmax[:], in_=sq_all[:], axis=AX.X),
              waits=[(c_sc, c_sc.n), (c_mv, c_mv.n)], inc=c_mv)
            E("vector", lambda e: e.tensor_copy(out=kmax_bf[:], in_=kmax[:]),
              waits=[(c_mv, c_mv.n)], inc=c_mv)
            km_mark = c_mv.n
            E("scalar", lambda e, hb=hb: e.activation(
                out=sq_all[:], in_=QnT[hb][:], func=AF.Square),
              waits=[(c_mv, km_mark), (c_bn_ev, c_bn_ev.n), (c_sc, c_sc.n)],
              inc=c_sc)
            E("scalar", lambda e, hb=hb: e.activation(
                out=sq_pe[0:64, :], in_=QpeT_aug[hb][0:64, :], func=AF.Square),
              waits=[(c_tq_ev, c_tq_ev.n), (c_sc, c_sc.n)], inc=c_sc)
            qsq_mark = c_sc.n
            smw = [(c_mv, km_mark)] + list(pm[0:2]) + \
                [(c_sdone, c_sdone.n)] + list(bank3_marks)
            E("tensor", lambda e: e.matmul(
                out=psB[0:1, 1700:1701], lhsT=kmax_bf[:], rhs=ones_col[:],
                start=True, stop=False),
              waits=smw)
            E("tensor", lambda e: e.matmul(
                out=psB[0:1, 1700:1701], lhsT=kpemax_bf[:], rhs=ones_col[0:64, :],
                start=False, stop=True),
              inc=c_mt)
            E("scalar", lambda e: e.activation(
                out=lnmk[:], in_=psB[0:1, 1700:1701], func=AF.Ln,
                scale=SCALE * SCALE * 1.05),
              waits=[(c_mt, c_mt.n)], inc=c_sc)
            lnmk_mark = c_sc.n
            for qt in range(NT):
                E("tensor", lambda e, qt=qt: e.matmul(
                    out=psB[:, 1040 + qt:1041 + qt],
                    lhsT=sq_all[:, qt * 128:(qt + 1) * 128], rhs=ones_col[:],
                    start=True, stop=False),
                  waits=([(c_sc, qsq_mark)] if qt == 0 else []))
                E("tensor", lambda e, qt=qt: e.matmul(
                    out=psB[:, 1040 + qt:1041 + qt],
                    lhsT=sq_pe[0:64, qt * 128:(qt + 1) * 128], rhs=ones_col[0:64, :],
                    start=False, stop=True),
                  inc=(c_mq if qt == NT - 1 else None))
            E("tensor", lambda e: e.matmul(
                out=psB[:, 1704:1705], lhsT=halfs1[:], rhs=lnmk[:],
                start=True, stop=True),
              waits=[(c_sc, lnmk_mark)], inc=c_mt)
            E("vector", lambda e: e.tensor_copy(out=lnmk_col[:], in_=psB[:, 1704:1705]),
              waits=[(c_mt, c_mt.n)], inc=c_mv)
            lc_mark = c_mv.n
            E("scalar", lambda e, hb=hb: e.activation(
                out=neg_mh[:, hb * 16:(hb + 1) * 16], in_=psB[:, 1040:1056], func=AF.Ln),
              waits=[(c_mq, c_mq.n)], inc=c_sc)
            E("scalar", lambda e, hb=hb: e.activation(
                out=neg_mh[:, hb * 16:(hb + 1) * 16],
                in_=neg_mh[:, hb * 16:(hb + 1) * 16], func=AF.Exp, scale=0.5,
                bias=lnmk_col[:]),
              waits=[(c_sc, c_sc.n), (c_mv, lc_mark)], inc=c_sc)
            E("vector", lambda e, hb=hb: e.tensor_scalar(
                out=neg_mh[:, hb * 16:(hb + 1) * 16],
                in0=neg_mh[:, hb * 16:(hb + 1) * 16], scalar1=-1.0,
                scalar2=None, op0=ALU.mult),
              waits=[(c_sc, c_sc.n)], inc=c_mv)
            return (c_mv, c_mv.n)

        head_dmas(0, first=True)
        otw_at = {}
        bank3_marks = []
        bcast_ev_mark = [None]
        prev_w2 = [None]

        def augstats(h, mh_mark):
            hb = h % 2
            E("vector", lambda e, hb=hb: e.tensor_copy(
                out=stat_h[:, 0:16], in_=neg_mh[:, hb * 16:(hb + 1) * 16]),
              waits=[mh_mark, (c_statT, c_statT.n)], inc=c_v)
            E("vector", lambda e, hb=hb: e.tensor_sub(
                out=stat_h[:, 16:32], in0=neg_mh[:, hb * 16:(hb + 1) * 16],
                in1=stat_h[:, 0:16]),
              waits=[(c_v, c_v.n)], inc=c_stat)
            E("tensor", lambda e: e.matmul(
                out=psB[0:32, 1536:1664], lhsT=stat_h[:], rhs=ident[:],
                start=True, stop=True),
              waits=[(c_stat, c_stat.n)] + list(bank3_marks), inc=c_statT)
            E("vector", lambda e: e.tensor_copy(out=sb_st[:], in_=psB[0:32, 1536:1664]),
              waits=[(c_statT, c_statT.n), (c_aug, c_aug.n)], inc=c_statT_ev)
            E("scalar", lambda e, hb=hb: e.dma_start(
                out=QpeT_aug[hb][64:65, :], in_=sb_st[0:16, :]),
              waits=[(c_statT_ev, c_statT_ev.n)], inc=c_aug, inc_amt=16)
            E("scalar", lambda e, hb=hb: e.dma_start(
                out=QpeT_aug[hb][65:66, :], in_=sb_st[16:32, :]),
              inc=c_aug, inc_amt=16)
            return (c_statT_ev, c_statT_ev.n)

        prep_marks, prep_psA = prep(0, slot_marks["A"] + slot_marks["B"], slot_marks["B"])
        mh_ready = emit_mhat(0, prep_marks)
        bank3_marks = [prep_marks[1], mh_ready]
        st_ev_mark = augstats(0, mh_ready)
        bank3_marks = [st_ev_mark]

        for h in range(NH):
            hb = h % 2
            stage_at = {0: 0, 3: 1, 4: 2, 5: 3, 7: 4, 9: 5}

            aug_mark = (c_aug, c_aug.n)
            av_head_base = c_av.n
            binit = [(c_pt, c_pt.n)]
            if bcast_ev_mark[0]:
                binit.append(bcast_ev_mark[0])
            if h == 0 and prev_w2[0]:
                binit.append(prev_w2[0])
            st_last = {0: list(binit), 1: list(binit), 2: list(binit)}
            sg_first = list(bank3_marks)
            st_ctr = 0
            av_abs = []
            pm_ev = [m for m in prep_marks if m]
            for kt in range(NT):
                qmin = kt * 128
                pto = 2048 * (kt % 2)
                ch_list = _chunks(qmin, S)
                pt_of_chunk = []
                for (q0, wd) in ch_list:
                    bsel = st_ctr % 3
                    st_ctr += 1
                    boff = 512 * bsel
                    stw = list(st_last[bsel]) if isinstance(st_last[bsel], list) \
                        else [st_last[bsel]]
                    E("tensor", lambda e, hb=hb, kt=kt, q0=q0, wd=wd, boff=boff: e.matmul(
                        out=psB[:, boff:boff + wd],
                        lhsT=KnT[hb][:, kt * 128:(kt + 1) * 128],
                        rhs=QnT[hb][:, q0:q0 + wd], start=True, stop=False),
                      waits=stw)
                    E("tensor", lambda e, hb=hb, kt=kt, q0=q0, wd=wd, boff=boff: e.matmul(
                        out=psB[:, boff:boff + wd],
                        lhsT=kpeT_aug[:, kt * 128:(kt + 1) * 128],
                        rhs=QpeT_aug[hb][:, q0:q0 + wd], start=False, stop=True),
                      waits=([aug_mark] if kt == 0 and q0 == qmin else []),
                      inc=c_st)
                    ew = [(c_st, c_st.n)]
                    if q0 == qmin:
                        if kt >= 2:
                            ew.append((c_av, av_abs[kt - 2]))
                        elif av_head_base > 0:
                            ew.append((c_av, av_head_base - (1 if kt == 0 else 0)))
                    E("scalar", lambda e, q0=q0, wd=wd, boff=boff, pto=pto: e.activation(
                        out=ptpool[:, pto + q0:pto + q0 + wd],
                        in_=psB[:, boff:boff + wd], func=AF.Exp),
                      waits=ew, inc=c_pt)
                    st_last[bsel] = (c_pt, c_pt.n)
                    pt_of_chunk.append(c_pt.n)
                    if q0 == qmin:
                        E("vector", lambda e, qmin=qmin, pto=pto: e.tensor_tensor(
                            out=ptpool[:, pto + qmin:pto + qmin + 128],
                            in0=ptpool[:, pto + qmin:pto + qmin + 128],
                            in1=maskT[:], op=ALU.mult),
                          waits=[(c_pt, c_pt.n)], inc=c_ptm)
                for i2, (q0, wd) in enumerate(ch_list):
                    avw = [(c_pt, pt_of_chunk[i2])]
                    if i2 == 0:
                        avw.append((c_ptm, c_ptm.n))
                        if kt == 0:
                            avw += pm_ev
                            if h > 0 or sq > 0:
                                avw.append((c_ot_ev, c_ot_ev.n))
                    E("tensor", lambda e, hb=hb, kt=kt, q0=q0, wd=wd, pto=pto: e.matmul(
                        out=psA[:, q0:q0 + wd],
                        lhsT=Vh[hb][:, kt * 128:(kt + 1) * 128],
                        rhs=ptpool[:, pto + q0:pto + q0 + wd],
                        start=(kt == 0),
                        stop=(kt == min(NT - 1, (q0 + wd - 1) // 128))),
                      waits=avw, inc=c_av)
                av_abs.append(c_av.n)
                if h + 1 < NH and kt in stage_at:
                    head_stage(h + 1, stage_at[kt])
                for qt in range(kt, NT):
                    sgw = []
                    if kt == 0 and qt == 0:
                        sgw = list(sg_first)
                    E("tensor", lambda e, kt=kt, qt=qt, pto=pto: e.matmul(
                        out=psB[:, 1536 + qt:1537 + qt],
                        lhsT=ptpool[:, pto + qt * 128:pto + (qt + 1) * 128],
                        rhs=ones_col[:],
                        start=(kt == 0 and qt == 0),
                        stop=(kt == NT - 1 and qt == NT - 1)),
                      waits=sgw,
                      inc=(c_sg if qt == NT - 1 else None))

            E("vector", lambda e, h=h: e.tensor_copy(
                out=ot_stage[h % 2][:], in_=psA[:, :]),
              waits=[(c_av, c_av.n)] +
                    ([(c_otw, c_otw.n)] if h >= 2 else []),
              inc=c_ot_ev)
            ev_mark = (c_ot_ev, c_ot_ev.n)
            E("vector", lambda e: e.reciprocal(out=invcol[:], in_=psB[:, 1536:1552]),
              waits=[(c_sg, c_sg.n)], inc=c_mv)
            rec_mark = (c_mv, c_mv.n)
            E("tensor", lambda e: e.matmul(
                out=psB[0:16, 1552:1680], lhsT=invcol[:], rhs=identf[:],
                start=True, stop=True),
              waits=[rec_mark], inc=c_statT)
            E("vector", lambda e: e.tensor_copy(out=invT[:], in_=psB[0:16, 1552:1680]),
              waits=[(c_statT, c_statT.n)], inc=c_v)
            E("vector", lambda e: e.tensor_copy(out=invThi[:], in_=invT[:]),
              waits=[(c_v, c_v.n)], inc=c_v)
            E("vector", lambda e: e.tensor_sub(out=invTlo[:], in0=invT[:], in1=invThi[:]),
              waits=[(c_v, c_v.n)], inc=c_v)
            iv_mark = (c_v, c_v.n)
            E("scalar", lambda e: e.dma_start(out=invflat[:, 0:2048], in_=invThi[:]),
              waits=[iv_mark, (c_aug, c_aug.n)] +
                    ([prev_w2[0]] if prev_w2[0] else []),
              inc=c_aug, inc_amt=16)
            E("sync", lambda e: e.dma_start(out=invflat[:, 2048:4096], in_=invTlo[:]),
              waits=[iv_mark, (c_flo, c_flo.n)] +
                    ([prev_w2[0]] if prev_w2[0] else []),
              inc=c_flo, inc_amt=16)
            flat_mark = [(c_aug, c_aug.n), (c_flo, c_flo.n)]
            bank3_marks = [(c_statT_ev, c_statT_ev.n), rec_mark, iv_mark]

            if h + 1 < NH:
                prep_marks, prep_psA = prep(
                    h + 1,
                    [ev_mark, (c_pt, c_pt.n)],
                    [(c_pt, c_pt.n)] + list(bank3_marks))

            pm01 = [prep_marks[0]] if h + 1 < NH else []
            for qc in range(2):
                bw2 = list(flat_mark) + [(c_pt, c_pt.n)] if qc == 0 else []
                E("tensor", lambda e, qc=qc: e.matmul(
                    out=psB[:, qc * 512:(qc + 1) * 512], lhsT=ones1[:],
                    rhs=invflat[:, qc * 512:(qc + 1) * 512], start=True, stop=False),
                  waits=bw2)
                E("tensor", lambda e, qc=qc: e.matmul(
                    out=psB[:, qc * 512:(qc + 1) * 512], lhsT=ones1[:],
                    rhs=invflat[:, 2048 + qc * 512:2048 + (qc + 1) * 512],
                    start=False, stop=True),
                  inc=(c_sg if qc == 1 else None))
            sgm1 = (c_sg, c_sg.n)
            E("scalar", lambda e: e.activation(
                out=invb[:, 0:1024], in_=psB[:, 0:1024], func=AF.Copy),
              waits=[sgm1, (c_sc, c_sc.n)], inc=c_sc)
            ib1 = (c_sc, c_sc.n)
            for qc in range(2, 4):
                bw2 = [ib1] if qc == 2 else []
                E("tensor", lambda e, qc=qc: e.matmul(
                    out=psB[:, (qc - 2) * 512:(qc - 1) * 512], lhsT=ones1[:],
                    rhs=invflat[:, qc * 512:(qc + 1) * 512], start=True, stop=False),
                  waits=bw2)
                E("tensor", lambda e, qc=qc: e.matmul(
                    out=psB[:, (qc - 2) * 512:(qc - 1) * 512], lhsT=ones1[:],
                    rhs=invflat[:, 2048 + qc * 512:2048 + (qc + 1) * 512],
                    start=False, stop=True),
                  inc=(c_sg if qc == 3 else None))
            sgm2 = (c_sg, c_sg.n)
            prev_w2[0] = sgm2
            E("scalar", lambda e: e.activation(
                out=invb[:, 1024:2048], in_=psB[:, 0:1024], func=AF.Copy),
              waits=[sgm2, (c_sc, c_sc.n)], inc=c_sc)
            ib2 = (c_sc, c_sc.n)
            bcast_ev_mark[0] = ib2

            if h + 1 < NH:
                mh_ready = emit_mhat(h + 1, prep_marks)

            E("vector", lambda e, h=h: e.tensor_mul(
                out=ot_stage[h % 2][:], in0=ot_stage[h % 2][:], in1=invb[:]),
              waits=[ev_mark, ib2, (c_v, c_v.n)], inc=c_v)
            mul_mark = (c_v, c_v.n)
            E("scalar", lambda e, h=h: e.dma_start(
                out=ot_s[h], in_=ot_stage[h % 2][:]),
              waits=[mul_mark], inc=c_otw, inc_amt=16)
            otw_at[h] = c_otw.n

            if h + 1 < NH:
                bank3_marks = [prep_marks[1], mh_ready]
                st_ev_mark = augstats(h + 1, mh_ready)
                bank3_marks = [st_ev_mark]

        slot_marks["A"] = [(c_ot_ev, c_ot_ev.n)]
        slot_marks["B"] = [(c_pt, c_pt.n), (c_v, c_v.n), (c_sg, c_sg.n),
                           (c_statT_ev, c_statT_ev.n)] + list(bank3_marks)

        gate_att = [(c_av, c_av.n), (c_ot_ev, c_ot_ev.n), (c_otw, c_otw.n),
                    (c_sg, c_sg.n), (c_v, c_v.n), (c_mv, c_mv.n),
                    (c_whf, c_whf.n),
                    (c_bn_ev, c_bn_ev.n), (c_c1_ev, c_c1_ev.n), (c_c2_ev, c_c2_ev.n),
                    (c_tq_ev, c_tq_ev.n), (c_statT_ev, c_statT_ev.n),
                    (c_aug, c_aug.n), (c_ptm, c_ptm.n)]
        g2ctx.close()
        close_right()
        open_right()
        wo_bf = rsb("wo_bf", [128, NH * 2048], BF16)
        OT_wo = rsb("OT_wo", [128, NH * 2048], BF16)
        x_re = [rsb(f"xre{i}", [128, D], F32) for i in range(2)]
        r1_sb = [rsb(f"r1sb{i}", [128, D], F32) for i in range(2)]
        ln2w = [rsb(f"ln2w{i}", [128, D], BF16) for i in range(2)]
        ln2ts = [rsb(f"ln2ts{i}", [128, D], BF16) for i in range(2)]

        for hh in range(NH):
            E("gpsimd", lambda e, hh=hh: e.dma_start(
                out=wo_bf[:, hh * 2048:(hh + 1) * 2048],
                in_=wo[hh * 128:(hh + 1) * 128, :]),
              waits=(list(gate_att) + [(c_fb, c_fb.n), (c_ffn, c_ffn.n)] if hh == 0 else []),
              inc=c_wo, inc_amt=16)
        for hh in range(NH):
            E("sync", lambda e, hh=hh: e.dma_start(
                out=OT_wo[:, hh * 2048:(hh + 1) * 2048], in_=ot_s[hh]),
              waits=(list(gate_att) if hh == 0 else []),
              inc=c_otld, inc_amt=16)
        wo_loaded = c_wo.n
        ot_loaded = c_otld.n
        r1_at = {}
        r1wp_at = {}
        trFw_at = {}
        l2w_at = {}
        evict_at = {}
        ln2_at = {}

        def emit_trW(tt):
            bi = tt % 2
            ps = psA if tt % 2 == 0 else psB
            for g in range(4):
                gs = 512 * (g % 2)
                tw = [(c_ln2, ln2_at[tt])]
                if g < 2:
                    tw.append((c_r1, r1_at[tt]))
                else:
                    tw.append((c_trF_ev, c_trF_ev.n - 1))
                for j in range(4):
                    blk = g * 4 + j
                    E("tensor", lambda e, blk=blk, gs=gs, j=j, bi=bi, ps=ps: e.matmul(
                        out=ps[:, gs + j * 128:gs + (j + 1) * 128],
                        lhsT=ln2w[bi][:, blk * 128:(blk + 1) * 128], rhs=ident[:],
                        start=(j == 0), stop=(j == 3)),
                      waits=(tw if j == 0 else []),
                      inc=(c_trF if j == 3 else None))
                evw = [(c_trF, c_trF.n)]
                if g == 0 and tt >= 2:
                    evw.append((c_l2w[bi], l2w_at[tt - 2]))
                E("scalar", lambda e, g=g, gs=gs, bi=bi, ps=ps: e.activation(
                    out=ln2ts[bi][:, g * 512:(g + 1) * 512], in_=ps[:, gs:gs + 512],
                    func=AF.Copy),
                  waits=evw, inc=c_trF_ev)
            trFw_at[tt] = c_trF.n
            evict_at[tt] = c_trF_ev.n
            E("scalar", lambda e, tt=tt, bi=bi: e.dma_start(
                out=ln2t_s[tt], in_=ln2ts[bi][:]),
              waits=[(c_trF_ev, c_trF_ev.n)], inc=c_l2w[bi], inc_amt=16)
            l2w_at[tt] = c_l2w[bi].n

        for tt in range(NT):
            bi = tt % 2
            sl = "A" if tt % 2 == 0 else "B"
            ps = psA if sl == "A" else psB
            xrw = list(gate_att) if tt < 2 else [(c_r1, r1_at[tt - 2])]
            E("sync", lambda e, sq=sq, tt=tt, bi=bi: e.dma_start(
                out=x_re[bi][:], in_=x_in[sq, tt * 128:(tt + 1) * 128, :]),
              waits=xrw, inc=c_xre[bi], inc_amt=16)
            xre_mark = c_xre[bi].n
            w0 = list(slot_marks[sl]) if tt < 2 else \
                [(c_r1, r1_at[tt - 2]), (c_trF_ev, evict_at[tt - 2])]
            if tt == 0:
                w0 += [(c_wo, wo_loaded), (c_otld, ot_loaded)]
            first = True
            for hh in range(NH):
                for cci in range(4):
                    E("tensor", lambda e, tt=tt, hh=hh, cci=cci, ps=ps: e.matmul(
                        out=ps[:, cci * 512:(cci + 1) * 512],
                        lhsT=OT_wo[:, hh * 2048 + tt * 128:hh * 2048 + (tt + 1) * 128],
                        rhs=wo_bf[:, hh * 2048 + cci * 512:hh * 2048 + (cci + 1) * 512],
                        start=(hh == 0), stop=(hh == NH - 1)),
                      waits=(w0 if first else []),
                      inc=(c_woG if (hh == NH - 1 and cci == 3) else None))
                    first = False
            if tt >= 1:
                emit_trW(tt - 1)
            E("vector", lambda e, bi=bi, ps=ps: e.tensor_add(
                out=r1_sb[bi][:], in0=ps[:, :], in1=x_re[bi][:]),
              waits=[(c_woG, c_woG.n), (c_xre[bi], xre_mark)] +
                    ([(c_r1wp[bi], r1wp_at[tt - 2]), (c_ln2, ln2_at[tt - 2])]
                     if tt >= 2 else []),
              inc=c_r1)
            r1_at[tt] = c_r1.n
            E("scalar", lambda e, bi=bi: e.activation(
                out=ptpool[:, 0:D], in_=r1_sb[bi][:], func=AF.Copy, scale=-1.0 / D,
                accum_out=stat1[:, 2:3]),
              waits=[(c_r1, c_r1.n)], inc=c_sc)
            E("scalar", lambda e, bi=bi: e.activation(
                out=ptpool[:, 0:D], in_=r1_sb[bi][:], func=AF.Square, scale=D ** -0.5,
                accum_out=stat1[:, 1:2]),
              waits=[(c_sc, c_sc.n)], inc=c_sc)
            E("scalar", lambda e: e.activation(
                out=stat1[:, 3:4], in_=stat1[:, 2:3], func=AF.Square),
              waits=[(c_sc, c_sc.n)], inc=c_sc)
            E("scalar", lambda e: e.activation(
                out=stat1[:, 4:5], in_=stat1[:, 3:4], func=AF.Identity, scale=-1.0,
                bias=stat1[:, 1:2]),
              waits=[(c_sc, c_sc.n)], inc=c_sc)
            E("scalar", lambda e: e.activation(
                out=stat1[:, 5:6], in_=stat1[:, 4:5], func=AF.Ln, bias=cvals[:, 0:1]),
              waits=[(c_sc, c_sc.n)], inc=c_sc)
            E("scalar", lambda e, tt=tt: e.activation(
                out=stat_ln2[:, 2 * tt + 1:2 * tt + 2], in_=stat1[:, 5:6], func=AF.Exp,
                scale=-0.5, bias=cvals[:, 3:4]),
              waits=[(c_sc, c_sc.n)], inc=c_ln2s)
            E("scalar", lambda e, tt=tt: e.activation(
                out=stat_ln2[:, 2 * tt:2 * tt + 1], in_=stat1[:, 2:3], func=AF.Identity,
                scale=stat_ln2[:, 2 * tt + 1:2 * tt + 2]),
              waits=[(c_ln2s, c_ln2s.n)], inc=c_ln2s)
            E("scalar", lambda e, tt=tt, bi=bi: e.dma_start(
                out=r1_s[tt * 128:(tt + 1) * 128, :], in_=r1_sb[bi][:]),
              waits=[(c_ln2s, c_ln2s.n)] +
                    ([(c_r1wp[bi], c_r1wp[bi].n)] if tt >= 2 else []),
              inc=c_r1wp[bi], inc_amt=16)
            r1wp_at[tt] = c_r1wp[bi].n
            E("scalar", lambda e, tt=tt, bi=bi: e.activation(
                out=ln2w[bi][:], in_=r1_sb[bi][:], func=AF.Identity,
                bias=stat_ln2[:, 2 * tt:2 * tt + 1],
                scale=stat_ln2[:, 2 * tt + 1:2 * tt + 2]),
              waits=[(c_ln2s, c_ln2s.n)] + ([(c_trF, trFw_at[tt - 2])] if tt >= 2 else []),
              inc=c_ln2)
            ln2_at[tt] = c_ln2.n
        emit_trW(NT - 1)
        slot_wo = {"A": [(c_r1, r1_at[14]), (c_trF_ev, evict_at[14])],
                   "B": [(c_r1, r1_at[15]), (c_trF_ev, evict_at[15])]}
        r1w_final = [c_r1wp[0].n, c_r1wp[1].n]
        l2w_final = [c_l2w[0].n, c_l2w[1].n]

        close_right()
        open_right()
        f_bf = rsb("f_bf", [128, 16 * 2048], BF16)
        r1f = [rsb(f"r1f{i}", [128, D], F32) for i in range(3)]
        ln2T = [rsb(f"ln2T{i}", [128, D], BF16) for i in range(2)]
        out_sb = [rsb(f"outsb{i}", [128, D], F32) for i in range(2)]

        gate_wo = [(c_woG, c_woG.n), (c_r1, c_r1.n)]

        fld_at = {}
        for kb in range(16):
            pch = c_fldp[kb % 4]
            eng = "gpsimd" if kb % 2 == 0 else "scalar"
            fw = []
            if kb <= 1:
                fw = list(gate_wo) + [(c_fb, c_fb.n)]
            if kb >= 4:
                fw.append((pch, pch.n))
            E(eng, lambda e, kb=kb: e.dma_start(
                out=f_bf[:, kb * 2048:(kb + 1) * 2048],
                in_=f_s[kb * 128:(kb + 1) * 128, :]),
              waits=fw, inc=pch, inc_amt=16)
            fld_at[kb] = (pch, pch.n)
        fld_done = [fld_at[kb] for kb in range(16)]
        outv_at = {}
        ffn_at = {}
        out_at = {}
        for tt in range(NT):
            bi = tt % 2
            ri = tt % 3
            sl = "A" if tt % 2 == 0 else "B"
            ps = psA if sl == "A" else psB
            rw = (list(gate_wo) if tt < 3 else [(c_outv, outv_at[tt - 3])]) + \
                [(c_r1wp[tt % 2], r1wp_at[tt])]
            E("sync", lambda e, tt=tt, ri=ri: e.dma_start(
                out=r1f[ri][:], in_=r1_s[tt * 128:(tt + 1) * 128, :]),
              waits=rw, inc=c_r1r[ri], inc_amt=16)
            r1r_mark = (c_r1r[ri], c_r1r[ri].n)
            E("sync", lambda e, tt=tt, bi=bi, l2w_at=l2w_at: e.dma_start(
                out=ln2T[bi][:], in_=ln2t_s[tt]),
              waits=[(c_l2w[bi], l2w_at[tt])] +
                    ([(c_ffn, ffn_at[tt - 2])] if tt >= 2 else []),
              inc=c_l2r[bi], inc_amt=16)
            l2r_mark = (c_l2r[bi], c_l2r[bi].n)
            ffw = [(c_bias, 2)]
            if tt >= 2:
                ffw.append((c_outv, outv_at[tt - 2]))
            else:
                ffw += slot_wo[sl]
            first = True
            for cci in range(4):
                E("tensor", lambda e, cci=cci, ps=ps: e.matmul(
                    out=ps[:, cci * 512:(cci + 1) * 512], lhsT=ones1[:],
                    rhs=bias_hl[:, cci * 512:(cci + 1) * 512], start=True, stop=False),
                  waits=(ffw if first else []))
                first = False
                E("tensor", lambda e, cci=cci, ps=ps: e.matmul(
                    out=ps[:, cci * 512:(cci + 1) * 512], lhsT=ones1[:],
                    rhs=bias_hl[:, D + cci * 512:D + (cci + 1) * 512],
                    start=False, stop=False))
            ffn_marks = {}
            for kb in range(16):
                for cci in range(4):
                    E("tensor", lambda e, kb=kb, cci=cci, bi=bi, ps=ps: e.matmul(
                        out=ps[:, cci * 512:(cci + 1) * 512],
                        lhsT=ln2T[bi][:, kb * 128:(kb + 1) * 128],
                        rhs=f_bf[:, kb * 2048 + cci * 512:kb * 2048 + (cci + 1) * 512],
                        start=False, stop=(kb == 15)),
                      waits=(([fld_at[kb]] if tt == 0 else []) +
                             ([l2r_mark] if kb == 0 else [])
                             if cci == 0 else []),
                      inc=(c_ffn if kb == 15 else None))
                    if kb == 15:
                        ffn_marks[cci] = c_ffn.n
            ffn_at[tt] = c_ffn.n
            ov0 = [(c_out[bi], out_at[tt - 2])] if tt >= 2 else []
            for cci in range(4):
                E("vector", lambda e, bi=bi, ri=ri, cci=cci, ps=ps: e.tensor_add(
                    out=out_sb[bi][:, cci * 512:(cci + 1) * 512],
                    in0=ps[:, cci * 512:(cci + 1) * 512],
                    in1=r1f[ri][:, cci * 512:(cci + 1) * 512]),
                  waits=[(c_ffn, ffn_marks[cci])] +
                        ([r1r_mark] + ov0 if cci == 0 else []),
                  inc=c_outv, inc_amt=(1 if cci < 3 else 2))
                if cci < 2:
                    outv_at[(tt, cci)] = c_outv.n
            outv_at[tt] = c_outv.n
            E("sync", lambda e, sq=sq, tt=tt, bi=bi: e.dma_start(
                out=x_out[sq, tt * 128:(tt + 1) * 128, :], in_=out_sb[bi][:]),
              waits=[(c_outv, outv_at[tt])], inc=c_out[bi], inc_amt=16)
            out_at[tt] = c_out[bi].n
        slot_marks["A"] = [(c_outv, outv_at[14])]
        slot_marks["B"] = [(c_outv, outv_at[15])]
        gate = [(c_ffn, c_ffn.n), (c_outv, c_outv.n), (c_trF_ev, c_trF_ev.n),
                (c_out[0], c_out[0].n), (c_out[1], c_out[1].n),
                (c_ln2, c_ln2.n), (c_r1r[0], c_r1r[0].n), (c_r1r[1], c_r1r[1].n),
                (c_r1r[2], c_r1r[2].n), (c_l2r[0], c_l2r[0].n),
                (c_l2r[1], c_l2r[1].n)]
        close_right()

    for sq in range(NSEQ):
        _emit_seq(sq)

    p.wait("sync", [(c_out[0], c_out[0].n), (c_out[1], c_out[1].n)])

    with nc.Block() as block:
        @block.tensor
        def _(eng):
            for f in p.ops["tensor"]:
                f(eng)

        @block.vector
        def _(eng):
            for f in p.ops["vector"]:
                f(eng)

        @block.scalar
        def _(eng):
            for f in p.ops["scalar"]:
                f(eng)

        @block.sync
        def _(eng):
            for f in p.ops["sync"]:
                f(eng)

        @block.gpsimd
        def _(eng):
            for f in p.ops["gpsimd"]:
                f(eng)

    ctx.close()
    return nc


def _host_consts():
    ident = np.eye(128, dtype=ml_dtypes.bfloat16)
    dmaskv = np.zeros((128, 128), np.float32)
    dmaskv[np.triu_indices(128, 1)] = NEG
    tmaskv = np.triu(np.ones((128, 128), np.float64)).astype(ml_dtypes.bfloat16)
    inv = 1.0 / (10000.0 ** (np.arange(0, DR, 2, dtype=np.float64) / DR))
    t = np.arange(S, dtype=np.float64)
    fr = np.outer(t, inv)
    emb = np.concatenate([fr, fr], axis=1)
    cosv = np.cos(emb).astype(np.float32)
    sinv = np.sin(emb)
    sinn = sinv.copy()
    sinn[:, 0:DR // 2] = -sinn[:, 0:DR // 2]
    sinn = sinn.astype(np.float32)
    return ident, dmaskv, tmaskv, cosv, sinn


def kernel(L0_attn_hidden_states, L0_attn_Wqa, L0_attn_q_a_ln_w, L0_attn_Wqb,
           L0_attn_Wkva, L0_attn_kv_a_ln_w, L0_attn_Wkvb, L0_attn_Wo,
           L0_ffn_factors, L0_ffn_bias):
    x = np.ascontiguousarray(L0_attn_hidden_states)
    ident, dmaskv, tmaskv, cosv, sinn = _host_consts()
    nc = build_nc()
    common = {
        "wqa": np.ascontiguousarray(L0_attn_Wqa),
        "qln": np.ascontiguousarray(L0_attn_q_a_ln_w),
        "wqb": np.ascontiguousarray(L0_attn_Wqb),
        "wkva": np.ascontiguousarray(L0_attn_Wkva),
        "kvln": np.ascontiguousarray(L0_attn_kv_a_ln_w),
        "wkvb": np.ascontiguousarray(L0_attn_Wkvb),
        "wo": np.ascontiguousarray(L0_attn_Wo),
        "fac": np.ascontiguousarray(L0_ffn_factors),
        "fbias": np.ascontiguousarray(L0_ffn_bias),
        "identc": ident, "identfc": np.eye(128, dtype=np.float32),
        "dmaskc": dmaskv, "tmaskc": tmaskv,
        "cosc": cosv, "sinnc": sinn,
    }
    in_maps = []
    for c in range(8):
        m = dict(common)
        m["x"] = np.ascontiguousarray(x[2 * c:2 * c + 2])
        in_maps.append(m)
    res = run_bass_kernel_spmd(nc, in_maps, core_ids=list(range(8)))
    return np.concatenate([r["out"] for r in res.results], axis=0)


if __name__ == "__main__":
    nc = build_nc()
    print("built ok; instructions:", nc.n_instructions())


# version 22 (final)
# speedup vs baseline: 1.1030x; 1.0286x over previous
import math
import numpy as np
import ml_dtypes
from contextlib import ExitStack

import concourse.bass as bass
import concourse.mybir as mybir
from concourse.bass_utils import run_bass_kernel_spmd

F32 = mybir.dt.float32
BF16 = mybir.dt.bfloat16
AF = mybir.ActivationFunctionType
ALU = mybir.AluOpType
AX = mybir.AxisListType

S = 2048
D = 2048
NT = 16
NH = 16
QL = 1536
KVL = 512
DN = 128
DR = 64
DV = 128
NSEQ = 2
SCALE = (DN + DR) ** -0.5
NEG = -1.0e30
EPS_RMS = 1e-6
EPS_LN = 1e-5
INV_SQRT_RANK = 1.0 / math.sqrt(8.0)


def _chunks(qmin, qmax, bank=512):
    out = []
    q = qmin
    while q < qmax:
        w = min(bank - (q % bank), qmax - q)
        out.append((q, w))
        q += w
    return out


class Ch:
    def __init__(self, nc, name):
        self.h = nc.alloc_semaphore(name=name)
        self.n = 0
        self.name = name

    @property
    def mark(self):
        return (self, self.n)


class Prog:
    ENGINES = ("tensor", "vector", "scalar", "sync", "gpsimd")

    def __init__(self, nc):
        self.nc = nc
        self.ops = {e: [] for e in self.ENGINES}

    def emit(self, eng, fn, waits=(), inc=None, inc_amt=1):
        wl = [(c.h if isinstance(c, Ch) else c, v) for c, v in waits]
        if inc is not None:
            inc.n += inc_amt
            ih, amt = inc.h, inc_amt
        else:
            ih, amt = None, 0

        def run(engine):
            for h, v in wl:
                engine.wait_ge(h, v)
            ins = fn(engine)
            if ih is not None:
                ins.then_inc(ih, amt)
        self.ops[eng].append(run)

    def wait(self, eng, waits):
        wl = [(c.h if isinstance(c, Ch) else c, v) for c, v in waits]

        def run(engine):
            for h, v in wl:
                engine.wait_ge(h, v)
        self.ops[eng].append(run)


def build_nc():
    nc = bass.Bass("TRN2", target_bir_lowering=False,
                   dynamic_dma_scratch_size=16384)

    x_in = nc.dram_tensor("x", [NSEQ, S, D], F32, kind="ExternalInput")
    wqa = nc.dram_tensor("wqa", [D, QL], F32, kind="ExternalInput")
    qln = nc.dram_tensor("qln", [QL], F32, kind="ExternalInput")
    wqb = nc.dram_tensor("wqb", [QL, NH * (DN + DR)], F32, kind="ExternalInput")
    wkva = nc.dram_tensor("wkva", [D, KVL + DR], F32, kind="ExternalInput")
    kvln = nc.dram_tensor("kvln", [KVL], F32, kind="ExternalInput")
    wkvb = nc.dram_tensor("wkvb", [KVL, NH * (DN + DV)], F32, kind="ExternalInput")
    wo = nc.dram_tensor("wo", [NH * DV, D], F32, kind="ExternalInput")
    fac = nc.dram_tensor("fac", [8, D, D], F32, kind="ExternalInput")
    fbias = nc.dram_tensor("fbias", [D], F32, kind="ExternalInput")
    identc = nc.dram_tensor("identc", [128, 128], BF16, kind="ExternalInput")
    identfc = nc.dram_tensor("identfc", [128, 128], F32, kind="ExternalInput")
    dmaskc = nc.dram_tensor("dmaskc", [128, 128], F32, kind="ExternalInput")
    tmaskc = nc.dram_tensor("tmaskc", [128, 128], BF16, kind="ExternalInput")
    cosc = nc.dram_tensor("cosc", [S, DR], F32, kind="ExternalInput")
    sinnc = nc.dram_tensor("sinnc", [S, DR], F32, kind="ExternalInput")

    x_out = nc.dram_tensor("out", [NSEQ, S, D], F32, kind="ExternalOutput")

    qpe_s = nc.dram_tensor("qpe_s", [S, NH * DR], BF16, kind="Internal")
    r1_s = nc.dram_tensor("r1_s", [S, D], F32, kind="Internal")
    f_s = nc.dram_tensor("f_s", [D, D], BF16, kind="Internal")
    ot_s = nc.dram_tensor("ot_s", [NH, 128, S], BF16, kind="Internal")
    ln2t_s = nc.dram_tensor("ln2t_s", [NT, 128, D], BF16, kind="Internal")

    ctx = ExitStack()

    def sb(name, shape, dt, side="left"):
        return ctx.enter_context(nc.sbuf_tensor(name, shape, dt, side=side))

    ident = sb("ident", [128, 128], BF16)
    identf = sb("identf", [128, 128], F32)
    diag = sb("diag", [128, 128], F32)
    maskT = sb("maskT", [128, 128], BF16)
    ones1 = sb("ones1", [1, 128], BF16)
    halfs1 = sb("halfs1", [1, 128], BF16)
    ones_col = sb("ones_col", [128, 1], BF16)
    bias_hl = sb("bias_hl", [1, 2 * D], BF16)
    qlnw = sb("qlnw", [128, 12], F32)
    kvlnw = sb("kvlnw", [128, 4], F32)
    ptpool = sb("ptpool", [128, 2 * S], BF16)
    acc_f = sb("acc_f", [128, D], F32)
    stat1 = sb("stat1", [128, 16], F32)
    stat_h = sb("stat_h", [128, 32], BF16)
    sb_st = sb("sb_st", [32, 128], BF16)
    stat_ln2 = sb("stat_ln2", [128, 2 * NT], F32)
    cvals = sb("cvals", [128, 4], F32)

    psA = ctx.enter_context(nc.psum_tensor("psA", [128, 2048], F32))
    psB = ctx.enter_context(nc.psum_tensor("psB", [128, 2048], F32))

    p = Prog(nc)
    E = p.emit
    ch = lambda name: Ch(nc, name)

    c_const = ch("const")
    c_fb = ch("fb")
    c_bias = ch("bias")
    c_x = [ch("xload0"), ch("xload1")]
    c_lnv = ch("lnv")
    c_ln1 = ch("ln1")
    c_trL = ch("trL"); c_trL_ev = ch("trL_ev")
    c_wa = ch("wa")
    c_ckv = ch("ckv"); c_ckv_ev = ch("ckv_ev"); c_rskv = ch("rskv")
    c_qa = ch("qa"); c_qa_ev = ch("qa_ev"); c_rsqa = ch("rsqa")
    c_trQ = ch("trQ"); c_trQ_ev = ch("trQ_ev")
    c_trC = ch("trC"); c_trC_ev = ch("trC_ev")
    c_wpe = ch("wpe"); c_wpef = ch("wpef")
    c_bpe = ch("bpe"); c_bpe_ev = ch("bpe_ev")
    c_qpew = [ch("qpew0"), ch("qpew1")]
    c_tkpe = ch("tkpe"); c_tkpe_ev = ch("tkpe_ev")
    c_wh = ch("wh"); c_whf = ch("whf"); c_qpeh = ch("qpeh")
    c_bn = ch("bn"); c_bn_ev = ch("bn_ev")
    c_c1 = ch("c1"); c_c1_ev = ch("c1_ev")
    c_c2 = ch("c2"); c_c2_ev = ch("c2_ev")
    c_tq = ch("tq"); c_tq_ev = ch("tq_ev")
    c_s = ch("s"); c_smax = ch("smax"); c_sdone = ch("sdone"); c_sln = ch("sln")
    c_stat = ch("stat"); c_statT = ch("statT"); c_statT_ev = ch("statT_ev")
    c_aug = ch("aug")
    c_st = ch("st"); c_pt = ch("pt"); c_ptm = ch("ptm"); c_av = ch("av")
    c_ot_ev = ch("ot_ev"); c_otw = ch("otw"); c_otld = ch("otld")
    c_wo = ch("wo"); c_xre = [ch("xre0"), ch("xre1")]; c_woG = ch("woG")
    c_r1 = ch("r1"); c_ln2s = ch("ln2s")
    c_r1wp = [ch("r1wp0"), ch("r1wp1")]
    c_fld = ch("fld"); c_r1r = [ch("r1r0"), ch("r1r1"), ch("r1r2")]; c_ln2 = ch("ln2")
    c_trF = ch("trF"); c_trF_ev = ch("trF_ev")
    c_ffn = ch("ffn"); c_outv = ch("outv")
    c_out = [ch("out0"), ch("out1")]
    c_cos = ch("cos"); c_bld = ch("bld")
    c_v = ch("vroll"); c_sc = ch("scroll")
    c_mq = ch("mq"); c_mt = ch("mt"); c_mv = ch("mv"); c_sg = ch("sg")
    c_l2w = [ch("l2w0"), ch("l2w1")]
    c_fldp = [ch(f"fldp{i}") for i in range(4)]
    c_l2r = [ch("l2r0"), ch("l2r1")]
    c_flo = ch("flo")

    for fn in [
        lambda e: e.dma_start(out=ident[:], in_=identc[:]),
        lambda e: e.dma_start(out=identf[:], in_=identfc[:]),
        lambda e: e.dma_start(out=diag[:], in_=dmaskc[:]),
        lambda e: e.dma_start(out=maskT[:], in_=tmaskc[:]),
        lambda e: e.dma_start(out=qlnw[:], in_=qln[:].rearrange("(a p) -> p a", p=128),
                              allow_slow_non_contiguous=True),
        lambda e: e.dma_start(out=kvlnw[:], in_=kvln[:].rearrange("(a p) -> p a", p=128),
                              allow_slow_non_contiguous=True),
    ]:
        E("sync", fn, inc=c_const, inc_amt=16)
    E("vector", lambda e: e.memset(ones1[:], 1.0))
    E("vector", lambda e: e.memset(halfs1[:], 0.5))
    E("vector", lambda e: e.memset(ones_col[:], 1.0))
    c_cc = ch("cc")
    E("vector", lambda e: e.memset(cvals[:, 0:1], EPS_LN))
    E("vector", lambda e: e.memset(cvals[:, 1:2], EPS_RMS))
    E("vector", lambda e: e.memset(cvals[:, 2:3], math.log(SCALE)))
    E("vector", lambda e: e.memset(cvals[:, 3:4], math.log(INV_SQRT_RANK)),
      inc=c_cc)
    p.wait("scalar", [(c_cc, 1), (c_const, c_const.n)])
    p.wait("tensor", [(c_const, c_const.n)])
    p.wait("vector", [(c_const, c_const.n)])

    E("sync", lambda e: e.dma_start(out=acc_f[0:1, 0:D], in_=fbias[:].unsqueeze(0)),
      inc=c_bld, inc_amt=16)
    E("vector", lambda e: e.tensor_copy(out=bias_hl[:, 0:D], in_=acc_f[0:1, 0:D]),
      waits=[(c_bld, 16)], inc=c_bias)
    E("vector", lambda e: e.tensor_sub(out=bias_hl[:, D:2 * D], in0=acc_f[0:1, 0:D],
                                       in1=bias_hl[:, 0:D]),
      waits=[(c_bias, 1)], inc=c_bias)

    def emit_f_build():
        p.wait("gpsimd", [(c_pt, 40)])
        for rb in range(16):
            E("gpsimd", lambda e, rb=rb: e.dma_start(
                out=acc_f[:], in_=fac[0, rb * 128:(rb + 1) * 128, :]),
              waits=[(c_bias, 2)] if rb == 0 else [(c_fb, c_fb.n)],
              inc=c_fb, inc_amt=16)
            for r in range(1, 8):
                E("gpsimd", lambda e, r=r, rb=rb: e.dma_start(
                    out=acc_f[:], in_=fac[r, rb * 128:(rb + 1) * 128, :],
                    accum_op=ALU.add),
                  waits=[(c_fb, c_fb.n)], inc=c_fb, inc_amt=16)
            E("gpsimd", lambda e, rb=rb: e.dma_start(
                out=f_s[rb * 128:(rb + 1) * 128, :], in_=acc_f[:]),
              waits=[(c_fb, c_fb.n)], inc=c_fb, inc_amt=16)

    slot_marks = {"A": [], "B": []}

    rctx = [None]

    def open_right():
        rctx[0] = ExitStack()

    _uid = [0]

    def rsb(name, shape, dt):
        _uid[0] += 1
        return rctx[0].enter_context(
            nc.sbuf_tensor(f"{name}_u{_uid[0]}", shape, dt, side="right"))

    def close_right():
        rctx[0].close()
        rctx[0] = None

    gate = []

    def _emit_seq(sq):
        nonlocal gate
        open_right()
        xs = [rsb(f"xs{i}", [128, D], F32) for i in range(2)]
        ln1_bf = rsb("ln1_bf", [128, D], BF16)
        _l1t = rsb("ln1T", [128, D], BF16)
        ln1T = [_l1t, _l1t]
        qa_n = rsb("qa_n", [128, QL], BF16)
        ckv_n = rsb("ckv_n", [128, KVL], BF16)
        rope1 = rsb("rope1", [128, 1024], BF16)
        rope2 = rsb("rope2", [128, 1024], BF16)
        cos_t = rsb("cos_t", [128, NT * DR], F32)
        sinn_t = rsb("sinn_t", [128, NT * DR], F32)
        kpe_tok = rsb("kpe_tok", [128, NT * DR], BF16)
        g2ctx = ExitStack()
        qa_nT = g2ctx.enter_context(nc.sbuf_tensor(f"qa_nT_{sq}", [128, 12 * S], BF16, side="left"))
        ckv_nT = g2ctx.enter_context(nc.sbuf_tensor(f"ckv_nT_{sq}", [128, 4 * S], BF16, side="left"))
        kpeT_aug = g2ctx.enter_context(nc.sbuf_tensor(f"kpeT_aug_{sq}", [66, S], BF16, side="left"))
        E("vector", lambda e: e.memset(kpeT_aug[64:66, :], 1.0),
          waits=(list(gate) if sq else []))
        wa_ctx = ExitStack()
        wa_bf = wa_ctx.enter_context(nc.sbuf_tensor(f"wa_bf_{sq}", [128, 16 * 2112], BF16, side="right"))

        qa_v = lambda: qa_nT[:].rearrange("p (a t) -> p a t", t=S)
        ckv_v = lambda: ckv_nT[:].rearrange("p (a t) -> p a t", t=S)

        cosv = cosc[:].rearrange("(a p) j -> p a j", p=128)
        sinv = sinnc[:].rearrange("(a p) j -> p a j", p=128)
        E("sync", lambda e: e.dma_start(
            out=cos_t[:].rearrange("p (a j) -> p a j", j=DR), in_=cosv),
          waits=list(gate), inc=c_cos, inc_amt=16)
        E("sync", lambda e: e.dma_start(
            out=sinn_t[:].rearrange("p (a j) -> p a j", j=DR), in_=sinv),
          inc=c_cos, inc_amt=16)
        cos_ready = c_cos.n

        for kb in range(16):
            E("gpsimd", lambda e, kb=kb: e.dma_start(
                out=wa_bf[:, kb * 2112:kb * 2112 + 1536],
                in_=wqa[kb * 128:(kb + 1) * 128, :]),
              waits=(list(gate) if kb == 0 else []), inc=c_wa, inc_amt=16)
            E("gpsimd", lambda e, kb=kb: e.dma_start(
                out=wa_bf[:, kb * 2112 + 1536:kb * 2112 + 2112],
                in_=wkva[kb * 128:(kb + 1) * 128, :]),
              inc=c_wa, inc_amt=16)
        wa_done = c_wa.n

        ln1_base = c_ln1.n
        for tt in range(NT):
            ti = tt % 2
            xw = list(gate) if tt == 0 else ([] if tt == 1 else [(c_ln1, ln1_base + tt - 1)])
            E("sync", lambda e, sq=sq, tt=tt, ti=ti: e.dma_start(
                out=xs[ti][:], in_=x_in[sq, tt * 128:(tt + 1) * 128, :]),
              waits=xw, inc=c_x[ti], inc_amt=16)
            E("scalar", lambda e, ti=ti: e.activation(
                out=ptpool[:, 0:D], in_=xs[ti][:], func=AF.Copy, scale=-1.0 / D,
                accum_out=stat1[:, 2:3]),
              waits=[(c_x[ti], c_x[ti].n)] + ([(c_av, c_av.n)] if (sq > 0 and tt == 0) else []),
              inc=c_sc)
            E("scalar", lambda e, ti=ti: e.activation(
                out=ptpool[:, 0:D], in_=xs[ti][:], func=AF.Square, scale=D ** -0.5,
                accum_out=stat1[:, 1:2]),
              waits=[(c_sc, c_sc.n)], inc=c_sc)
            E("scalar", lambda e: e.activation(
                out=stat1[:, 3:4], in_=stat1[:, 2:3], func=AF.Square),
              waits=[(c_sc, c_sc.n)], inc=c_sc)
            E("scalar", lambda e: e.activation(
                out=stat1[:, 4:5], in_=stat1[:, 3:4], func=AF.Identity, scale=-1.0,
                bias=stat1[:, 1:2]),
              waits=[(c_sc, c_sc.n)], inc=c_sc)
            E("scalar", lambda e: e.activation(
                out=stat1[:, 5:6], in_=stat1[:, 4:5], func=AF.Ln, bias=cvals[:, 0:1]),
              waits=[(c_sc, c_sc.n)], inc=c_sc)
            E("scalar", lambda e: e.activation(
                out=stat1[:, 5:6], in_=stat1[:, 5:6], func=AF.Exp, scale=-0.5),
              waits=[(c_sc, c_sc.n)], inc=c_sc)
            E("scalar", lambda e: e.activation(
                out=stat1[:, 6:7], in_=stat1[:, 2:3], func=AF.Identity,
                scale=stat1[:, 5:6]),
              waits=[(c_sc, c_sc.n)], inc=c_sc)
            lw = [(c_sc, c_sc.n)] + \
                ([(c_trL, c_trL.n)] if tt > 0 or sq > 0 else [])
            E("scalar", lambda e, ti=ti: e.activation(
                out=ln1_bf[:], in_=xs[ti][:], func=AF.Identity,
                bias=stat1[:, 6:7], scale=stat1[:, 5:6]),
              waits=lw, inc=c_ln1)

            for g in range(4):
                gs = 512 * (g % 2)
                tw = [(c_ln1, c_ln1.n), (c_trQ_ev, c_trQ_ev.n)]
                if sq == 0 and tt == 0 and g < 2:
                    pass
                else:
                    tw.append((c_trL_ev, c_trL_ev.n - 1))
                if sq > 0 and tt == 0 and g < 2:
                    tw += slot_marks["B"]
                for j in range(4):
                    blk = g * 4 + j
                    E("tensor", lambda e, blk=blk, gs=gs, j=j: e.matmul(
                        out=psB[:, gs + j * 128:gs + (j + 1) * 128],
                        lhsT=ln1_bf[:, blk * 128:(blk + 1) * 128], rhs=ident[:],
                        start=(j == 0), stop=(j == 3)),
                      waits=(tw if j == 0 else []),
                      inc=(c_trL if j == 3 else None))
                E("vector", lambda e, ti=ti, g=g, gs=gs: e.tensor_copy(
                    out=ln1T[ti][:, g * 512:(g + 1) * 512], in_=psB[:, gs:gs + 512]),
                  waits=[(c_trL, c_trL.n), (c_qa, c_qa.n), (c_ckv, c_ckv.n)],
                  inc=c_trL_ev)

            cw = [(c_trL_ev, c_trL_ev.n), (c_wa, wa_done)]
            if sq == 0 and tt == 0:
                pass
            elif tt == 0:
                cw += slot_marks["B"]
            else:
                cw.append((c_ckv_ev, c_ckv_ev.n))
            qw = [(c_trL_ev, c_trL_ev.n)]
            if sq == 0 and tt == 0:
                pass
            elif tt == 0:
                qw += slot_marks["A"]
            else:
                qw.append((c_qa_ev, c_qa_ev.n))
            for kb in range(16):
                for cci in range(5):
                    if cci < 3:
                        dst, do, wdt, woff = psA, cci * 512, 512, cci * 512
                    elif cci == 3:
                        dst, do, wdt, woff = psB, 1024, 512, 1536
                    else:
                        dst, do, wdt, woff = psB, 1536, 64, 2048
                    E("tensor", lambda e, ti=ti, kb=kb, dst=dst, do=do, wdt=wdt, woff=woff: e.matmul(
                        out=dst[:, do:do + wdt],
                        lhsT=ln1T[ti][:, kb * 128:(kb + 1) * 128],
                        rhs=wa_bf[:, kb * 2112 + woff:kb * 2112 + woff + wdt],
                        start=(kb == 0), stop=(kb == 15)),
                      waits=((qw + cw) if (cci == 0 and kb == 0) else []),
                      inc=((c_qa if cci == 2 else (c_ckv if cci == 4 else None))
                           if kb == 15 else None))
            E("scalar", lambda e: e.activation(
                out=ptpool[:, 0:512], in_=psB[:, 1024:1536], func=AF.Square,
                accum_out=stat1[:, 7:8]),
              waits=[(c_ckv, c_ckv.n)], inc=c_sc)
            E("scalar", lambda e: e.activation(
                out=stat1[:, 8:9], in_=stat1[:, 7:8], func=AF.Ln, bias=cvals[:, 1:2],
                scale=1.0 / KVL),
              waits=[(c_sc, c_sc.n)], inc=c_sc)
            E("scalar", lambda e: e.activation(
                out=stat1[:, 8:9], in_=stat1[:, 8:9], func=AF.Exp, scale=-0.5),
              waits=[(c_sc, c_sc.n)], inc=c_rskv)
            ckw = [(c_rskv, c_rskv.n)] + ([(c_trC, c_trC.n)] if tt > 0 or sq > 0 else [])
            E("vector", lambda e: e.tensor_scalar(
                out=ckv_n[:], in0=psB[:, 1024:1536], scalar1=stat1[:, 8:9], scalar2=None,
                op0=ALU.mult), waits=ckw, inc=c_ckv_ev)
            E("vector", lambda e, tt=tt: e.tensor_tensor(
                out=rope1[:, 0:DR], in0=psB[:, 1536:1600],
                in1=cos_t[:, tt * DR:(tt + 1) * DR], op=ALU.mult),
              waits=[(c_cos, cos_ready)], inc=c_v)
            E("vector", lambda e, tt=tt: e.tensor_tensor(
                out=rope2[:, 0:32], in0=psB[:, 1568:1600],
                in1=sinn_t[:, tt * DR:tt * DR + 32], op=ALU.mult), inc=c_v)
            E("vector", lambda e, tt=tt: e.tensor_tensor(
                out=rope2[:, 32:DR], in0=psB[:, 1536:1568],
                in1=sinn_t[:, tt * DR + 32:(tt + 1) * DR], op=ALU.mult), inc=c_v)
            E("vector", lambda e, tt=tt: e.tensor_add(
                out=kpe_tok[:, tt * DR:(tt + 1) * DR], in0=rope1[:, 0:DR],
                in1=rope2[:, 0:DR]),
              waits=[(c_v, c_v.n)], inc=c_ckv_ev)

            pass
            pass
            E("scalar", lambda e: e.activation(
                out=ptpool[:, 0:1536], in_=psA[:, 0:1536], func=AF.Square,
                accum_out=stat1[:, 9:10]),
              waits=[(c_qa, c_qa.n)], inc=c_sc)
            E("scalar", lambda e: e.activation(
                out=stat1[:, 10:11], in_=stat1[:, 9:10], func=AF.Ln, bias=cvals[:, 1:2],
                scale=1.0 / QL),
              waits=[(c_sc, c_sc.n)], inc=c_sc)
            E("scalar", lambda e: e.activation(
                out=stat1[:, 10:11], in_=stat1[:, 10:11], func=AF.Exp, scale=-0.5,
                bias=cvals[:, 2:3]),
              waits=[(c_sc, c_sc.n)], inc=c_rsqa)
            qnw = [(c_rsqa, c_rsqa.n)] + ([(c_trQ, c_trQ.n)] if tt > 0 or sq > 0 else [])
            E("vector", lambda e: e.tensor_scalar(
                out=qa_n[:], in0=psA[:, 0:1536], scalar1=stat1[:, 10:11], scalar2=None,
                op0=ALU.mult), waits=qnw, inc=c_qa_ev)

            tcw = [(c_ckv_ev, c_ckv_ev.n - 1), (c_trL_ev, c_trL_ev.n),
                   (c_trQ_ev, c_trQ_ev.n)]
            if tt > 0 or sq > 0:
                tcw.append((c_trC_ev, c_trC_ev.n))
            for j in range(4):
                E("tensor", lambda e, j=j: e.matmul(
                    out=psB[:, j * 128:(j + 1) * 128],
                    lhsT=ckv_n[:, j * 128:(j + 1) * 128], rhs=ident[:],
                    start=(j == 0), stop=(j == 3)),
                  waits=(tcw if j == 0 else []),
                  inc=(c_trC if j == 3 else None))
            E("vector", lambda e, tt=tt: e.tensor_copy(
                out=ckv_v()[:, :, tt * 128:(tt + 1) * 128],
                in_=psB[:, 0:512].rearrange("p (a t) -> p a t", t=128)),
              waits=[(c_trC, c_trC.n)], inc=c_trC_ev)

            for g in range(3):
                gs = 512 if g % 2 == 0 else 0
                tqw = [(c_qa_ev, c_qa_ev.n)]
                tqw.append((c_trC_ev, c_trC_ev.n) if gs == 0 else (c_trL_ev, c_trL_ev.n))
                if not (sq == 0 and tt == 0 and g == 0):
                    tqw.append((c_trQ_ev, c_trQ_ev.n - 1))
                for j in range(4):
                    blk = g * 4 + j
                    E("tensor", lambda e, blk=blk, gs=gs, j=j: e.matmul(
                        out=psB[:, gs + j * 128:gs + (j + 1) * 128],
                        lhsT=qa_n[:, blk * 128:(blk + 1) * 128], rhs=ident[:],
                        start=(j == 0), stop=(j == 3)),
                      waits=(tqw if j == 0 else []),
                      inc=(c_trQ if j == 3 else None))
                E("vector", lambda e, g=g, tt=tt, gs=gs: e.tensor_copy(
                    out=qa_v()[:, g * 4:(g + 1) * 4, tt * 128:(tt + 1) * 128],
                    in_=psB[:, gs:gs + 512].rearrange("p (a t) -> p a t", t=128)),
                  waits=[(c_trQ, c_trQ.n)], inc=c_trQ_ev)

        slot_marks["A"] = [(c_qa_ev, c_qa_ev.n)]
        slot_marks["B"] = [(c_trQ_ev, c_trQ_ev.n), (c_trC_ev, c_trC_ev.n),
                           (c_ckv_ev, c_ckv_ev.n)]

        wa_ctx.close()
        bctx = ExitStack()
        wqbpe_bf = bctx.enter_context(nc.sbuf_tensor(f"wqbpe_bf_{sq}", [128, 12 * 1024], BF16, side="right"))
        qpe_stage = [bctx.enter_context(nc.sbuf_tensor(f"qpes{i}_{sq}", [128, NH * DR], BF16, side="right"))
                     for i in range(2)]
        pe_cols = lambda kb: wqb[kb * 128:(kb + 1) * 128, :].rearrange(
            "p (h c) -> p h c", c=DN + DR)[:, :, DN:DN + DR]
        for kb in range(12):
            E("gpsimd", lambda e, kb=kb: e.dma_start(
                out=rope1[:].rearrange("p (h j) -> p h j", j=DR), in_=pe_cols(kb)),
              waits=([(c_qa, c_qa.n), (c_ckv, c_ckv.n), (c_ckv_ev, c_ckv_ev.n),
                      (c_trQ_ev, c_trQ_ev.n)]
                     if kb == 0 else [(c_wpef, c_wpef.n)]),
              inc=c_wpe, inc_amt=16)
            E("scalar", lambda e, kb=kb: e.mul(
                out=wqbpe_bf[:, kb * 1024:(kb + 1) * 1024], in_=rope1[:],
                mul=qlnw[:, kb:kb + 1]),
              waits=[(c_wpe, c_wpe.n)], inc=c_wpef)
        if sq == 0:
            emit_f_build()
        bpe_ev_at = {}
        for tt in range(NT):
            base = 1024 * (tt % 2)
            bw = [(c_wpef, c_wpef.n)]
            if tt < 2:
                bw += slot_marks["A"]
            else:
                bw.append((c_bpe_ev, bpe_ev_at[tt - 2]))
            for kb in range(12):
                for cci in range(2):
                    E("tensor", lambda e, tt=tt, kb=kb, cci=cci, base=base: e.matmul(
                        out=psA[:, base + cci * 512:base + (cci + 1) * 512],
                        lhsT=qa_v()[:, kb, tt * 128:(tt + 1) * 128],
                        rhs=wqbpe_bf[:, kb * 1024 + cci * 512:kb * 1024 + (cci + 1) * 512],
                        start=(kb == 0), stop=(kb == 11)),
                      waits=(bw if (cci == 0 and kb == 0) else []),
                      inc=(c_bpe if (kb == 11 and cci == 1) else None))
            pe = psA[:, base:base + 1024].rearrange("p (h j) -> p h j", j=DR)
            cos_b = cos_t[:, tt * DR:(tt + 1) * DR].unsqueeze(1).broadcast_to([128, NH, DR])
            sin_lo = sinn_t[:, tt * DR:tt * DR + 32].unsqueeze(1).broadcast_to([128, NH, 32])
            sin_hi = sinn_t[:, tt * DR + 32:(tt + 1) * DR].unsqueeze(1).broadcast_to([128, NH, 32])
            rw = [(c_bpe, c_bpe.n), (c_bpe_ev, c_bpe_ev.n)]
            if tt >= 2:
                rw.append((c_qpew[tt % 2], c_qpew[tt % 2].n))
            E("vector", lambda e, pe=pe, cos_b=cos_b: e.tensor_tensor(
                out=rope1[:].rearrange("p (h j) -> p h j", j=DR), in0=pe, in1=cos_b,
                op=ALU.mult), waits=rw, inc=c_v)
            E("vector", lambda e, pe=pe, sin_lo=sin_lo: e.tensor_tensor(
                out=rope2[:].rearrange("p (h j) -> p h j", j=DR)[:, :, 0:32],
                in0=pe[:, :, 32:DR], in1=sin_lo, op=ALU.mult), inc=c_v)
            E("vector", lambda e, pe=pe, sin_hi=sin_hi: e.tensor_tensor(
                out=rope2[:].rearrange("p (h j) -> p h j", j=DR)[:, :, 32:DR],
                in0=pe[:, :, 0:32], in1=sin_hi, op=ALU.mult), inc=c_v)
            E("vector", lambda e, tt=tt: e.tensor_add(
                out=qpe_stage[tt % 2][:], in0=rope1[:], in1=rope2[:]),
              waits=[(c_v, c_v.n)], inc=c_bpe_ev)
            bpe_ev_at[tt] = c_bpe_ev.n
            E("scalar", lambda e, tt=tt: e.dma_start(
                out=qpe_s[tt * 128:(tt + 1) * 128, :], in_=qpe_stage[tt % 2][:]),
              waits=[(c_bpe_ev, c_bpe_ev.n)], inc=c_qpew[tt % 2], inc_amt=16)
        slot_marks["A"] = [(c_bpe_ev, c_bpe_ev.n)]

        for g in range(4):
            gs = 512 * (g % 2)
            kw = [(c_ckv_ev, c_ckv_ev.n)]
            if g < 2:
                kw += slot_marks["B"]
            else:
                kw.append((c_tkpe_ev, c_tkpe_ev.n - 1))
            for j in range(4):
                tt2 = g * 4 + j
                E("tensor", lambda e, tt2=tt2, gs=gs, j=j: e.matmul(
                    out=psB[0:64, gs + j * 128:gs + (j + 1) * 128],
                    lhsT=kpe_tok[:, tt2 * DR:(tt2 + 1) * DR], rhs=ident[:],
                    start=(j == 0), stop=(j == 3)),
                  waits=(kw if j == 0 else []),
                  inc=(c_tkpe if j == 3 else None))
            E("vector", lambda e, g=g, gs=gs: e.tensor_copy(
                out=kpeT_aug[0:64, g * 512:(g + 1) * 512], in_=psB[0:64, gs:gs + 512]),
              waits=[(c_tkpe, c_tkpe.n)], inc=c_tkpe_ev)
        slot_marks["B"] = [(c_tkpe_ev, c_tkpe_ev.n)]
        kpesq_pending = [True]

        bctx.close()
        close_right()
        open_right()
        ot_stage = [rsb(f"otst{i}", [128, S], BF16) for i in range(2)]
        stage_w = rsb("stage_w", [128, 1536], F32)
        sq_all = rsb("sq_all", [128, S], BF16)
        sq_pe = rsb("sq_pe", [64, S], BF16)
        sq_q = rsb("sq_q", [128, S], BF16)
        kpesq = rsb("kpesq", [64, S], BF16)
        neg_mh = rsb("neg_mh", [128, 32], F32)
        stat_sa = rsb("stat_sa", [128, 32], F32)
        stat_sb = rsb("stat_sb", [128, 32], F32)
        kmax = rsb("kmax", [128, 1], F32)
        kmax_bf = rsb("kmax_bf", [128, 1], BF16)
        kpemax_bf = rsb("kpemax_bf", [64, 1], BF16)
        lnmk = rsb("lnmk", [1, 1], BF16)
        lnmk_col = rsb("lnmk_col", [128, 1], F32)
        invcol = rsb("invcol", [128, 16], F32)
        invT = rsb("invT", [16, 128], F32)
        invb = rsb("invb", [128, D], F32)
        invThi = rsb("invThi", [16, 128], BF16)
        invTlo = rsb("invTlo", [16, 128], BF16)
        invflat = rsb("invflat", [1, 4096], BF16)
        wqbn_h = [rsb(f"wqbnh{i}", [128, 12 * 128], BF16) for i in range(2)]
        wkvbk_h = [rsb(f"wkvbkh{i}", [128, 4 * 128], BF16) for i in range(2)]
        wkvbv_h = [rsb(f"wkvbvh{i}", [128, 4 * 128], BF16) for i in range(2)]
        QnT = [rsb(f"QnT{i}", [128, S], BF16) for i in range(2)]
        QpeT_aug = [rsb(f"QpeTa{i}", [66, S], BF16) for i in range(2)]
        KnT = [rsb(f"KnT{i}", [128, S], BF16) for i in range(2)]
        Vh = [rsb(f"Vh{i}", [128, S], BF16) for i in range(2)]
        qpe_sl = [rsb(f"qpesl{i}", [128, NH * DR], BF16) for i in range(2)]

        gate_A = [(c_bpe, c_bpe.n), (c_bpe_ev, c_bpe_ev.n), (c_tkpe, c_tkpe.n),
                  (c_wpef, c_wpef.n), (c_qpew[0], c_qpew[0].n), (c_qpew[1], c_qpew[1].n), (c_tkpe_ev, c_tkpe_ev.n)]

        def head_stage(h, stage, first=False):
            hb = h % 2
            if stage == 0:
                hw = list(gate_A) if first else [(c_whf, c_whf.n)]
                E("sync", lambda e, h=h: e.dma_start(
                    out=stage_w[:].rearrange("p (a c) -> p a c", c=128),
                    in_=wqb[:, h * 192:h * 192 + 128].rearrange("(a p) c -> p a c", p=128)),
                  waits=hw, inc=c_wh, inc_amt=16)
            elif stage in (1, 2, 3):
                for kb in range((stage - 1) * 4, stage * 4):
                    fw = []
                    if kb == 0:
                        fw = [(c_wh, c_wh.n), (c_bn, c_bn.n)]
                    E("scalar", lambda e, kb=kb, hb=hb: e.mul(
                        out=wqbn_h[hb][:, kb * 128:(kb + 1) * 128],
                        in_=stage_w[:, kb * 128:(kb + 1) * 128], mul=qlnw[:, kb:kb + 1]),
                      waits=fw, inc=(c_whf if kb == 11 else None))
                if stage == 3:
                    E("sync", lambda e, h=h: e.dma_start(
                        out=stage_w[:, 0:512].rearrange("p (a c) -> p a c", c=128),
                        in_=wkvb[:, h * 256:h * 256 + 128].rearrange("(a p) c -> p a c", p=128)),
                      waits=[(c_whf, c_whf.n)], inc=c_wh, inc_amt=16)
            elif stage == 4:
                for kb in range(4):
                    fw = []
                    if kb == 0:
                        fw = [(c_wh, c_wh.n), (c_c1, c_c1.n)]
                    E("scalar", lambda e, kb=kb, hb=hb: e.mul(
                        out=wkvbk_h[hb][:, kb * 128:(kb + 1) * 128],
                        in_=stage_w[:, kb * 128:(kb + 1) * 128], mul=kvlnw[:, kb:kb + 1]),
                      waits=fw, inc=(c_whf if kb == 3 else None))
                E("sync", lambda e, h=h: e.dma_start(
                    out=stage_w[:, 0:512].rearrange("p (a c) -> p a c", c=128),
                    in_=wkvb[:, h * 256 + 128:h * 256 + 256].rearrange("(a p) c -> p a c", p=128)),
                  waits=[(c_whf, c_whf.n)], inc=c_wh, inc_amt=16)
            elif stage == 5:
                for kb in range(4):
                    fw = []
                    if kb == 0:
                        fw = [(c_wh, c_wh.n), (c_c2, c_c2.n)]
                    E("scalar", lambda e, kb=kb, hb=hb: e.mul(
                        out=wkvbv_h[hb][:, kb * 128:(kb + 1) * 128],
                        in_=stage_w[:, kb * 128:(kb + 1) * 128], mul=kvlnw[:, kb:kb + 1]),
                      waits=fw, inc=(c_whf if kb == 3 else None))
                E("sync", lambda e, h=h, hb=hb: e.dma_start(
                    out=qpe_sl[hb][:].rearrange("p (a j) -> p a j", j=DR),
                    in_=qpe_s[:, h * DR:(h + 1) * DR].rearrange("(a p) j -> p a j", p=128)),
                  waits=[(c_qpew[0], c_qpew[0].n), (c_qpew[1], c_qpew[1].n),
                         (c_tq, c_tq.n)], inc=c_qpeh, inc_amt=16)

        def head_dmas(h, first=False):
            for st in range(6):
                head_stage(h, st, first=first)

        def prep(h, pb0, pb1):
            hb = h % 2
            whf_mark = (c_whf, c_whf.n)
            qpe_mark = (c_qpeh, c_qpeh.n)
            groups = []

            def bn_group(cci):
                def g(bps, boff, waits):
                    for kb in range(12):
                        E("tensor", lambda e, kb=kb, cci=cci, bps=bps, boff=boff: e.matmul(
                            out=bps[:, boff:boff + 512],
                            lhsT=wqbn_h[hb][:, kb * 128:(kb + 1) * 128],
                            rhs=qa_v()[:, kb, cci * 512:(cci + 1) * 512],
                            start=(kb == 0), stop=(kb == 11)),
                          waits=(waits if kb == 0 else []),
                          inc=(c_bn if kb == 11 else None))
                    E("scalar", lambda e, cci=cci, bps=bps, boff=boff: e.activation(
                        out=QnT[hb][:, cci * 512:(cci + 1) * 512],
                        in_=bps[:, boff:boff + 512], func=AF.Copy),
                      waits=[(c_bn, c_bn.n)], inc=c_bn_ev)
                    return (c_bn_ev, c_bn_ev.n)
                return g, [whf_mark]

            def c1_group(cci):
                def g(bps, boff, waits):
                    for kb in range(4):
                        E("tensor", lambda e, kb=kb, cci=cci, bps=bps, boff=boff: e.matmul(
                            out=bps[:, boff:boff + 512],
                            lhsT=wkvbk_h[hb][:, kb * 128:(kb + 1) * 128],
                            rhs=ckv_v()[:, kb, cci * 512:(cci + 1) * 512],
                            start=(kb == 0), stop=(kb == 3)),
                          waits=(waits if kb == 0 else []),
                          inc=(c_c1 if kb == 3 else None))
                    E("scalar", lambda e, cci=cci, bps=bps, boff=boff: e.activation(
                        out=KnT[hb][:, cci * 512:(cci + 1) * 512],
                        in_=bps[:, boff:boff + 512], func=AF.Copy),
                      waits=[(c_c1, c_c1.n)], inc=c_c1_ev)
                    return (c_c1_ev, c_c1_ev.n)
                return g, [whf_mark]

            def c2_group(gg):
                def g(bps, boff, waits):
                    first = [True]
                    for ti2 in range(4):
                        tt2 = gg * 4 + ti2
                        for kb in range(4):
                            E("tensor", lambda e, tt2=tt2, kb=kb, ti2=ti2, bps=bps, boff=boff: e.matmul(
                                out=bps[:, boff + ti2 * 128:boff + (ti2 + 1) * 128],
                                lhsT=ckv_v()[:, kb, tt2 * 128:(tt2 + 1) * 128],
                                rhs=wkvbv_h[hb][:, kb * 128:(kb + 1) * 128],
                                start=(ti2 == 0 and kb == 0), stop=(ti2 == 3 and kb == 3)),
                              waits=(waits if first[0] else []),
                              inc=(c_c2 if (ti2 == 3 and kb == 3) else None))
                            first[0] = False
                    E("vector", lambda e, gg=gg, bps=bps, boff=boff: e.tensor_copy(
                        out=Vh[hb][:, gg * 512:(gg + 1) * 512],
                        in_=bps[:, boff:boff + 512]),
                      waits=[(c_c2, c_c2.n)], inc=c_c2_ev)
                    return (c_c2_ev, c_c2_ev.n)
                return g, [whf_mark]

            def tq_group(gg):
                def g(bps, boff, waits):
                    for j in range(4):
                        tt2 = gg * 4 + j
                        E("tensor", lambda e, tt2=tt2, j=j, bps=bps, boff=boff: e.matmul(
                            out=bps[0:64, boff + j * 128:boff + (j + 1) * 128],
                            lhsT=qpe_sl[hb][:, tt2 * DR:(tt2 + 1) * DR], rhs=ident[:],
                            start=(j == 0), stop=(j == 3)),
                          waits=(waits if j == 0 else []),
                          inc=(c_tq if j == 3 else None))
                    E("vector", lambda e, gg=gg, bps=bps, boff=boff: e.tensor_copy(
                        out=QpeT_aug[hb][0:64, gg * 512:(gg + 1) * 512],
                        in_=bps[0:64, boff:boff + 512]),
                      waits=[(c_tq, c_tq.n)], inc=c_tq_ev)
                    return (c_tq_ev, c_tq_ev.n)
                return g, [qpe_mark]

            for cci in range(4):
                groups.append(c1_group(cci))
            for cci in range(4):
                groups.append(bn_group(cci))
            for gg in range(4):
                groups.append(tq_group(gg))
            for gg in range(4):
                groups.append(c2_group(gg))

            banks = [(psA, 0), (psA, 512), (psA, 1024), (psA, 1536),
                     (psB, 1024), (psB, 1536)]
            last = {i: None for i in range(6)}
            for i, (g, extra) in enumerate(groups):
                bsel = i % 6
                bps, boff = banks[bsel]
                w = list(extra)
                if last[bsel] is None:
                    w += (pb1 if bsel == 5 else pb0)
                else:
                    w.append(last[bsel])
                last[bsel] = g(bps, boff, w)
            marks = [last[4] or pb0[0], last[5] or pb0[0],
                     (c_bn_ev, c_bn_ev.n), (c_c1_ev, c_c1_ev.n),
                     (c_c2_ev, c_c2_ev.n), (c_tq_ev, c_tq_ev.n)]
            return marks, [last[i] for i in range(4)]

        def emit_mhat(h, pm):
            hb = h % 2
            if kpesq_pending[0]:
                kpesq_pending[0] = False
                E("vector", lambda e: e.tensor_mul(
                    out=kpesq[0:64, :], in0=kpeT_aug[0:64, :], in1=kpeT_aug[0:64, :]),
                  waits=[(c_tkpe_ev, c_tkpe_ev.n)], inc=c_mv)
                E("vector", lambda e: e.reduce_max(
                    out=kmax[0:64, :], in_=kpesq[0:64, :], axis=AX.X),
                  waits=[(c_mv, c_mv.n)], inc=c_mv)
                E("vector", lambda e: e.tensor_copy(
                    out=kpemax_bf[:], in_=kmax[0:64, :]),
                  waits=[(c_mv, c_mv.n)], inc=c_mv)
            E("scalar", lambda e, hb=hb: e.activation(
                out=sq_all[:], in_=KnT[hb][:], func=AF.Square),
              waits=[(c_mv, c_mv.n), (c_c1_ev, c_c1_ev.n), (c_sc, c_sc.n)],
              inc=c_sc)
            E("vector", lambda e: e.reduce_max(
                out=kmax[:], in_=sq_all[:], axis=AX.X),
              waits=[(c_sc, c_sc.n), (c_mv, c_mv.n)], inc=c_mv)
            E("vector", lambda e: e.tensor_copy(out=kmax_bf[:], in_=kmax[:]),
              waits=[(c_mv, c_mv.n)], inc=c_mv)
            km_mark = c_mv.n
            E("scalar", lambda e, hb=hb: e.activation(
                out=sq_q[:], in_=QnT[hb][:], func=AF.Square),
              waits=[(c_mq, c_mq.n), (c_bn_ev, c_bn_ev.n), (c_sc, c_sc.n)],
              inc=c_sc)
            E("scalar", lambda e, hb=hb: e.activation(
                out=sq_pe[0:64, :], in_=QpeT_aug[hb][0:64, :], func=AF.Square),
              waits=[(c_tq_ev, c_tq_ev.n), (c_sc, c_sc.n)], inc=c_sc)
            qsq_mark = c_sc.n
            smw = [(c_mv, km_mark)] + list(pm[0:2]) + \
                [(c_sdone, c_sdone.n)] + list(bank3_marks)
            E("tensor", lambda e: e.matmul(
                out=psB[0:1, 1700:1701], lhsT=kmax_bf[:], rhs=ones_col[:],
                start=True, stop=False),
              waits=smw)
            E("tensor", lambda e: e.matmul(
                out=psB[0:1, 1700:1701], lhsT=kpemax_bf[:], rhs=ones_col[0:64, :],
                start=False, stop=True),
              inc=c_mt)
            E("scalar", lambda e: e.activation(
                out=lnmk[:], in_=psB[0:1, 1700:1701], func=AF.Ln,
                scale=SCALE * SCALE * 1.05),
              waits=[(c_mt, c_mt.n)], inc=c_sc)
            lnmk_mark = c_sc.n
            for qt in range(NT):
                E("tensor", lambda e, qt=qt: e.matmul(
                    out=psB[:, 1040 + qt:1041 + qt],
                    lhsT=sq_q[:, qt * 128:(qt + 1) * 128], rhs=ones_col[:],
                    start=True, stop=False),
                  waits=([(c_sc, qsq_mark)] if qt == 0 else []))
                E("tensor", lambda e, qt=qt: e.matmul(
                    out=psB[:, 1040 + qt:1041 + qt],
                    lhsT=sq_pe[0:64, qt * 128:(qt + 1) * 128], rhs=ones_col[0:64, :],
                    start=False, stop=True),
                  inc=(c_mq if qt == NT - 1 else None))
            E("tensor", lambda e: e.matmul(
                out=psB[:, 1704:1705], lhsT=halfs1[:], rhs=lnmk[:],
                start=True, stop=True),
              waits=[(c_sc, lnmk_mark)], inc=c_mt)
            E("vector", lambda e: e.tensor_copy(out=lnmk_col[:], in_=psB[:, 1704:1705]),
              waits=[(c_mt, c_mt.n)], inc=c_mv)
            lc_mark = c_mv.n
            E("scalar", lambda e, hb=hb: e.activation(
                out=neg_mh[:, hb * 16:(hb + 1) * 16], in_=psB[:, 1040:1056], func=AF.Ln),
              waits=[(c_mq, c_mq.n)], inc=c_sc)
            E("scalar", lambda e, hb=hb: e.activation(
                out=neg_mh[:, hb * 16:(hb + 1) * 16],
                in_=neg_mh[:, hb * 16:(hb + 1) * 16], func=AF.Exp, scale=0.5,
                bias=lnmk_col[:]),
              waits=[(c_sc, c_sc.n), (c_mv, lc_mark)], inc=c_sc)
            E("vector", lambda e, hb=hb: e.tensor_scalar(
                out=neg_mh[:, hb * 16:(hb + 1) * 16],
                in0=neg_mh[:, hb * 16:(hb + 1) * 16], scalar1=-1.0,
                scalar2=None, op0=ALU.mult),
              waits=[(c_sc, c_sc.n)], inc=c_mv)
            return (c_mv, c_mv.n)

        head_dmas(0, first=True)
        otw_at = {}
        bank3_marks = []
        bcast_ev_mark = [None]
        prev_w2 = [None]

        def augstats(h, mh_mark):
            hb = h % 2
            E("vector", lambda e, hb=hb: e.tensor_copy(
                out=stat_h[:, 0:16], in_=neg_mh[:, hb * 16:(hb + 1) * 16]),
              waits=[mh_mark, (c_statT, c_statT.n)], inc=c_v)
            E("vector", lambda e, hb=hb: e.tensor_sub(
                out=stat_h[:, 16:32], in0=neg_mh[:, hb * 16:(hb + 1) * 16],
                in1=stat_h[:, 0:16]),
              waits=[(c_v, c_v.n)], inc=c_stat)
            E("tensor", lambda e: e.matmul(
                out=psB[0:32, 1536:1664], lhsT=stat_h[:], rhs=ident[:],
                start=True, stop=True),
              waits=[(c_stat, c_stat.n)] + list(bank3_marks), inc=c_statT)
            E("vector", lambda e: e.tensor_copy(out=sb_st[:], in_=psB[0:32, 1536:1664]),
              waits=[(c_statT, c_statT.n), (c_aug, c_aug.n)], inc=c_statT_ev)
            E("scalar", lambda e, hb=hb: e.dma_start(
                out=QpeT_aug[hb][64:65, :], in_=sb_st[0:16, :]),
              waits=[(c_statT_ev, c_statT_ev.n)], inc=c_aug, inc_amt=16)
            E("scalar", lambda e, hb=hb: e.dma_start(
                out=QpeT_aug[hb][65:66, :], in_=sb_st[16:32, :]),
              inc=c_aug, inc_amt=16)
            return (c_statT_ev, c_statT_ev.n)

        prep_marks, prep_psA = prep(0, slot_marks["A"] + slot_marks["B"], slot_marks["B"])
        mh_ready = emit_mhat(0, prep_marks)
        bank3_marks = [prep_marks[1], mh_ready]
        st_ev_mark = augstats(0, mh_ready)
        bank3_marks = [st_ev_mark]

        for h in range(NH):
            hb = h % 2
            stage_at = {0: 0, 3: 1, 4: 2, 5: 3, 7: 4, 9: 5}

            aug_mark = (c_aug, c_aug.n)
            av_head_base = c_av.n
            binit = [(c_pt, c_pt.n)]
            if bcast_ev_mark[0]:
                binit.append(bcast_ev_mark[0])
            if h == 0 and prev_w2[0]:
                binit.append(prev_w2[0])
            st_last = {0: list(binit), 1: list(binit), 2: list(binit)}
            sg_first = list(bank3_marks)
            st_ctr = 0
            av_abs = []
            pm_ev = [m for m in prep_marks if m]
            for kt in range(NT):
                qmin = kt * 128
                pto = 2048 * (kt % 2)
                ch_list = _chunks(qmin, S)
                pt_of_chunk = []
                for (q0, wd) in ch_list:
                    bsel = st_ctr % 3
                    st_ctr += 1
                    boff = 512 * bsel
                    stw = list(st_last[bsel]) if isinstance(st_last[bsel], list) \
                        else [st_last[bsel]]
                    E("tensor", lambda e, hb=hb, kt=kt, q0=q0, wd=wd, boff=boff: e.matmul(
                        out=psB[:, boff:boff + wd],
                        lhsT=KnT[hb][:, kt * 128:(kt + 1) * 128],
                        rhs=QnT[hb][:, q0:q0 + wd], start=True, stop=False),
                      waits=stw)
                    E("tensor", lambda e, hb=hb, kt=kt, q0=q0, wd=wd, boff=boff: e.matmul(
                        out=psB[:, boff:boff + wd],
                        lhsT=kpeT_aug[:, kt * 128:(kt + 1) * 128],
                        rhs=QpeT_aug[hb][:, q0:q0 + wd], start=False, stop=True),
                      waits=([aug_mark] if kt == 0 and q0 == qmin else []),
                      inc=c_st)
                    ew = [(c_st, c_st.n)]
                    if q0 == qmin:
                        if kt >= 2:
                            ew.append((c_av, av_abs[kt - 2]))
                        elif av_head_base > 0:
                            ew.append((c_av, av_head_base - (1 if kt == 0 else 0)))
                    E("scalar", lambda e, q0=q0, wd=wd, boff=boff, pto=pto: e.activation(
                        out=ptpool[:, pto + q0:pto + q0 + wd],
                        in_=psB[:, boff:boff + wd], func=AF.Exp),
                      waits=ew, inc=c_pt)
                    st_last[bsel] = (c_pt, c_pt.n)
                    pt_of_chunk.append(c_pt.n)
                    if q0 == qmin:
                        E("vector", lambda e, qmin=qmin, pto=pto: e.tensor_tensor(
                            out=ptpool[:, pto + qmin:pto + qmin + 128],
                            in0=ptpool[:, pto + qmin:pto + qmin + 128],
                            in1=maskT[:], op=ALU.mult),
                          waits=[(c_pt, c_pt.n)], inc=c_ptm)
                for i2, (q0, wd) in enumerate(ch_list):
                    avw = [(c_pt, pt_of_chunk[i2])]
                    if i2 == 0:
                        avw.append((c_ptm, c_ptm.n))
                        if kt == 0:
                            avw += pm_ev
                            if h > 0 or sq > 0:
                                avw.append((c_ot_ev, c_ot_ev.n))
                    E("tensor", lambda e, hb=hb, kt=kt, q0=q0, wd=wd, pto=pto: e.matmul(
                        out=psA[:, q0:q0 + wd],
                        lhsT=Vh[hb][:, kt * 128:(kt + 1) * 128],
                        rhs=ptpool[:, pto + q0:pto + q0 + wd],
                        start=(kt == 0),
                        stop=(kt == min(NT - 1, (q0 + wd - 1) // 128))),
                      waits=avw, inc=c_av)
                av_abs.append(c_av.n)
                if h + 1 < NH and kt in stage_at:
                    head_stage(h + 1, stage_at[kt])
                for qt in range(kt, NT):
                    sgw = []
                    if kt == 0 and qt == 0:
                        sgw = list(sg_first)
                    E("tensor", lambda e, kt=kt, qt=qt, pto=pto: e.matmul(
                        out=psB[:, 1536 + qt:1537 + qt],
                        lhsT=ptpool[:, pto + qt * 128:pto + (qt + 1) * 128],
                        rhs=ones_col[:],
                        start=(kt == 0 and qt == 0),
                        stop=(kt == NT - 1 and qt == NT - 1)),
                      waits=sgw,
                      inc=(c_sg if qt == NT - 1 else None))

            E("vector", lambda e, h=h: e.tensor_copy(
                out=ot_stage[h % 2][:], in_=psA[:, :]),
              waits=[(c_av, c_av.n)] +
                    ([(c_otw, c_otw.n)] if h >= 2 else []),
              inc=c_ot_ev)
            ev_mark = (c_ot_ev, c_ot_ev.n)
            E("vector", lambda e: e.reciprocal(out=invcol[:], in_=psB[:, 1536:1552]),
              waits=[(c_sg, c_sg.n)], inc=c_mv)
            rec_mark = (c_mv, c_mv.n)
            E("tensor", lambda e: e.matmul(
                out=psB[0:16, 1552:1680], lhsT=invcol[:], rhs=identf[:],
                start=True, stop=True),
              waits=[rec_mark], inc=c_statT)
            E("vector", lambda e: e.tensor_copy(out=invT[:], in_=psB[0:16, 1552:1680]),
              waits=[(c_statT, c_statT.n)], inc=c_v)
            E("vector", lambda e: e.tensor_copy(out=invThi[:], in_=invT[:]),
              waits=[(c_v, c_v.n)], inc=c_v)
            E("vector", lambda e: e.tensor_sub(out=invTlo[:], in0=invT[:], in1=invThi[:]),
              waits=[(c_v, c_v.n)], inc=c_v)
            iv_mark = (c_v, c_v.n)
            E("scalar", lambda e: e.dma_start(out=invflat[:, 0:2048], in_=invThi[:]),
              waits=[iv_mark, (c_aug, c_aug.n)] +
                    ([prev_w2[0]] if prev_w2[0] else []),
              inc=c_aug, inc_amt=16)
            E("sync", lambda e: e.dma_start(out=invflat[:, 2048:4096], in_=invTlo[:]),
              waits=[iv_mark, (c_flo, c_flo.n)] +
                    ([prev_w2[0]] if prev_w2[0] else []),
              inc=c_flo, inc_amt=16)
            flat_mark = [(c_aug, c_aug.n), (c_flo, c_flo.n)]
            bank3_marks = [(c_statT_ev, c_statT_ev.n), rec_mark, iv_mark]

            if h + 1 < NH:
                prep_marks, prep_psA = prep(
                    h + 1,
                    [ev_mark, (c_pt, c_pt.n)],
                    [(c_pt, c_pt.n)] + list(bank3_marks))

            pm01 = [prep_marks[0]] if h + 1 < NH else []
            for qc in range(2):
                bw2 = list(flat_mark) + [(c_pt, c_pt.n)] if qc == 0 else []
                E("tensor", lambda e, qc=qc: e.matmul(
                    out=psB[:, qc * 512:(qc + 1) * 512], lhsT=ones1[:],
                    rhs=invflat[:, qc * 512:(qc + 1) * 512], start=True, stop=False),
                  waits=bw2)
                E("tensor", lambda e, qc=qc: e.matmul(
                    out=psB[:, qc * 512:(qc + 1) * 512], lhsT=ones1[:],
                    rhs=invflat[:, 2048 + qc * 512:2048 + (qc + 1) * 512],
                    start=False, stop=True),
                  inc=(c_sg if qc == 1 else None))
            sgm1 = (c_sg, c_sg.n)
            E("scalar", lambda e: e.activation(
                out=invb[:, 0:1024], in_=psB[:, 0:1024], func=AF.Copy),
              waits=[sgm1, (c_sc, c_sc.n)], inc=c_sc)
            ib1 = (c_sc, c_sc.n)
            for qc in range(2, 4):
                bw2 = [ib1] if qc == 2 else []
                E("tensor", lambda e, qc=qc: e.matmul(
                    out=psB[:, (qc - 2) * 512:(qc - 1) * 512], lhsT=ones1[:],
                    rhs=invflat[:, qc * 512:(qc + 1) * 512], start=True, stop=False),
                  waits=bw2)
                E("tensor", lambda e, qc=qc: e.matmul(
                    out=psB[:, (qc - 2) * 512:(qc - 1) * 512], lhsT=ones1[:],
                    rhs=invflat[:, 2048 + qc * 512:2048 + (qc + 1) * 512],
                    start=False, stop=True),
                  inc=(c_sg if qc == 3 else None))
            sgm2 = (c_sg, c_sg.n)
            prev_w2[0] = sgm2
            E("scalar", lambda e: e.activation(
                out=invb[:, 1024:2048], in_=psB[:, 0:1024], func=AF.Copy),
              waits=[sgm2, (c_sc, c_sc.n)], inc=c_sc)
            ib2 = (c_sc, c_sc.n)
            bcast_ev_mark[0] = ib2

            if h + 1 < NH:
                mh_ready = emit_mhat(h + 1, prep_marks)

            E("vector", lambda e, h=h: e.tensor_mul(
                out=ot_stage[h % 2][:], in0=ot_stage[h % 2][:], in1=invb[:]),
              waits=[ev_mark, ib2, (c_v, c_v.n)], inc=c_v)
            mul_mark = (c_v, c_v.n)
            E("scalar", lambda e, h=h: e.dma_start(
                out=ot_s[h], in_=ot_stage[h % 2][:]),
              waits=[mul_mark], inc=c_otw, inc_amt=16)
            otw_at[h] = c_otw.n

            if h + 1 < NH:
                bank3_marks = [prep_marks[1], mh_ready]
                st_ev_mark = augstats(h + 1, mh_ready)
                bank3_marks = [st_ev_mark]

        slot_marks["A"] = [(c_ot_ev, c_ot_ev.n)]
        slot_marks["B"] = [(c_pt, c_pt.n), (c_v, c_v.n), (c_sg, c_sg.n),
                           (c_statT_ev, c_statT_ev.n)] + list(bank3_marks)

        gate_att = [(c_av, c_av.n), (c_ot_ev, c_ot_ev.n), (c_otw, c_otw.n),
                    (c_sg, c_sg.n), (c_v, c_v.n), (c_mv, c_mv.n),
                    (c_whf, c_whf.n),
                    (c_bn_ev, c_bn_ev.n), (c_c1_ev, c_c1_ev.n), (c_c2_ev, c_c2_ev.n),
                    (c_tq_ev, c_tq_ev.n), (c_statT_ev, c_statT_ev.n),
                    (c_aug, c_aug.n), (c_ptm, c_ptm.n)]
        g2ctx.close()
        close_right()
        open_right()
        wo_bf = rsb("wo_bf", [128, NH * 2048], BF16)
        OT_wo = rsb("OT_wo", [128, NH * 2048], BF16)
        x_re = [rsb(f"xre{i}", [128, D], F32) for i in range(2)]
        r1_sb = [rsb(f"r1sb{i}", [128, D], F32) for i in range(2)]
        ln2w = [rsb(f"ln2w{i}", [128, D], BF16) for i in range(2)]
        ln2ts = [rsb(f"ln2ts{i}", [128, D], BF16) for i in range(2)]

        for hh in range(NH):
            E("gpsimd", lambda e, hh=hh: e.dma_start(
                out=wo_bf[:, hh * 2048:(hh + 1) * 2048],
                in_=wo[hh * 128:(hh + 1) * 128, :]),
              waits=(list(gate_att) + [(c_fb, c_fb.n), (c_ffn, c_ffn.n)] if hh == 0 else []),
              inc=c_wo, inc_amt=16)
        for hh in range(NH):
            E("sync", lambda e, hh=hh: e.dma_start(
                out=OT_wo[:, hh * 2048:(hh + 1) * 2048], in_=ot_s[hh]),
              waits=(list(gate_att) if hh == 0 else []),
              inc=c_otld, inc_amt=16)
        wo_loaded = c_wo.n
        ot_loaded = c_otld.n
        r1_at = {}
        r1wp_at = {}
        trFw_at = {}
        l2w_at = {}
        evict_at = {}
        ln2_at = {}

        def emit_trW(tt):
            bi = tt % 2
            ps = psA if tt % 2 == 0 else psB
            for g in range(4):
                gs = 512 * (g % 2)
                tw = [(c_ln2, ln2_at[tt])]
                if g < 2:
                    tw.append((c_r1, r1_at[tt]))
                else:
                    tw.append((c_trF_ev, c_trF_ev.n - 1))
                for j in range(4):
                    blk = g * 4 + j
                    E("tensor", lambda e, blk=blk, gs=gs, j=j, bi=bi, ps=ps: e.matmul(
                        out=ps[:, gs + j * 128:gs + (j + 1) * 128],
                        lhsT=ln2w[bi][:, blk * 128:(blk + 1) * 128], rhs=ident[:],
                        start=(j == 0), stop=(j == 3)),
                      waits=(tw if j == 0 else []),
                      inc=(c_trF if j == 3 else None))
                evw = [(c_trF, c_trF.n)]
                if g == 0 and tt >= 2:
                    evw.append((c_l2w[bi], l2w_at[tt - 2]))
                E("scalar", lambda e, g=g, gs=gs, bi=bi, ps=ps: e.activation(
                    out=ln2ts[bi][:, g * 512:(g + 1) * 512], in_=ps[:, gs:gs + 512],
                    func=AF.Copy),
                  waits=evw, inc=c_trF_ev)
            trFw_at[tt] = c_trF.n
            evict_at[tt] = c_trF_ev.n
            E("scalar", lambda e, tt=tt, bi=bi: e.dma_start(
                out=ln2t_s[tt], in_=ln2ts[bi][:]),
              waits=[(c_trF_ev, c_trF_ev.n)], inc=c_l2w[bi], inc_amt=16)
            l2w_at[tt] = c_l2w[bi].n

        for tt in range(NT):
            bi = tt % 2
            sl = "A" if tt % 2 == 0 else "B"
            ps = psA if sl == "A" else psB
            xrw = list(gate_att) if tt < 2 else [(c_r1, r1_at[tt - 2])]
            E("sync", lambda e, sq=sq, tt=tt, bi=bi: e.dma_start(
                out=x_re[bi][:], in_=x_in[sq, tt * 128:(tt + 1) * 128, :]),
              waits=xrw, inc=c_xre[bi], inc_amt=16)
            xre_mark = c_xre[bi].n
            w0 = list(slot_marks[sl]) if tt < 2 else \
                [(c_r1, r1_at[tt - 2]), (c_trF_ev, evict_at[tt - 2])]
            if tt == 0:
                w0 += [(c_wo, wo_loaded), (c_otld, ot_loaded)]
            first = True
            for hh in range(NH):
                for cci in range(4):
                    E("tensor", lambda e, tt=tt, hh=hh, cci=cci, ps=ps: e.matmul(
                        out=ps[:, cci * 512:(cci + 1) * 512],
                        lhsT=OT_wo[:, hh * 2048 + tt * 128:hh * 2048 + (tt + 1) * 128],
                        rhs=wo_bf[:, hh * 2048 + cci * 512:hh * 2048 + (cci + 1) * 512],
                        start=(hh == 0), stop=(hh == NH - 1)),
                      waits=(w0 if first else []),
                      inc=(c_woG if (hh == NH - 1 and cci == 3) else None))
                    first = False
            if tt >= 1:
                emit_trW(tt - 1)
            E("vector", lambda e, bi=bi, ps=ps: e.tensor_add(
                out=r1_sb[bi][:], in0=ps[:, :], in1=x_re[bi][:]),
              waits=[(c_woG, c_woG.n), (c_xre[bi], xre_mark)] +
                    ([(c_r1wp[bi], r1wp_at[tt - 2]), (c_ln2, ln2_at[tt - 2])]
                     if tt >= 2 else []),
              inc=c_r1)
            r1_at[tt] = c_r1.n
            E("scalar", lambda e, bi=bi: e.activation(
                out=ptpool[:, 0:D], in_=r1_sb[bi][:], func=AF.Copy, scale=-1.0 / D,
                accum_out=stat1[:, 2:3]),
              waits=[(c_r1, c_r1.n)], inc=c_sc)
            E("scalar", lambda e, bi=bi: e.activation(
                out=ptpool[:, 0:D], in_=r1_sb[bi][:], func=AF.Square, scale=D ** -0.5,
                accum_out=stat1[:, 1:2]),
              waits=[(c_sc, c_sc.n)], inc=c_sc)
            E("scalar", lambda e: e.activation(
                out=stat1[:, 3:4], in_=stat1[:, 2:3], func=AF.Square),
              waits=[(c_sc, c_sc.n)], inc=c_sc)
            E("scalar", lambda e: e.activation(
                out=stat1[:, 4:5], in_=stat1[:, 3:4], func=AF.Identity, scale=-1.0,
                bias=stat1[:, 1:2]),
              waits=[(c_sc, c_sc.n)], inc=c_sc)
            E("scalar", lambda e: e.activation(
                out=stat1[:, 5:6], in_=stat1[:, 4:5], func=AF.Ln, bias=cvals[:, 0:1]),
              waits=[(c_sc, c_sc.n)], inc=c_sc)
            E("scalar", lambda e, tt=tt: e.activation(
                out=stat_ln2[:, 2 * tt + 1:2 * tt + 2], in_=stat1[:, 5:6], func=AF.Exp,
                scale=-0.5, bias=cvals[:, 3:4]),
              waits=[(c_sc, c_sc.n)], inc=c_ln2s)
            E("scalar", lambda e, tt=tt: e.activation(
                out=stat_ln2[:, 2 * tt:2 * tt + 1], in_=stat1[:, 2:3], func=AF.Identity,
                scale=stat_ln2[:, 2 * tt + 1:2 * tt + 2]),
              waits=[(c_ln2s, c_ln2s.n)], inc=c_ln2s)
            E("scalar", lambda e, tt=tt, bi=bi: e.dma_start(
                out=r1_s[tt * 128:(tt + 1) * 128, :], in_=r1_sb[bi][:]),
              waits=[(c_ln2s, c_ln2s.n)] +
                    ([(c_r1wp[bi], c_r1wp[bi].n)] if tt >= 2 else []),
              inc=c_r1wp[bi], inc_amt=16)
            r1wp_at[tt] = c_r1wp[bi].n
            E("scalar", lambda e, tt=tt, bi=bi: e.activation(
                out=ln2w[bi][:], in_=r1_sb[bi][:], func=AF.Identity,
                bias=stat_ln2[:, 2 * tt:2 * tt + 1],
                scale=stat_ln2[:, 2 * tt + 1:2 * tt + 2]),
              waits=[(c_ln2s, c_ln2s.n)] + ([(c_trF, trFw_at[tt - 2])] if tt >= 2 else []),
              inc=c_ln2)
            ln2_at[tt] = c_ln2.n
        emit_trW(NT - 1)
        slot_wo = {"A": [(c_r1, r1_at[14]), (c_trF_ev, evict_at[14])],
                   "B": [(c_r1, r1_at[15]), (c_trF_ev, evict_at[15])]}
        r1w_final = [c_r1wp[0].n, c_r1wp[1].n]
        l2w_final = [c_l2w[0].n, c_l2w[1].n]

        close_right()
        open_right()
        f_bf = rsb("f_bf", [128, 16 * 2048], BF16)
        r1f = [rsb(f"r1f{i}", [128, D], F32) for i in range(3)]
        ln2T = [rsb(f"ln2T{i}", [128, D], BF16) for i in range(2)]
        out_sb = [rsb(f"outsb{i}", [128, D], F32) for i in range(2)]

        gate_wo = [(c_woG, c_woG.n), (c_r1, c_r1.n)]

        fld_at = {}
        for kb in range(16):
            pch = c_fldp[kb % 4]
            eng = "gpsimd" if kb % 2 == 0 else "scalar"
            fw = []
            if kb <= 1:
                fw = list(gate_wo) + [(c_fb, c_fb.n)]
            if kb >= 4:
                fw.append((pch, pch.n))
            E(eng, lambda e, kb=kb: e.dma_start(
                out=f_bf[:, kb * 2048:(kb + 1) * 2048],
                in_=f_s[kb * 128:(kb + 1) * 128, :]),
              waits=fw, inc=pch, inc_amt=16)
            fld_at[kb] = (pch, pch.n)
        fld_done = [fld_at[kb] for kb in range(16)]
        outv_at = {}
        ffn_at = {}
        out_at = {}
        for tt in range(NT):
            bi = tt % 2
            ri = tt % 3
            sl = "A" if tt % 2 == 0 else "B"
            ps = psA if sl == "A" else psB
            rw = (list(gate_wo) if tt < 3 else [(c_outv, outv_at[tt - 3])]) + \
                [(c_r1wp[tt % 2], r1wp_at[tt])]
            E("sync", lambda e, tt=tt, ri=ri: e.dma_start(
                out=r1f[ri][:], in_=r1_s[tt * 128:(tt + 1) * 128, :]),
              waits=rw, inc=c_r1r[ri], inc_amt=16)
            r1r_mark = (c_r1r[ri], c_r1r[ri].n)
            E("sync", lambda e, tt=tt, bi=bi, l2w_at=l2w_at: e.dma_start(
                out=ln2T[bi][:], in_=ln2t_s[tt]),
              waits=[(c_l2w[bi], l2w_at[tt])] +
                    ([(c_ffn, ffn_at[tt - 2])] if tt >= 2 else []),
              inc=c_l2r[bi], inc_amt=16)
            l2r_mark = (c_l2r[bi], c_l2r[bi].n)
            ffw = [(c_bias, 2)]
            if tt >= 2:
                ffw.append((c_outv, outv_at[tt - 2]))
            else:
                ffw += slot_wo[sl]
            first = True
            for cci in range(4):
                E("tensor", lambda e, cci=cci, ps=ps: e.matmul(
                    out=ps[:, cci * 512:(cci + 1) * 512], lhsT=ones1[:],
                    rhs=bias_hl[:, cci * 512:(cci + 1) * 512], start=True, stop=False),
                  waits=(ffw if first else []))
                first = False
                E("tensor", lambda e, cci=cci, ps=ps: e.matmul(
                    out=ps[:, cci * 512:(cci + 1) * 512], lhsT=ones1[:],
                    rhs=bias_hl[:, D + cci * 512:D + (cci + 1) * 512],
                    start=False, stop=False))
            ffn_marks = {}
            for kb in range(16):
                for cci in range(4):
                    E("tensor", lambda e, kb=kb, cci=cci, bi=bi, ps=ps: e.matmul(
                        out=ps[:, cci * 512:(cci + 1) * 512],
                        lhsT=ln2T[bi][:, kb * 128:(kb + 1) * 128],
                        rhs=f_bf[:, kb * 2048 + cci * 512:kb * 2048 + (cci + 1) * 512],
                        start=False, stop=(kb == 15)),
                      waits=(([fld_at[kb]] if tt == 0 else []) +
                             ([l2r_mark] if kb == 0 else [])
                             if cci == 0 else []),
                      inc=(c_ffn if kb == 15 else None))
                    if kb == 15:
                        ffn_marks[cci] = c_ffn.n
            ffn_at[tt] = c_ffn.n
            ov0 = [(c_out[bi], out_at[tt - 2])] if tt >= 2 else []
            for cci in range(4):
                E("vector", lambda e, bi=bi, ri=ri, cci=cci, ps=ps: e.tensor_add(
                    out=out_sb[bi][:, cci * 512:(cci + 1) * 512],
                    in0=ps[:, cci * 512:(cci + 1) * 512],
                    in1=r1f[ri][:, cci * 512:(cci + 1) * 512]),
                  waits=[(c_ffn, ffn_marks[cci])] +
                        ([r1r_mark] + ov0 if cci == 0 else []),
                  inc=c_outv, inc_amt=(1 if cci < 3 else 2))
                if cci < 2:
                    outv_at[(tt, cci)] = c_outv.n
            outv_at[tt] = c_outv.n
            E("sync", lambda e, sq=sq, tt=tt, bi=bi: e.dma_start(
                out=x_out[sq, tt * 128:(tt + 1) * 128, :], in_=out_sb[bi][:]),
              waits=[(c_outv, outv_at[tt])], inc=c_out[bi], inc_amt=16)
            out_at[tt] = c_out[bi].n
        slot_marks["A"] = [(c_outv, outv_at[14])]
        slot_marks["B"] = [(c_outv, outv_at[15])]
        gate = [(c_ffn, c_ffn.n), (c_outv, c_outv.n), (c_trF_ev, c_trF_ev.n),
                (c_out[0], c_out[0].n), (c_out[1], c_out[1].n),
                (c_ln2, c_ln2.n), (c_r1r[0], c_r1r[0].n), (c_r1r[1], c_r1r[1].n),
                (c_r1r[2], c_r1r[2].n), (c_l2r[0], c_l2r[0].n),
                (c_l2r[1], c_l2r[1].n)]
        close_right()

    for sq in range(NSEQ):
        _emit_seq(sq)

    p.wait("sync", [(c_out[0], c_out[0].n), (c_out[1], c_out[1].n)])

    with nc.Block() as block:
        @block.tensor
        def _(eng):
            for f in p.ops["tensor"]:
                f(eng)

        @block.vector
        def _(eng):
            for f in p.ops["vector"]:
                f(eng)

        @block.scalar
        def _(eng):
            for f in p.ops["scalar"]:
                f(eng)

        @block.sync
        def _(eng):
            for f in p.ops["sync"]:
                f(eng)

        @block.gpsimd
        def _(eng):
            for f in p.ops["gpsimd"]:
                f(eng)

    ctx.close()
    return nc


def _host_consts():
    ident = np.eye(128, dtype=ml_dtypes.bfloat16)
    dmaskv = np.zeros((128, 128), np.float32)
    dmaskv[np.triu_indices(128, 1)] = NEG
    tmaskv = np.triu(np.ones((128, 128), np.float64)).astype(ml_dtypes.bfloat16)
    inv = 1.0 / (10000.0 ** (np.arange(0, DR, 2, dtype=np.float64) / DR))
    t = np.arange(S, dtype=np.float64)
    fr = np.outer(t, inv)
    emb = np.concatenate([fr, fr], axis=1)
    cosv = np.cos(emb).astype(np.float32)
    sinv = np.sin(emb)
    sinn = sinv.copy()
    sinn[:, 0:DR // 2] = -sinn[:, 0:DR // 2]
    sinn = sinn.astype(np.float32)
    return ident, dmaskv, tmaskv, cosv, sinn


def kernel(L0_attn_hidden_states, L0_attn_Wqa, L0_attn_q_a_ln_w, L0_attn_Wqb,
           L0_attn_Wkva, L0_attn_kv_a_ln_w, L0_attn_Wkvb, L0_attn_Wo,
           L0_ffn_factors, L0_ffn_bias):
    x = np.ascontiguousarray(L0_attn_hidden_states)
    ident, dmaskv, tmaskv, cosv, sinn = _host_consts()
    nc = build_nc()
    common = {
        "wqa": np.ascontiguousarray(L0_attn_Wqa),
        "qln": np.ascontiguousarray(L0_attn_q_a_ln_w),
        "wqb": np.ascontiguousarray(L0_attn_Wqb),
        "wkva": np.ascontiguousarray(L0_attn_Wkva),
        "kvln": np.ascontiguousarray(L0_attn_kv_a_ln_w),
        "wkvb": np.ascontiguousarray(L0_attn_Wkvb),
        "wo": np.ascontiguousarray(L0_attn_Wo),
        "fac": np.ascontiguousarray(L0_ffn_factors),
        "fbias": np.ascontiguousarray(L0_ffn_bias),
        "identc": ident, "identfc": np.eye(128, dtype=np.float32),
        "dmaskc": dmaskv, "tmaskc": tmaskv,
        "cosc": cosv, "sinnc": sinn,
    }
    in_maps = []
    for c in range(8):
        m = dict(common)
        m["x"] = np.ascontiguousarray(x[2 * c:2 * c + 2])
        in_maps.append(m)
    res = run_bass_kernel_spmd(nc, in_maps, core_ids=list(range(8)))
    return np.concatenate([r["out"] for r in res.results], axis=0)


if __name__ == "__main__":
    nc = build_nc()
    print("built ok; instructions:", nc.n_instructions())
